# version 10
# speedup vs baseline: 1.1321x; 1.0068x over previous
import numpy as np
import ml_dtypes
from contextlib import ExitStack

import concourse.bass as bass
import concourse.mybir as mybir

F32 = mybir.dt.float32
BF16 = mybir.dt.bfloat16
I32 = mybir.dt.int32
AF = mybir.ActivationFunctionType
ALU = mybir.AluOpType
AX = mybir.AxisListType

D = 2048
H = 16
HD = 128
DFF = 5632
NCHK = 16
FCHK = 44
EPS = 1e-5

NF4 = np.array([-1.0, -0.6962, -0.5251, -0.3949, -0.2844, -0.1848, -0.0911,
                0.0, 0.0796, 0.1609, 0.2461, 0.3379, 0.4407, 0.5626, 0.723,
                1.0], dtype=ml_dtypes.bfloat16).astype(np.float32)
NF4_MID = [float(x) for x in ((NF4[:-1] + NF4[1:]) / 2)]
NF4_DELTA = [float(x) for x in (NF4[1:] - NF4[:-1])]


def host_consts(core_id):
    c = {}
    c["ident"] = np.eye(128, dtype=np.float32).astype(ml_dtypes.bfloat16)
    c["identf"] = np.eye(128, dtype=np.float32)
    c["ones1"] = np.ones((1, 128), dtype=np.float32)
    c["onesc"] = np.ones((128, 1), dtype=np.float32)
    c["pos_127"] = (127 - np.arange(128, dtype=np.float32)).reshape(128, 1)
    i = np.arange(128)
    dT = (i[None, :] - i[:, None]).astype(np.float32)
    dT[dT < 0] = 1e9
    c["diffT"] = dT
    ho = np.zeros((128, H, H), dtype=np.float32)
    for h in range(H):
        ho[:, h, h] = 1.0
    c["headones"] = ho.reshape(128, H * H)
    eb = np.zeros((2, 128), dtype=np.float32)
    eb[0, 0:64] = 1.0
    eb[1, 64:128] = 1.0
    c["ebase2"] = eb
    c["posmat"] = np.tile(np.arange(1, 129, dtype=np.float32), (128, 4))
    m8 = np.zeros((1, 8), dtype=np.float32)
    if core_id % 2 == 1:
        m8[0, core_id - 1] = 1.0
    c["mask8"] = m8
    return c


SM_LDB = 0
SM_CD128 = 16
SM_CDSUP = 32
SM_KDP = 48
SM_KDCD = 64
SM_NW = 192
SM_HO = 208
SM_M8B = 464
SM_ST = 472
SM_ARG = 492
SM_ROW = 640
SM_MEAN = 920
SM_P127 = 960
SM_ONEC = 961
SM_RMS = 962
SM_KARG = 476


class MK:
    def __init__(self, tpc=4096, n_cores=8, phases=None):
        self.tpc = tpc
        self.n_cores = n_cores
        self.sg_tok = min(1024, tpc)
        self.cps = self.sg_tok // 128
        self.nsg = tpc // self.sg_tok
        self.gb_tok = min(512, tpc)
        self.ngb = tpc // self.gb_tok
        self.cpb = self.gb_tok // 128
        self.gc_tok = min(512, tpc)
        self.ngc = tpc // self.gc_tok
        self.cpc = self.gc_tok // 128
        self.nchunks = tpc // 128
        self.phases = phases or ["setup", "deq", "A", "AGstate", "B", "C"]
        self.nc = bass.Bass("TRN2", target_bir_lowering=False)
        self.stack = ExitStack()
        self.declare()

    def declare(self):
        nc = self.nc
        t = self.tpc
        self.inp = {}
        for name, shape, dt in [
            ("x", [t, D], F32),
            ("log_decay", [16, 1], F32),
            ("Wq", [D, D], F32), ("Wk", [D, D], F32), ("Wv", [D, D], F32),
            ("Wo", [D, D], F32), ("Wg1", [D, HD], F32), ("Wg2", [HD, D], F32),
            ("norm_w", [NCHK, 128], F32),
            ("w1s", [DFF // 8, D], BF16), ("w2s", [DFF // 8, D], BF16),
            ("w3s", [D // 8, DFF], BF16),
            ("w1ts", [D, DFF // 8], BF16), ("w2ts", [D, DFF // 8], BF16),
            ("w3ts", [DFF, D // 8], BF16),
        ]:
            self.inp[name] = nc.dram_tensor(name, shape, dt, kind="ExternalInput").ap()
        for name, arr in host_consts(0).items():
            dtv = BF16 if arr.dtype == ml_dtypes.bfloat16 else F32
            self.inp[name] = nc.dram_tensor(
                name, list(arr.shape), dtv, kind="ExternalInput").ap()
        self.outp = nc.dram_tensor("out", [t, D], F32, kind="ExternalOutput").ap()

        dr = lambda n, s, d: nc.dram_tensor(n, s, d, kind="Internal").ap()
        self.wq_bf = dr("wq_bf", [NCHK, 128, D], BF16)
        self.wk_bf = dr("wk_bf", [NCHK, 128, D], BF16)
        self.wv_bf = dr("wv_bf", [NCHK, 128, D], BF16)
        self.wo_bf = dr("wo_bf", [NCHK, 128, D], BF16)
        self.wg2d = dr("wg2d", [HD, D], BF16)
        self.q_sp = dr("q_sp", [NCHK, 128, t], BF16)
        self.k_sp = dr("k_sp", [t, D], BF16)
        self.v_sp = dr("v_sp", [t, D], BF16)
        self.xmid = dr("xmid", [t, D], F32)
        self.ag_a_in = dr("ag_a_in", [128, D], F32)
        self.ag_a_out = dr("ag_a_out", [self.n_cores, 128, D], F32)
        self.mean_ag_in = dr("mean_ag_in", [1, 4], F32)
        self.mean_ag_out = dr("mean_ag_out", [self.n_cores, 1, 4], F32)
        self.sflat = dr("sflat", [2, 22528], F32)
        self.w1sh = dr("w1sh", [D, DFF // 8], BF16)
        self.w2sh = dr("w2sh", [D, DFF // 8], BF16)
        self.w3sh = dr("w3sh", [DFF, D // 8], BF16)
        self.w1gr = dr("w1gr", [8, D, DFF // 8], BF16)
        self.w2gr = dr("w2gr", [8, D, DFF // 8], BF16)
        self.w1g = dr("w1g", [D, DFF], BF16)
        self.w2g = dr("w2g", [D, DFF], BF16)
        self.w3g = dr("w3g", [8, DFF, D // 8], BF16)

        sb = lambda n, s, d: self.stack.enter_context(nc.sbuf_tensor(n, s, d))
        self.BB0 = sb("BB0", [128, 32768], BF16)
        self.BB1 = sb("BB1", [128, 16384], BF16)
        self.BB2 = sb("BB2", [128, 16384], BF16)
        self.BB3 = sb("BB3", [128, 4096], BF16)
        self.G1R = sb("G1R", [128, 4096], BF16)
        self.CB = sb("CB", [128, 4096], BF16)
        self.IDT = sb("IDT", [128, 128], BF16)
        self.BF0 = sb("BF0", [128, 6144], F32)
        self.BF1 = sb("BF1", [128, 2048], F32)
        self.BF2 = sb("BF2", [128, 2048], F32)
        self.BF3 = sb("BF3", [128, 2048], F32)
        self.SM = sb("SM", [128, 1024], F32)
        self.ITMP = sb("ITMP", [128, 256], I32)
        self.XB = nc.virtual_sbuf_tensor("XB", [128, 4096], F32)
        nc._tensor(self.XB.name, self.XB.shape, self.XB.dtype, type="SB")
        mx = nc.lookup_mloc(self.XB)
        mx.allocated = True
        mx.addr = nc.lookup_mloc(self.BB0).addr + 49152
        self.ps_f = [self.stack.enter_context(
            nc.psum_tensor(f"psf{i}", [128, 512], F32)) for i in range(6)]
        self.ps_b = self.stack.enter_context(
            nc.psum_tensor("psb", [128, 2048], BF16))
        self.bar = self.stack.enter_context(nc.semaphore("bar"))
        self._sems = {}

    def sem(self, name):
        if name not in self._sems:
            self._sems[name] = self.stack.enter_context(self.nc.semaphore(name))
        return self._sems[name]

    @property
    def xb(self):
        return ((self.nchunks + 1) // 2, self.nchunks // 2)

    def build(self):
        nc = self.nc
        reg = {"setup": PhSetup, "deq": PhDeq, "A": PhA, "AGstate": PhAGState,
               "B": PhB, "C": PhC}
        self.ph_objs = [reg[p](self) for p in self.phases]

        def drive(engname):
            def cb(eng):
                for i, ph in enumerate(self.ph_objs):
                    eng.wait_ge(self.bar, 5 * i)
                    getattr(ph, engname)(eng)
                    eng.sem_inc(self.bar, 1)
            return cb

        with nc.Block() as block:
            block.sync(drive("sync"))
            block.tensor(drive("tensor"))
            block.vector(drive("vector"))
            block.scalar(drive("scalar"))
            block.gpsimd(drive("gpsimd"))
        return nc


class Phase:
    def __init__(self, mk: MK):
        self.mk = mk
        self.nc = mk.nc
        self.plan()

    def plan(self): pass
    def sync(self, eng): pass
    def tensor(self, eng): pass
    def vector(self, eng): pass
    def scalar(self, eng): pass
    def gpsimd(self, eng): pass


class PhSetup(Phase):

    def plan(self):
        mk = self.mk
        self.sl = mk.sem("su_l")
        self.sg_ = mk.sem("su_g")
        self.sc_ = mk.sem("su_c")

    def scalar(self, eng):
        mk, i = self.mk, self.mk.inp
        SM = mk.SM
        eng.dma_start(out=SM[0:1, SM_ROW:SM_ROW + 16],
                      in_=i["log_decay"].rearrange("h o -> o h")).then_inc(self.sl, 16)
        eng.dma_start(out=SM[0:1, SM_ROW + 16:SM_ROW + 144],
                      in_=i["ones1"][:, :]).then_inc(self.sl, 16)
        eng.dma_start(out=SM[0:2, SM_ROW + 144:SM_ROW + 272],
                      in_=i["ebase2"][:, :]).then_inc(self.sl, 16)
        eng.dma_start(out=SM[0:1, SM_ROW + 272:SM_ROW + 280],
                      in_=i["mask8"][:, :]).then_inc(self.sl, 16)
        eng.dma_start(out=SM[:, SM_P127:SM_P127 + 1],
                      in_=i["pos_127"][:, :]).then_inc(self.sl, 16)
        eng.dma_start(out=SM[:, SM_ONEC:SM_ONEC + 1],
                      in_=i["onesc"][:, :]).then_inc(self.sl, 16)
        eng.dma_start(out=SM[:, SM_NW:SM_NW + 16].rearrange("p (c o) -> p c o", c=NCHK),
                      in_=i["norm_w"].rearrange("c p -> p c").unsqueeze(2),
                      allow_slow_non_contiguous=True).then_inc(self.sl, 16)
        eng.dma_start(out=SM[:, SM_HO:SM_HO + 256],
                      in_=i["headones"][:, :]).then_inc(self.sl, 16)
        eng.dma_start(out=mk.BF1[:, 0:128], in_=i["diffT"][:, :]).then_inc(self.sl, 16)
        eng.dma_start(out=mk.IDT[:, :], in_=i["ident"][:, :]).then_inc(self.sl, 16)
        eng.wait_ge(self.sc_, 2)
        eng.activation(SM[:, SM_CD128:SM_CD128 + 16], SM[:, SM_LDB:SM_LDB + 16],
                       AF.Exp, scale=128.0)
        eng.activation(SM[:, SM_CDSUP:SM_CDSUP + 16], SM[:, SM_LDB:SM_LDB + 16],
                       AF.Exp, scale=float(128 * mk.cps))
        eng.activation(SM[:, SM_KDP:SM_KDP + 16], SM[:, SM_KARG:SM_KARG + 16], AF.Exp)
        for j in range(mk.cps):
            eng.activation(SM[:, SM_KDCD + 16 * j:SM_KDCD + 16 * (j + 1)],
                           SM[:, SM_ARG + 16 * j:SM_ARG + 16 * (j + 1)], AF.Exp)
        eng.wait_ge(self.sc_, 3)
        for h in range(H):
            eng.activation(mk.CB[:, 2048 + 128 * h:2048 + 128 * (h + 1)],
                           mk.BF2[:, 128 * h:128 * (h + 1)], AF.Exp)
        eng.drain()

    def tensor(self, eng):
        mk = self.mk
        SM = mk.SM
        eng.wait_ge(self.sl, 16 * 10)
        eng.matmul(mk.ps_f[0][:, 0:16], SM[0:1, SM_ROW + 16:SM_ROW + 144],
                   SM[0:1, SM_ROW:SM_ROW + 16], start=True, stop=False)
        mm = eng.matmul(mk.ps_f[0][:, 16:24], SM[0:1, SM_ROW + 16:SM_ROW + 144],
                        SM[0:1, SM_ROW + 272:SM_ROW + 280], start=False, stop=True)
        mm.then_inc(self.sc_, 1)

    def vector(self, eng):
        mk = self.mk
        SM = mk.SM
        eng.wait_ge(self.sc_, 1)
        eng.tensor_copy(SM[:, SM_LDB:SM_LDB + 16], mk.ps_f[0][:, 0:16])
        eng.tensor_copy(SM[:, SM_M8B:SM_M8B + 8], mk.ps_f[0][:, 16:24])
        eng.drain()
        eng.tensor_scalar_mul(SM[:, SM_KARG:SM_KARG + 16], SM[:, SM_LDB:SM_LDB + 16],
                              SM[:, SM_P127:SM_P127 + 1])
        eng.drain()
        for j in range(mk.cps):
            eng.tensor_scalar_mul(SM[:, SM_ARG + 16 * j:SM_ARG + 16 * (j + 1)],
                                  SM[:, SM_LDB:SM_LDB + 16],
                                  float(128 * (mk.cps - 1 - j)))
        eng.drain()
        for j in range(mk.cps):
            eng.tensor_add(SM[:, SM_ARG + 16 * j:SM_ARG + 16 * (j + 1)],
                           SM[:, SM_ARG + 16 * j:SM_ARG + 16 * (j + 1)],
                           SM[:, SM_KARG:SM_KARG + 16])
        eng.drain()
        eng.sem_inc(self.sc_, 1)
        for h in range(H):
            eng.tensor_scalar_mul(mk.BF2[:, 128 * h:128 * (h + 1)],
                                  mk.BF1[:, 0:128],
                                  SM[:, SM_LDB + h:SM_LDB + h + 1])
        eng.drain()
        eng.sem_inc(self.sc_, 1)

    def gpsimd(self, eng):
        mk, i = self.mk, self.mk.inp
        eng.dma_start(out=mk.CB[:, 0:2048].rearrange("p (c g) -> p c g", c=NCHK),
                      in_=i["Wg1"].rearrange("(c p) g -> p c g", p=128)
                      ).then_inc(self.sg_, 16)
        for name, dst in [("Wq", mk.wq_bf), ("Wk", mk.wk_bf), ("Wv", mk.wv_bf),
                          ("Wo", mk.wo_bf)]:
            eng.dma_start(out=dst[:, :, :],
                          in_=i[name].rearrange("(c p) e -> c p e", p=128)
                          ).then_inc(self.sg_, 16)
        eng.dma_start(out=mk.wg2d[:, :], in_=i["Wg2"][:, :]).then_inc(self.sg_, 16)
        eng.wait_ge(self.sg_, 16 * 6)


class PhA(Phase):

    CUT = 9
    CUT0 = 9

    def plan(self):
        mk = self.mk
        for n in ["x0", "x1", "a_ln", "tr", "trev", "a_mm", "a_ev2",
                  "sp0", "sp1", "a_spk0", "a_spk1", "a_spv0", "a_spv1",
                  "a_kvk0", "a_kvk1", "a_kvv0", "a_kvv1",
                  "a_kt", "a_st", "a_g1t", "a_g1s", "a_mrg",
                  "st_sq", "st_var", "st_sqr", "st_stat", "ag_a_d", "ag_a",
                  "a_w0", "a_w1", "a_w2", "a_w3"]:
            mk.sem(n)
        self.h2 = mk.sg_tok // 512
        self.nsh = mk.sg_tok // 512
        self.gq = NCHK * self.nsh
        self.gkv = mk.cps * 4
        self.gsg = self.gq + 2 * self.gkv
        self.sp_sg = self.gq + 2 * mk.cps

    def spill_done_wait(self, eng, sg):
        S = self.mk.sem
        for par in range(2):
            eng.wait_ge(S(f"sp{par}"), 16 * ((sg + 1) * self.gq // 2))
            eng.wait_ge(S(f"a_spk{par}"), 16 * ((sg + 1) * self.mk.cps // 2))
            eng.wait_ge(S(f"a_spv{par}"), 16 * ((sg + 1) * self.mk.cps // 2))

    def sync(self, eng):
        mk = self.mk
        S = mk.sem
        for sg in range(mk.nsg):
            for ci in range(mk.cps):
                gci = sg * mk.cps + ci
                if gci >= 2:
                    eng.wait_ge(S("a_ln"), gci - 1)
                eng.dma_start(out=mk.BF0[:, 2048 * (gci % 2):2048 * (gci % 2) + 2048],
                              in_=mk.inp["x"][128 * gci:128 * (gci + 1), :]
                              ).then_inc(S(f"x{gci % 2}"), 16)
            for wi, wsrc in enumerate([mk.wq_bf, mk.wk_bf, mk.wv_bf][:max(0, self.CUT - 2)]):
                eng.wait_ge(S("a_mm"), sg * self.gsg
                            + (0 if wi == 0 else self.gq + (wi - 1) * self.gkv))
                for dg in range(4):
                    eng.dma_start(out=mk.BB0[:, 8192 * dg:8192 * (dg + 1)]
                                  .rearrange("p (c e) -> p c e", c=4),
                                  in_=wsrc[4 * dg:4 * (dg + 1), :, :]
                                  .rearrange("c p e -> p c e")
                                  ).then_inc(S(f"a_w{dg}"), 16)
            if self.CUT < 6:
                continue
            self.spill_done_wait(eng, sg)
            if sg > 0:
                eng.wait_ge(S("a_st"), sg * mk.cps)
            for ci in range(mk.cps):
                if ci >= 2:
                    eng.wait_ge(S("a_st"), sg * mk.cps + ci - 1)
                gci = sg * mk.cps + ci
                n = sg * (mk.cps // 2) + ci // 2 + 1
                eng.dma_start(
                    out=mk.BB2[:, 4096 * (ci % 2):4096 * (ci % 2) + 2048],
                    in_=mk.k_sp[128 * gci:128 * (gci + 1), :]
                    ).then_inc(S(f"a_kvk{ci % 2}"), 16)
                eng.dma_start(
                    out=mk.BB2[:, 4096 * (ci % 2) + 2048:4096 * (ci % 2) + 4096],
                    in_=mk.v_sp[128 * gci:128 * (gci + 1), :]
                    ).then_inc(S(f"a_kvv{ci % 2}"), 16)

    def scalar(self, eng):
        mk = self.mk
        SM = mk.SM
        S = mk.sem
        for sg in range(mk.nsg):
            for ci in range(mk.cps):
                gci = sg * mk.cps + ci
                st = SM_ST + 8 * (gci % 2)
                xs = mk.BF0[:, 2048 * (gci % 2):2048 * (gci % 2) + 2048]
                eng.wait_ge(S(f"x{gci % 2}"), 16 * (gci // 2 + 1))
                if self.CUT0 >= 2:
                    eng.activation(mk.BF0[:, 4096:6144], xs, AF.Square,
                                   accum_out=SM[:, st + 1:st + 2])
                    eng.drain()
                eng.sem_inc(S("st_sq"), 1)
                eng.wait_ge(S("st_var"), gci + 1)
                if self.CUT0 >= 2:
                    eng.activation(SM[:, st + 5:st + 6], SM[:, st + 3:st + 4],
                                   AF.Sqrt)
                    eng.drain()
                eng.sem_inc(S("st_sqr"), 1)
                eng.wait_ge(S("st_stat"), gci + 1)
                if ci == 0 and sg > 0:
                    eng.wait_ge(S("a_st"), sg * mk.cps)
                if gci >= 1:
                    eng.wait_ge(S("tr"), 4 * gci)
                if self.CUT0 >= 3:
                    eng.activation(mk.BB3[:, 0:2048], xs, AF.Identity,
                                   bias=SM[:, st + 4:st + 5],
                                   scale=SM[:, st + 6:st + 7])
                    eng.drain()
                eng.sem_inc(S("a_ln"), 1)
            for ns in range(self.h2 if self.CUT >= 2 else 0):
                eng.wait_ge(S("a_g1t"), sg * self.h2 + ns + 1)
                eng.activation(mk.G1R[:, sg * mk.sg_tok + 512 * ns:
                                      sg * mk.sg_tok + 512 * ns + 512],
                               mk.ps_f[4 + ns % 2][:, :], AF.Identity)
                eng.drain()
                eng.sem_inc(S("a_g1s"), 1)
            for g in range(self.gq if self.CUT >= 3 else 0):
                ec, ns = divmod(g, self.nsh)
                eng.wait_ge(S("a_mm"), sg * self.gsg + g + 1)
                if g >= 2:
                    eng.wait_ge(S(f"sp{g % 2}"),
                                16 * (self.gq // 2 * sg + (g - 2) // 2 + 1))
                slot = mk.BB3[:, 512 * (g % 2):512 * (g % 2) + 512]
                eng.activation(slot, mk.ps_f[4 + g % 2][:, :], AF.Silu)
                eng.drain()
                eng.sem_inc(S("a_ev2"), 1)
                eng.dma_start(out=mk.q_sp[ec, :, sg * mk.sg_tok + 512 * ns:
                                          sg * mk.sg_tok + 512 * ns + 512],
                              in_=slot).then_inc(S(f"sp{g % 2}"), 16)
            for pi, (sp, tag) in enumerate(
                    [(mk.k_sp, "k"), (mk.v_sp, "v")][:max(0, self.CUT - 3)]):
                base = self.gq + pi * self.gkv
                for g in range(self.gkv):
                    tc, es = divmod(g, 4)
                    eng.wait_ge(S("a_mm"), sg * self.gsg + base + g + 1)
                    if g == 0:
                        if pi == 0:
                            for par in range(2):
                                eng.wait_ge(S(f"sp{par}"),
                                            16 * ((sg + 1) * self.gq // 2))
                        else:
                            for par in range(2):
                                eng.wait_ge(S(f"a_spk{par}"),
                                            16 * ((sg + 1) * mk.cps // 2))
                    if tc >= 2 and es == 0:
                        eng.wait_ge(S(f"a_sp{tag}{tc % 2}"),
                                    16 * (mk.cps // 2 * sg + (tc - 2) // 2 + 1))
                    slot = mk.BB3[:, 2048 * (tc % 2):2048 * (tc % 2) + 2048]
                    eng.activation(slot[:, 512 * es:512 * es + 512],
                                   mk.ps_f[4 + g % 2][:, :], AF.Silu)
                    eng.drain()
                    eng.sem_inc(S("a_ev2"), 1)
                    if es == 3:
                        gci = sg * mk.cps + tc
                        eng.dma_start(out=sp[128 * gci:128 * (gci + 1), :],
                                      in_=slot).then_inc(S(f"a_sp{tag}{tc % 2}"), 16)

    def vector(self, eng):
        mk = self.mk
        SM = mk.SM
        S = mk.sem
        for sg in range(mk.nsg):
            for ci in range(mk.cps):
                gci = sg * mk.cps + ci
                st = SM_ST + 8 * (gci % 2)
                xs = mk.BF0[:, 2048 * (gci % 2):2048 * (gci % 2) + 2048]
                eng.wait_ge(S(f"x{gci % 2}"), 16 * (gci // 2 + 1))
                if gci >= 2:
                    eng.wait_ge(S("a_ln"), gci - 1)
                if self.CUT0 >= 2:
                    eng.reduce_sum(SM[:, st:st + 1], xs, AX.X)
                eng.wait_ge(S("st_sq"), gci + 1)
                eng.drain()
                if self.CUT0 >= 2:
                    eng.tensor_scalar_mul(SM[:, st + 2:st + 3], SM[:, st:st + 1],
                                          1.0 / D)
                    eng.tensor_scalar_mul(SM[:, st + 3:st + 4],
                                          SM[:, st + 1:st + 2], 1.0 / D)
                    eng.drain()
                    eng.tensor_mul(SM[:, st + 7:st + 8], SM[:, st + 2:st + 3],
                                   SM[:, st + 2:st + 3])
                    eng.drain()
                    eng.tensor_sub(SM[:, st + 3:st + 4], SM[:, st + 3:st + 4],
                                   SM[:, st + 7:st + 8])
                    eng.drain()
                    eng.tensor_scalar_add(SM[:, st + 3:st + 4],
                                          SM[:, st + 3:st + 4], EPS)
                    eng.drain()
                eng.sem_inc(S("st_var"), 1)
                eng.wait_ge(S("st_sqr"), gci + 1)
                if self.CUT0 >= 2:
                    eng.reciprocal(SM[:, st + 6:st + 7], SM[:, st + 5:st + 6])
                    eng.drain()
                    eng.tensor_scalar(SM[:, st + 4:st + 5], SM[:, st + 2:st + 3],
                                      SM[:, st + 6:st + 7], -1.0,
                                      ALU.mult, ALU.mult)
                    eng.drain()
                eng.sem_inc(S("st_stat"), 1)
                eng.wait_ge(S("tr"), gci * 4 + 4)
                if self.CUT0 >= 5:
                    for dc in range(NCHK):
                        eng.tensor_copy(
                            mk.BB1[:, 1024 * dc + 128 * ci:
                                   1024 * dc + 128 * ci + 128],
                            mk.ps_b[:, 128 * dc:128 * dc + 128])
                    eng.drain()
                eng.sem_inc(S("trev"), 4)
            for ci in range(mk.cps if self.CUT >= 6 else 0):
                n = sg * (mk.cps // 2) + ci // 2 + 1
                eng.wait_ge(S(f"a_kvk{ci % 2}"), 16 * n)
                if ci >= 1:
                    eng.wait_ge(S("a_st"), sg * mk.cps + ci)
                kv = mk.BB2[:, 4096 * (ci % 2):4096 * (ci % 2) + 2048]
                kd = SM[:, SM_KDCD + 16 * ci:SM_KDCD + 16 * (ci + 1)]
                eng.tensor_tensor(
                    mk.BB3[:, 0:2048].rearrange("p (h e) -> p h e", h=H),
                    kv.rearrange("p (h e) -> p h e", h=H),
                    kd.unsqueeze(2).broadcast_to([128, H, HD]),
                    ALU.mult)
                eng.drain()
                eng.sem_inc(S("a_kt"), 1)
            if self.CUT < 6:
                continue
            eng.wait_ge(S("a_st"), (sg + 1) * mk.cps)
            if sg == 0:
                eng.memset(mk.BF1[:, :], 0.0)
                eng.drain()
            eng.tensor_tensor(
                mk.BF1[:, :].rearrange("p (h e) -> p h e", h=H),
                mk.BF1[:, :].rearrange("p (h e) -> p h e", h=H),
                SM[:, SM_CDSUP:SM_CDSUP + 16].unsqueeze(2)
                .broadcast_to([128, H, HD]), ALU.mult)
            eng.drain()
            for b in range(4):
                eng.tensor_add(mk.BF1[:, 512 * b:512 * b + 512],
                               mk.BF1[:, 512 * b:512 * b + 512], mk.ps_f[b][:, :])
            eng.drain()
            eng.sem_inc(S("a_mrg"), 1)

    def tensor(self, eng):
        mk = self.mk
        S = mk.sem
        for sg in range(mk.nsg):
            for ci in range(mk.cps):
                gci = sg * mk.cps + ci
                eng.wait_ge(S("a_ln"), gci + 1)
                for s in range(4):
                    b = (0, 2, 1, 3)[s]
                    if gci * 4 + s >= 4:
                        eng.wait_ge(S("trev"), gci * 4 + s - 3)
                    for k in range(4):
                        dc = b * 4 + k
                        mm = eng.transpose(
                            mk.ps_b[:, 512 * b + 128 * k:512 * b + 128 * k + 128],
                            mk.BB3[:, 128 * dc:128 * dc + 128], mk.IDT[:, :])
                    mm.then_inc(S("tr"), 1)
            eng.wait_ge(S("trev"), ((sg + 1) * mk.cps) * 4)
            if sg > 0 and self.CUT >= 3:
                eng.wait_ge(S("a_ev2"), sg * self.gsg)
            for ns in range(self.h2 if self.CUT >= 2 else 0):
                for dc in range(NCHK):
                    mm = eng.matmul(mk.ps_f[4 + ns % 2][:, :],
                               mk.CB[:, 128 * dc:128 * dc + 128],
                               mk.BB1[:, 1024 * dc + 512 * ns:
                                      1024 * dc + 512 * ns + 512],
                               start=(dc == 0), stop=(dc == NCHK - 1))
                mm.then_inc(S("a_g1t"), 1)
            eng.wait_ge(S("a_g1s"), (sg + 1) * self.h2 if self.CUT >= 2 else 0)
            for g in range(self.gq if self.CUT >= 3 else 0):
                ec, ns = divmod(g, self.nsh)
                if sg * self.gsg + g - 1 > 0:
                    eng.wait_ge(S("a_ev2"), sg * self.gsg + g - 1)
                for dc in range(NCHK):
                    if g == 0:
                        eng.wait_ge(S(f"a_w{dc // 4}"), 16 * (3 * sg + 1))
                    mm = eng.matmul(mk.ps_f[4 + g % 2][:, :],
                               mk.BB0[:, 2048 * dc + 128 * ec:
                                      2048 * dc + 128 * ec + 128],
                               mk.BB1[:, 1024 * dc + 512 * ns:
                                      1024 * dc + 512 * ns + 512],
                               start=(dc == 0), stop=(dc == NCHK - 1))
                mm.then_inc(S("a_mm"), 1)
            for g in range(self.gkv if self.CUT >= 4 else 0):
                tc, es = divmod(g, 4)
                eng.wait_ge(S("a_ev2"), sg * self.gsg + self.gq + g - 1)
                for dc in range(NCHK):
                    if g == 0:
                        eng.wait_ge(S(f"a_w{dc // 4}"), 16 * (3 * sg + 2))
                    mm = eng.matmul(mk.ps_f[4 + g % 2][:, :],
                               mk.BB1[:, 1024 * dc + 128 * tc:
                                      1024 * dc + 128 * tc + 128],
                               mk.BB0[:, 2048 * dc + 512 * es:
                                      2048 * dc + 512 * es + 512],
                               start=(dc == 0), stop=(dc == NCHK - 1))
                mm.then_inc(S("a_mm"), 1)
            for g in range(self.gkv if self.CUT >= 5 else 0):
                tc, es = divmod(g, 4)
                eng.wait_ge(S("a_ev2"),
                            sg * self.gsg + self.gq + self.gkv + g - 1)
                for dc in range(NCHK):
                    if g == 0:
                        eng.wait_ge(S(f"a_w{dc // 4}"), 16 * (3 * sg + 3))
                    mm = eng.matmul(mk.ps_f[4 + g % 2][:, :],
                               mk.BB1[:, 1024 * dc + 128 * tc:
                                      1024 * dc + 128 * tc + 128],
                               mk.BB0[:, 2048 * dc + 512 * es:
                                      2048 * dc + 512 * es + 512],
                               start=(dc == 0), stop=(dc == NCHK - 1))
                mm.then_inc(S("a_mm"), 1)
            if sg > 0 and self.CUT >= 6:
                eng.wait_ge(S("a_mrg"), sg)
            for ci in range(mk.cps if self.CUT >= 6 else 0):
                eng.wait_ge(S("a_kt"), sg * mk.cps + ci + 1)
                eng.wait_ge(S(f"a_kvv{ci % 2}"),
                            16 * (sg * (mk.cps // 2) + ci // 2 + 1))
                kt = mk.BB3[:, 0:2048]
                vv = mk.BB2[:, 4096 * (ci % 2) + 2048:4096 * (ci % 2) + 4096]
                for h in range(H):
                    mm = eng.matmul(
                        mk.ps_f[h // 4][:, 128 * (h % 4):128 * (h % 4) + 128],
                        kt[:, 128 * h:128 * h + 128],
                        vv[:, 128 * h:128 * h + 128],
                        start=(ci == 0 and h % 4 == 0),
                        stop=(ci == mk.cps - 1 and h % 4 == 3))
                mm.then_inc(S("a_st"), 1)

    def gpsimd(self, eng):
        if self.CUT < 7:
            return
        mk = self.mk
        S = mk.sem
        eng.wait_ge(S("a_mrg"), mk.nsg)
        eng.dma_start(out=mk.ag_a_in[:, :], in_=mk.BF1[:, :]).then_inc(S("ag_a_d"), 16)
        eng.wait_ge(S("ag_a_d"), 16)
        eng.collective_compute(
            "AllGather", ALU.bypass,
            replica_groups=[list(range(mk.n_cores))],
            ins=[mk.ag_a_in], outs=[mk.ag_a_out],
        ).then_inc(S("ag_a"), 1)


class PhAGState(Phase):

    def plan(self):
        for n in ["ag_a", "ags_l", "ags_v"]:
            self.mk.sem(n)

    def gpsimd(self, eng):
        mk = self.mk
        S = mk.sem
        eng.wait_ge(S("ag_a"), 1)
        for r in range(mk.n_cores):
            eng.wait_ge(S("ags_v"), r)
            eng.dma_start(out=mk.BF0[:, 0:2048], in_=mk.ag_a_out[r, :, :]
                          ).then_inc(S("ags_l"), 16)
        eng.wait_ge(S("ags_v"), mk.n_cores)

    def vector(self, eng):
        mk = self.mk
        S = mk.sem
        eng.memset(mk.BF1[:, :], 0.0)
        eng.drain()
        eng.sem_inc(S("ags_v"), 0) if False else None
        for r in range(mk.n_cores):
            eng.wait_ge(S("ags_l"), 16 * (r + 1))
            eng.scalar_tensor_tensor(mk.BF1[:, :], mk.BF0[:, 0:2048],
                                     mk.SM[:, SM_M8B + r:SM_M8B + r + 1],
                                     mk.BF1[:, :], ALU.mult, ALU.add)
            eng.drain()
            eng.sem_inc(S("ags_v"), 1)


class PhDeq(Phase):

    def plan(self):
        mk = self.mk
        for n in ["d_s1", "d_part", "d_agm", "d_mld",
                  "d_mn", "d_bnc", "d_arl", "d_art", "d_arv",
                  "d_wt", "d_stg", "d_exp", "d_eev", "d_fin", "d_out",
                  "agw", "c_wready", "su_l"]:
            mk.sem(n)
        self.NB = 180224

    def sync(self, eng):
        mk = self.mk
        S = mk.sem
        i = mk.inp
        for wi, name in enumerate(["w1s", "w2s", "w3s"]):
            if wi > 0:
                eng.wait_ge(S("d_s1"), wi)
            sh = i[name]
            flat = sh.rearrange("a b -> (a b)").rearrange("(s q) -> s q", q=16384)
            eng.dma_start(out=mk.BB1[0:88, :], in_=flat).then_inc(S("d_wt"), 16)
        eng.wait_ge(S("d_bnc"), 16 * 3)
        for a in range(2):
            for t in range(2):
                gi = 2 * a + t
                if gi >= 1:
                    eng.wait_ge(S("d_art"), gi)
                eng.dma_start(
                    out=mk.BF0[0:128, 4000 + 88 * t:4000 + 88 * (t + 1)],
                    in_=mk.sflat[a, :].rearrange("(x y) -> x y", y=88)
                    [128 * t:128 * (t + 1), :]).then_inc(S("d_arl"), 16)
        for fc in range(FCHK):
            if fc > 0:
                eng.wait_ge(S("d_fin"), fc)
            eng.dma_start(out=mk.BB3[:, 0:256],
                          in_=i["w3ts"][128 * fc:128 * (fc + 1), :]
                          ).then_inc(S("d_wt"), 16)
        for wi, name in enumerate(["w1ts", "w2ts"]):
            eng.wait_ge(S("d_bnc"), 16 * (5 + 2 * wi))
            for a in range(2):
                for t in range(6):
                    gi = 4 + wi * 12 + a * 6 + t
                    rows = 128 if t < 5 else 64
                    eng.wait_ge(S("d_art"), gi)
                    eng.dma_start(
                        out=mk.BF0[0:rows, 4000 + 32 * (t % 2):
                                   4000 + 32 * (t % 2) + 32],
                        in_=mk.sflat[a, :].rearrange("(x y) -> x y", y=32)
                        [128 * t:128 * t + rows, :]).then_inc(S("d_arl"), 16)
            for dc in range(NCHK):
                ti = FCHK + wi * NCHK + dc
                eng.wait_ge(S("d_fin"), ti)
                eng.dma_start(out=mk.BB3[:, 0:704],
                              in_=i[name][128 * dc:128 * (dc + 1), :]
                              ).then_inc(S("d_wt"), 16)

    def tensor(self, eng):
        mk = self.mk
        S = mk.sem
        SM = mk.SM
        for wi in range(3):
            eng.wait_ge(S("d_s1"), wi + 1)
            mm = eng.matmul(mk.ps_f[0][0:1, wi:wi + 1],
                            SM[0:88, 940 + wi:941 + wi],
                            SM[0:88, SM_ONEC:SM_ONEC + 1],
                            start=(wi == 0), stop=(wi == 2))
        mm.then_inc(S("d_part"), 1)
        eng.wait_ge(S("d_mld"), 16)
        for wi in range(3):
            mm = eng.matmul(mk.ps_f[1][0:1, wi:wi + 1],
                            SM[0:8, 924 + wi:925 + wi],
                            SM[0:8, SM_ONEC:SM_ONEC + 1],
                            start=(wi == 0), stop=(wi == 2))
        mm.then_inc(S("d_mn"), 1)
        eng.wait_ge(S("d_mn"), 2)
        mm = eng.matmul(mk.ps_f[1][:, 8:11], SM[0:1, SM_ROW + 16:SM_ROW + 144],
                        SM[0:1, 928:931], start=True, stop=True)
        mm.then_inc(S("d_mn"), 1)
        eng.wait_ge(S("su_l"), 16 * 11)
        for a in range(2):
            for t in range(2):
                gi = 2 * a + t
                eng.wait_ge(S("d_arl"), 16 * (gi + 1))
                mm = eng.transpose(mk.ps_f[2][0:88, 0:128],
                                   mk.BF0[0:128, 4000 + 88 * t:4000 + 88 * (t + 1)],
                                   SM[0:128, 472:600])
                mm.then_inc(S("d_art"), 1)
                eng.wait_ge(S("d_arv"), gi + 1)
        for fc in range(FCHK):
            eng.wait_ge(S("d_stg"), 32 * (fc + 1))
            eng.wait_ge(S("d_eev"), 2 * fc)
            mm = eng.matmul(mk.ps_f[4 + fc % 2][:, :],
                            SM[0:2, SM_ROW + 144:SM_ROW + 272],
                            mk.BF0[0:2, 1408:1920],
                            start=True, stop=True)
            mm.then_inc(S("d_exp"), 1)
        for wi in range(2):
            for a in range(2):
                for t in range(6):
                    gi = 4 + wi * 12 + a * 6 + t
                    rows = 128 if t < 5 else 64
                    eng.wait_ge(S("d_arl"), 16 * (gi + 1))
                    mm = eng.transpose(
                        mk.ps_f[2][0:32, 0:rows],
                        mk.BF0[0:rows, 4000 + 32 * (t % 2):
                               4000 + 32 * (t % 2) + 32],
                        SM[0:rows, 472:472 + rows])
                    mm.then_inc(S("d_art"), 1)
                    eng.wait_ge(S("d_arv"), gi + 1)
            for dc in range(NCHK):
                ti = FCHK + wi * NCHK + dc
                eng.wait_ge(S("d_stg"), 32 * (ti + 1))
                for rnd in range(2):
                    eng.wait_ge(S("d_eev"), 2 * ti + rnd)
                    off = 1408 + 704 * rnd
                    eng.matmul(mk.ps_f[4][:, 0:512],
                               SM[0:2, SM_ROW + 144:SM_ROW + 272],
                               mk.BF0[0:2, off:off + 512],
                               start=True, stop=True)
                    mm = eng.matmul(mk.ps_f[5][:, 0:192],
                                    SM[0:2, SM_ROW + 144:SM_ROW + 272],
                                    mk.BF0[0:2, off + 512:off + 704],
                                    start=True, stop=True)
                    mm.then_inc(S("d_exp"), 1)

    def vector(self, eng):
        mk = self.mk
        S = mk.sem
        SM = mk.SM
        BF1, BF2 = mk.BF1, mk.BF2
        for wi in range(3):
            eng.wait_ge(S("d_wt"), 16 * (wi + 1))
            eng.tensor_reduce(BF1[0:88, 256 * wi:256 * (wi + 1)],
                              mk.BB1[0:88, :].rearrange("s (b e) -> s b e", e=64),
                              AX.X, ALU.max, apply_absolute_value=True)
            eng.drain()
            eng.reduce_sum(SM[0:88, 940 + wi:941 + wi],
                           BF1[0:88, 256 * wi:256 * (wi + 1)], AX.X)
            eng.drain()
            eng.sem_inc(S("d_s1"), 1)
        eng.wait_ge(S("d_part"), 1)
        eng.memset(SM[0:1, 920:924], 0.0)
        eng.drain()
        eng.tensor_copy(SM[0:1, 920:923], mk.ps_f[0][0:1, 0:3])
        eng.drain()
        eng.sem_inc(S("d_part"), 1)
        eng.wait_ge(S("d_mn"), 1)
        eng.tensor_scalar_mul(SM[0:1, 928:931], mk.ps_f[1][0:1, 0:3],
                              1.0 / self.NB)
        eng.drain()
        eng.sem_inc(S("d_mn"), 1)
        eng.wait_ge(S("d_mn"), 3)
        eng.tensor_copy(SM[:, 936:939], mk.ps_f[1][:, 8:11])
        eng.drain()
        eng.sem_inc(S("d_mn"), 1)
        for wi in range(3):
            s1 = BF1[0:88, 256 * wi:256 * (wi + 1)]
            eng.tensor_scalar(BF2[0:88, 0:256], s1, SM[0:88, 936 + wi:937 + wi],
                              None, ALU.subtract)
            eng.drain()
            eng.tensor_reduce(SM[0:88, 944:945], BF2[0:88, 0:256], AX.X,
                              ALU.max, apply_absolute_value=True)
            eng.drain()
            eng.reciprocal(SM[0:88, 945:946], SM[0:88, 944:945])
            eng.drain()
            eng.tensor_scalar_mul(SM[0:88, 946:947], SM[0:88, 945:946], 128.0)
            eng.drain()
            eng.reciprocal(SM[0:88, 947:948], SM[0:88, 946:947])
            eng.drain()
            eng.tensor_scalar_mul(BF2[0:88, 256:512], BF2[0:88, 0:256],
                                  SM[0:88, 946:947])
            eng.drain()
            eng.tensor_copy(mk.ITMP[0:88, :], BF2[0:88, 256:512])
            eng.drain()
            eng.tensor_copy(BF2[0:88, 256:512], mk.ITMP[0:88, :])
            eng.drain()
            eng.tensor_scalar_min(BF2[0:88, 256:512], BF2[0:88, 256:512], 127.0)
            eng.drain()
            eng.tensor_scalar_max(BF2[0:88, 256:512], BF2[0:88, 256:512], -128.0)
            eng.drain()
            eng.tensor_scalar_mul(mk.BB3[0:88, 3584:3840], BF2[0:88, 256:512],
                                  SM[0:88, 947:948])
            eng.drain()
            eng.tensor_scalar(BF2[0:88, 512 + 256 * wi:768 + 256 * wi],
                              mk.BB3[0:88, 3584:3840],
                              SM[0:88, 936 + wi:937 + wi], None, ALU.add)
            eng.drain()
            eng.reciprocal(BF2[0:88, 1280 + 256 * wi:1536 + 256 * wi], s1)
            eng.drain()
            eng.sem_inc(S("d_s1"), 1)
        for a in range(2):
            for t in range(2):
                gi = 2 * a + t
                eng.wait_ge(S("d_art"), gi + 1)
                eng.tensor_copy(
                    BF1[0:88, 768 + 256 * a + 128 * t:768 + 256 * a + 128 * (t + 1)],
                    mk.ps_f[2][0:88, 0:128])
                eng.drain()
                eng.sem_inc(S("d_arv"), 1)
        for fc in range(FCHK):
            eng.wait_ge(S("d_exp"), fc + 1)
            eng.tensor_copy(mk.BF3[:, 0:512], mk.ps_f[4 + fc % 2][:, :])
            eng.drain()
            eng.sem_inc(S("d_eev"), 2)
            self.emit_tile(eng, fc, 256)
        for wi in range(2):
            for a in range(2):
                for t in range(6):
                    gi = 4 + wi * 12 + a * 6 + t
                    cols = 128 if t < 5 else 64
                    eng.wait_ge(S("d_art"), gi + 1)
                    eng.tensor_copy(
                        mk.BF0[0:32, 704 * a + 128 * t:704 * a + 128 * t + cols],
                        mk.ps_f[2][0:32, 0:cols])
                    eng.drain()
                    eng.sem_inc(S("d_arv"), 1)
            for dc in range(NCHK):
                ti = FCHK + wi * NCHK + dc
                for rnd in range(2):
                    eng.wait_ge(S("d_exp"), FCHK + wi * 2 * NCHK + 2 * dc
                                + rnd + 1)
                    eng.tensor_copy(mk.BF3[:, 704 * rnd:704 * rnd + 512],
                                    mk.ps_f[4][:, 0:512])
                    eng.tensor_copy(mk.BF3[:, 704 * rnd + 512:704 * rnd + 704],
                                    mk.ps_f[5][:, 0:192])
                    eng.drain()
                    eng.sem_inc(S("d_eev"), 1)
                self.emit_tile(eng, ti, 704)
        eng.wait_ge(S("d_out"), 16 * (FCHK + 2 * NCHK))

    def emit_tile(self, eng, ti, W):
        mk = self.mk
        S = mk.sem
        if True:
            eng.wait_ge(S("d_wt"), 16 * (ti + 4))
            if ti >= 1:
                eng.wait_ge(S("d_out"), 16 * ti)
            wt = mk.BB3[:, 0:W]
            sc = mk.BB3[:, W:2 * W]
            acc = mk.BB3[:, 2 * W:3 * W]
            ind = mk.BB3[:, 3 * W:4 * W]
            eng.tensor_tensor(sc, wt, mk.BF3[:, 0:W], ALU.mult)
            eng.drain()
            eng.tensor_scalar(acc, sc, NF4_MID[0], NF4_DELTA[0],
                              ALU.is_gt, ALU.mult)
            eng.drain()
            for j in range(1, 15):
                eng.tensor_scalar(ind, sc, NF4_MID[j], None, ALU.is_gt)
                eng.drain()
                eng.scalar_tensor_tensor(acc, ind, NF4_DELTA[j], acc,
                                         ALU.mult, ALU.add)
                eng.drain()
            eng.scalar_tensor_tensor(ind, acc, -1.0, mk.BF3[:, W:2 * W],
                                     ALU.add, ALU.mult)
            eng.drain()
            eng.sem_inc(S("d_fin"), 1)

    def scalar(self, eng):
        mk = self.mk
        S = mk.sem
        SM = mk.SM
        i = mk.inp
        eng.dma_start(out=SM[:, 472:600], in_=i["identf"][:, :]
                      ).then_inc(S("su_l"), 16)
        eng.wait_ge(S("d_part"), 2)
        eng.dma_start(out=mk.mean_ag_in[:, :], in_=SM[0:1, 920:924]
                      ).then_inc(S("d_bnc"), 16)
        eng.wait_ge(S("d_s1"), 6)
        eng.dma_start(out=mk.sflat[0, :].rearrange("(a b) -> a b", b=256),
                      in_=mk.BF2[0:88, 1792:2048]).then_inc(S("d_bnc"), 16)
        eng.dma_start(out=mk.sflat[1, :].rearrange("(a b) -> a b", b=256),
                      in_=mk.BF2[0:88, 1024:1280]).then_inc(S("d_bnc"), 16)
        for fc in range(FCHK):
            eng.wait_ge(S("d_fin"), fc + 1)
            eng.dma_start(out=mk.w3sh[128 * fc:128 * (fc + 1), :],
                          in_=mk.BB3[:, 768:1024]).then_inc(S("d_out"), 16)
        for wi in range(2):
            eng.wait_ge(S("d_arv"), 4 + wi * 12)
            eng.dma_start(out=mk.sflat[0, :].rearrange("(a b) -> a b", b=256),
                          in_=mk.BF2[0:88, 1280 + 256 * wi:1536 + 256 * wi]
                          ).then_inc(S("d_bnc"), 16)
            eng.dma_start(out=mk.sflat[1, :].rearrange("(a b) -> a b", b=256),
                          in_=mk.BF2[0:88, 512 + 256 * wi:768 + 256 * wi]
                          ).then_inc(S("d_bnc"), 16)
            sh = mk.w1sh if wi == 0 else mk.w2sh
            for dc in range(NCHK):
                ti = FCHK + wi * NCHK + dc
                eng.wait_ge(S("d_fin"), ti + 1)
                eng.dma_start(out=sh[128 * dc:128 * (dc + 1), :],
                              in_=mk.BB3[:, 2112:2816]).then_inc(S("d_out"), 16)

    def gpsimd(self, eng):
        mk = self.mk
        S = mk.sem
        eng.wait_ge(S("d_bnc"), 16)
        eng.collective_compute(
            "AllGather", ALU.bypass, replica_groups=[list(range(mk.n_cores))],
            ins=[mk.mean_ag_in], outs=[mk.mean_ag_out]).then_inc(S("d_agm"), 1)
        eng.wait_ge(S("d_agm"), 1)
        eng.dma_start(out=mk.SM[0:8, 924:928],
                      in_=mk.mean_ag_out.rearrange("r o w -> (r o) w")
                      ).then_inc(S("d_mld"), 16)
        for fc in range(FCHK):
            eng.wait_ge(S("d_arv"), 4)
            if fc >= 1:
                eng.wait_ge(S("d_exp"), fc)
            eng.dma_start(out=mk.BF0[0:2, 1408:1664],
                          in_=mk.BF1[2 * fc:2 * fc + 2, 768:1024]
                          ).then_inc(S("d_stg"), 16)
            eng.dma_start(out=mk.BF0[0:2, 1664:1920],
                          in_=mk.BF1[2 * fc:2 * fc + 2, 1024:1280]
                          ).then_inc(S("d_stg"), 16)
        for wi in range(2):
            for dc in range(NCHK):
                ti = FCHK + wi * NCHK + dc
                eng.wait_ge(S("d_arv"), 4 + (wi + 1) * 12)
                eng.wait_ge(S("d_exp"), FCHK + wi * 2 * NCHK + 2 * dc)
                eng.dma_start(out=mk.BF0[0:2, 1408:2112],
                              in_=mk.BF0[2 * dc:2 * dc + 2, 0:704]
                              ).then_inc(S("d_stg"), 16)
                eng.dma_start(out=mk.BF0[0:2, 2112:2816],
                              in_=mk.BF0[2 * dc:2 * dc + 2, 704:1408]
                              ).then_inc(S("d_stg"), 16)
        eng.wait_ge(S("d_out"), 16 * (FCHK + 2 * NCHK))
        for (src, dst) in ((mk.w1sh, mk.w1gr), (mk.w2sh, mk.w2gr),
                           (mk.w3sh, mk.w3g)):
            eng.collective_compute(
                "AllGather", ALU.bypass,
                replica_groups=[list(range(mk.n_cores))],
                ins=[src], outs=[dst]).then_inc(S("agw"), 1)



class PhB(Phase):

    def plan(self):
        mk = self.mk
        for n in ["b_wo", "b_gate", "b_gev", "b_sc", "b_msk", "b_qt", "b_o",
                  "b_kt", "b_stA", "b_stB", "b_sevA", "b_sevB", "b_og",
                  "b_on", "b_attn", "b_aev", "b_ln", "b_Sbf", "b_init",
                  "b_q0", "b_q1", "b_k0", "b_k1", "b_v0",
                  "b_v1", "b_f0", "b_f1",
                  "b_initl", "b_nw2", "b_lstat", "b_lnorm"]:
            mk.sem(n)
        self.nck = mk.nchunks
        mkk = self.mk
        self.xb0, self.xb1 = mkk.xb
        self.trb = 4 * self.nck
        self.stb = self.nck
        self.spb = mkk.nsg * ((NCHK * (mkk.sg_tok // 512)) // 2)

    def sync(self, eng):
        mk = self.mk
        S = mk.sem
        for ec in range(NCHK):
            eng.dma_start(out=mk.BB0[:, 2048 * ec:2048 * (ec + 1)],
                          in_=mk.wo_bf[ec, :, :]).then_inc(S("b_wo"), 16)
        eng.wait_ge(S("b_init"), 2)
        for gci in range(self.nck):
            p = gci % 2
            if gci >= 2:
                eng.wait_ge(S("b_ln"), gci - 1)
            eng.dma_start(out=mk.BF0[:, 2048 * p:2048 * p + 2048],
                          in_=mk.inp["x"][128 * gci:128 * (gci + 1), :]
                          ).then_inc(S(f"x{p}"), 16)
            if gci >= 2:
                eng.wait_ge(S("b_qt"), gci - 1)
            eng.dma_start(out=mk.BB1[:, 12288 + 2048 * p:12288 + 2048 * p + 2048]
                          .rearrange("p (h e) -> p h e", h=H),
                          in_=mk.q_sp[:, :, 128 * gci:128 * (gci + 1)]
                          .rearrange("h p e -> p h e")).then_inc(S(f"b_q{p}"), 16)
            if gci >= 2:
                eng.wait_ge(S("b_kt"), gci - 1)
            eng.dma_start(out=mk.BB1[:, 2048 * p:2048 * p + 2048],
                          in_=mk.k_sp[128 * gci:128 * (gci + 1), :]
                          ).then_inc(S(f"b_k{p}"), 16)
            if gci >= 2:
                eng.wait_ge(S("b_stB"), gci - 1)
            eng.dma_start(out=mk.BB1[:, 4096 + 2048 * p:4096 + 2048 * p + 2048],
                          in_=mk.v_sp[128 * gci:128 * (gci + 1), :]
                          ).then_inc(S(f"b_v{p}"), 16)
            if gci >= 2:
                eng.wait_ge(S("b_sc"), gci - 1)
            for h in range(H):
                eng.dma_start(
                    out=mk.BB1[:, 8192 + 2048 * p + 128 * h:
                               8192 + 2048 * p + 128 * (h + 1)],
                    in_=mk.k_sp[128 * gci:128 * (gci + 1),
                                128 * h:128 * (h + 1)],
                    transpose=True).then_inc(S(f"b_f{p}"), 16)

    def tensor(self, eng):
        mk = self.mk
        S = mk.sem
        eng.wait_ge(S("b_initl"), 16 * 4)
        for sl in range(4):
            mm = eng.matmul(mk.ps_f[sl][:, :],
                            mk.SM[0:1, SM_ROW + 16:SM_ROW + 144],
                            mk.BF0[0:1, 512 * sl:512 * (sl + 1)],
                            start=True, stop=True)
        mm.then_inc(S("b_nw2"), 1)
        eng.wait_ge(S("b_init"), 2)
        eng.wait_ge(S("b_wo"), 16 * NCHK)
        for gci in range(self.nck):
            p = gci % 2
            if gci >= 1:
                eng.wait_ge(S("b_aev"), 8 * gci)
            for es in range(4):
                mm = eng.matmul(mk.ps_f[4 + es % 2][:, :],
                                mk.G1R[:, 128 * gci:128 * (gci + 1)],
                                mk.BB2[:, 4096 + 512 * es:4096 + 512 * (es + 1)],
                                start=True, stop=True)
                mm.then_inc(S("b_gate"), 1)
                if es >= 1:
                    eng.wait_ge(S("b_gev"), 4 * gci + es)
            eng.wait_ge(S(f"b_f{p}"), 16 * H * (gci // 2 + 1))
            eng.wait_ge(S(f"b_q{p}"), 16 * (gci // 2 + 1))
            if gci >= 1:
                eng.wait_ge(S("b_og"), 4 * gci)
            for h in range(H):
                mm = eng.matmul(mk.ps_f[h // 4][:, 128 * (h % 4):128 * (h % 4) + 128],
                                mk.BB1[:, 8192 + 2048 * p + 128 * h:
                                       8192 + 2048 * p + 128 * (h + 1)],
                                mk.BB1[:, 12288 + 2048 * p + 128 * h:
                                       12288 + 2048 * p + 128 * (h + 1)],
                                start=(h % 4 == 0), stop=(h % 4 == 3))
            mm.then_inc(S("b_sc"), 1)
            eng.wait_ge(S("b_msk"), gci + 1)
            eng.wait_ge(S("b_qt"), gci + 1)
            eng.wait_ge(S("b_Sbf"), gci + 1)
            eng.wait_ge(S(f"b_v{p}"), 16 * (gci // 2 + 1))
            for h in range(H):
                eng.matmul(mk.ps_f[h // 4][:, 128 * (h % 4):128 * (h % 4) + 128],
                           mk.BB2[:, 2048 + 128 * h:2048 + 128 * (h + 1)],
                           mk.BB1[:, 4096 + 2048 * p + 128 * h:
                                  4096 + 2048 * p + 128 * (h + 1)],
                           start=(h % 4 == 0), stop=False)
                mm = eng.matmul(
                    mk.ps_f[h // 4][:, 128 * (h % 4):128 * (h % 4) + 128],
                    mk.BB2[:, 14336 + 128 * h:14336 + 128 * (h + 1)],
                    mk.BB2[:, 128 * h:128 * (h + 1)],
                    start=False, stop=(h % 4 == 3))
            mm.then_inc(S("b_o"), 1)
            eng.wait_ge(S("b_kt"), gci + 1)
            eng.wait_ge(S("b_gev"), 4 * (gci + 1))
            for half in range(2):
                if half == 1:
                    eng.wait_ge(S("b_sevA"), gci + 1)
                for hh in range(8):
                    h = 8 * half + hh
                    bank = 4 + hh // 4
                    eng.matmul(mk.ps_f[bank][:, 128 * (hh % 4):128 * (hh % 4) + 128],
                               mk.BF2[:, 128 * h:128 * (h + 1)],
                               mk.BF1[:, 128 * h:128 * (h + 1)],
                               start=(hh % 4 == 0), stop=False)
                    mm = eng.matmul(
                        mk.ps_f[bank][:, 128 * (hh % 4):128 * (hh % 4) + 128],
                        mk.BB3[:, 128 * h:128 * (h + 1)],
                        mk.BB1[:, 4096 + 2048 * p + 128 * h:
                               4096 + 2048 * p + 128 * (h + 1)],
                        start=False, stop=(hh % 4 == 3))
                mm.then_inc(S("b_stA" if half == 0 else "b_stB"), 1)
            eng.wait_ge(S("b_on"), gci + 1)
            for half in range(2):
                if half == 0 and gci >= 1:
                    eng.wait_ge(S("trev"), self.trb + 2 * gci)
                if half == 1:
                    eng.wait_ge(S("trev"), self.trb + 2 * gci + 1)
                for hh in range(8):
                    h = 8 * half + hh
                    mm = eng.transpose(
                        mk.ps_b[:, 1024 * half + 128 * hh:
                                1024 * half + 128 * (hh + 1)],
                        mk.BB2[:, 6144 + 128 * h:6144 + 128 * (h + 1)],
                        mk.IDT[:, :])
                mm.then_inc(S("tr"), 1)
            eng.wait_ge(S("b_sevB"), gci + 1)
            eng.wait_ge(S("trev"), self.trb + 2 * (gci + 1))
            for dms in range(4):
                if dms >= 2:
                    eng.wait_ge(S("b_aev"), 8 * gci + 2 * (dms - 1))
                for half in range(2):
                    for ec in range(8 * half, 8 * half + 8):
                        mm = eng.matmul(
                            mk.ps_f[4 + dms % 2][:, :],
                            mk.BB2[:, 12288 + 128 * ec:12288 + 128 * (ec + 1)],
                            mk.BB0[:, 2048 * ec + 512 * dms:
                                   2048 * ec + 512 * (dms + 1)],
                            start=(ec == 0), stop=(ec == NCHK - 1))
                    mm.then_inc(S("b_attn"), 1)

    def vector(self, eng):
        mk = self.mk
        S = mk.sem
        SM = mk.SM
        eng.wait_ge(S("b_initl"), 16 * 4)
        for h in range(H):
            eng.tensor_scalar_mul(mk.BF2[:, 128 * h:128 * (h + 1)],
                                  mk.BF0[:, 2048:2176],
                                  SM[:, SM_CD128 + h:SM_CD128 + h + 1])
        for h in range(H):
            eng.tensor_scalar_mul(mk.BF3[:, 128 * h:128 * (h + 1)],
                                  mk.BF0[:, 4096:4224],
                                  SM[:, SM_LDB + h:SM_LDB + h + 1])
        eng.drain()
        eng.sem_inc(S("b_init"), 1)
        for gci in range(self.nck):
            p = gci % 2
            if gci == 0:
                eng.wait_ge(S("b_init"), 1)
            else:
                eng.wait_ge(S("b_sevB"), gci)
                eng.wait_ge(S("b_o"), gci)
            eng.tensor_copy(mk.BB2[:, 0:2048], mk.BF1[:, :])
            eng.drain()
            eng.sem_inc(S("b_Sbf"), 1)
            eng.wait_ge(S("b_sc"), gci + 1)
            for b in range(4):
                eng.tensor_tensor(
                    mk.BB2[:, 2048 + 512 * b:2048 + 512 * (b + 1)],
                    mk.ps_f[b][:, :],
                    mk.CB[:, 2048 + 512 * b:2048 + 512 * (b + 1)],
                    ALU.mult)
            eng.drain()
            eng.sem_inc(S("b_msk"), 1)
            eng.wait_ge(S(f"b_q{p}"), 16 * (gci // 2 + 1))
            eng.tensor_mul(mk.BB2[:, 14336:16384],
                           mk.BB1[:, 12288 + 2048 * p:12288 + 2048 * p + 2048],
                           mk.CB[:, 0:2048])
            eng.drain()
            eng.sem_inc(S("b_qt"), 1)
            eng.wait_ge(S(f"b_k{p}"), 16 * (gci // 2 + 1))
            if gci >= 1:
                eng.wait_ge(S("b_stB"), gci)
            eng.tensor_tensor(
                mk.BB3[:, 0:2048].rearrange("p (h e) -> p h e", h=H),
                mk.BB1[:, 2048 * p:2048 * p + 2048]
                .rearrange("p (h e) -> p h e", h=H),
                SM[:, SM_KDP:SM_KDP + 16].unsqueeze(2).broadcast_to([128, H, HD]),
                ALU.mult)
            eng.drain()
            eng.sem_inc(S("b_kt"), 1)
            eng.wait_ge(S("b_o"), gci + 1)
            eng.wait_ge(S("b_gev"), 4 * (gci + 1))
            if gci >= 1:
                pp = (gci - 1) % 2
                eng.wait_ge(S(f"sp{pp}"),
                            16 * (self.spb + (gci - 1) // 2 + 1))
            for b in range(4):
                eng.tensor_tensor(mk.BF3[:, 512 * b:512 * (b + 1)],
                                  mk.ps_f[b][:, :],
                                  mk.BB2[:, 8192 + 512 * b:8192 + 512 * (b + 1)],
                                  ALU.mult)
            eng.drain()
            eng.sem_inc(S("b_og"), 4)
            rs = SM_RMS + 16 * p
            eng.wait_ge(S("st_sq"), self.stb + 2 * gci + 1)
            eng.tensor_reduce(SM[:, rs:rs + 16],
                              mk.BF0[:, 4096:6144].rearrange(
                                  "p (h e) -> p h e", h=H),
                              AX.X, ALU.add)
            eng.drain()
            eng.tensor_scalar(SM[:, rs:rs + 16], SM[:, rs:rs + 16],
                              1.0 / HD, EPS, ALU.mult, ALU.add)
            eng.drain()
            eng.sem_inc(S("st_var"), 1)
            eng.wait_ge(S("st_sqr"), self.stb + 2 * gci + 1)
            eng.reciprocal(SM[:, rs:rs + 16], SM[:, rs:rs + 16])
            eng.drain()
            eng.tensor_tensor(
                mk.BF3[:, :].rearrange("p (h e) -> p h e", h=H),
                mk.BF3[:, :].rearrange("p (h e) -> p h e", h=H),
                SM[:, rs:rs + 16].unsqueeze(2).broadcast_to([128, H, HD]),
                ALU.mult)
            eng.drain()
            eng.tensor_mul(mk.BB2[:, 6144:8192], mk.BF3[:, :],
                           mk.BB2[:, 10240:12288])
            eng.drain()
            eng.sem_inc(S("b_on"), 1)
            eng.wait_ge(S("b_aev"), 8 * (gci + 1))
            st = SM_ST + 8 * p
            eng.reduce_sum(SM[:, st:st + 1], mk.BF3[:, :], AX.X)
            eng.wait_ge(S("st_sq"), self.stb + 2 * gci + 2)
            eng.drain()
            eng.tensor_scalar_mul(SM[:, st + 2:st + 3], SM[:, st:st + 1], 1.0 / D)
            eng.tensor_scalar_mul(SM[:, st + 3:st + 4], SM[:, st + 1:st + 2],
                                  1.0 / D)
            eng.drain()
            eng.tensor_mul(SM[:, st + 7:st + 8], SM[:, st + 2:st + 3],
                           SM[:, st + 2:st + 3])
            eng.drain()
            eng.tensor_sub(SM[:, st + 3:st + 4], SM[:, st + 3:st + 4],
                           SM[:, st + 7:st + 8])
            eng.drain()
            eng.tensor_scalar_add(SM[:, st + 3:st + 4], SM[:, st + 3:st + 4], EPS)
            eng.drain()
            eng.sem_inc(S("st_var"), 1)
            eng.wait_ge(S("st_sqr"), self.stb + 2 * gci + 2)
            eng.reciprocal(SM[:, st + 6:st + 7], SM[:, st + 5:st + 6])
            eng.drain()
            eng.tensor_scalar(SM[:, st + 4:st + 5], SM[:, st + 2:st + 3],
                              SM[:, st + 6:st + 7], -1.0, ALU.mult, ALU.mult)
            eng.drain()
            eng.sem_inc(S("b_lstat"), 1)
            eng.wait_ge(S("b_lnorm"), gci + 1)
            xbase = self.xb0 if p == 0 else self.xb1
            eng.wait_ge(S(f"x{p}"), 16 * (xbase + gci // 2 + 1))
            eng.tensor_add(mk.BF3[:, :], mk.BF0[:, 4096:6144],
                           mk.BF0[:, 2048 * p:2048 * p + 2048])
            eng.drain()
            eng.sem_inc(S("b_ln"), 1)

    def scalar(self, eng):
        mk = self.mk
        S = mk.sem
        SM = mk.SM
        i = mk.inp
        eng.dma_start(out=mk.BF0[:, 4096:4224], in_=i["posmat"][:, 0:128]
                      ).then_inc(S("b_initl"), 16)
        eng.dma_start(out=mk.BF0[:, 2048:2176], in_=i["identf"][:, :]
                      ).then_inc(S("b_initl"), 16)
        eng.dma_start(out=mk.BF0[0:1, 0:2048],
                      in_=i["norm_w"].rearrange("(o c) p -> o (c p)", o=1)
                      ).then_inc(S("b_initl"), 16)
        eng.dma_start(out=mk.BB2[:, 4096:6144], in_=mk.wg2d[:, :]
                      ).then_inc(S("b_initl"), 16)
        eng.wait_ge(S("b_init"), 1)
        for h in range(H):
            eng.activation(mk.CB[:, 128 * h:128 * (h + 1)],
                           mk.BF3[:, 128 * h:128 * (h + 1)], AF.Exp)
        eng.drain()
        eng.wait_ge(S("b_nw2"), 1)
        for sl in range(4):
            eng.activation(mk.BB2[:, 10240 + 512 * sl:10240 + 512 * (sl + 1)],
                           mk.ps_f[sl][:, :], AF.Identity)
        eng.drain()
        eng.sem_inc(S("b_init"), 1)
        for gci in range(self.nck):
            p = gci % 2
            for es in range(4):
                eng.wait_ge(S("b_gate"), 4 * gci + es + 1)
                eng.activation(mk.BB2[:, 8192 + 512 * es:8192 + 512 * (es + 1)],
                               mk.ps_f[4 + es % 2][:, :], AF.Sigmoid)
                eng.drain()
                eng.sem_inc(S("b_gev"), 1)
            eng.wait_ge(S("b_stA"), gci + 1)
            eng.activation(mk.BF1[:, 0:512], mk.ps_f[4][:, :], AF.Identity)
            eng.activation(mk.BF1[:, 512:1024], mk.ps_f[5][:, :], AF.Identity)
            eng.drain()
            eng.sem_inc(S("b_sevA"), 1)
            eng.wait_ge(S("b_stB"), gci + 1)
            eng.activation(mk.BF1[:, 1024:1536], mk.ps_f[4][:, :], AF.Identity)
            eng.activation(mk.BF1[:, 1536:2048], mk.ps_f[5][:, :], AF.Identity)
            eng.drain()
            eng.sem_inc(S("b_sevB"), 1)
            eng.wait_ge(S("b_og"), 4 * (gci + 1))
            eng.activation(mk.BF0[:, 4096:6144], mk.BF3[:, :], AF.Square)
            eng.drain()
            eng.sem_inc(S("st_sq"), 1)
            rs = SM_RMS + 16 * p
            eng.wait_ge(S("st_var"), self.stb + 2 * gci + 1)
            eng.activation(SM[:, rs:rs + 16], SM[:, rs:rs + 16], AF.Sqrt)
            eng.drain()
            eng.sem_inc(S("st_sqr"), 1)
            for half in range(2):
                eng.wait_ge(S("tr"), self.trb + 2 * gci + half + 1)
                for hh in range(8):
                    h = 8 * half + hh
                    eng.tensor_copy(
                        mk.BB2[:, 12288 + 128 * h:12288 + 128 * (h + 1)],
                        mk.ps_b[:, 1024 * half + 128 * hh:
                                1024 * half + 128 * (hh + 1)])                         if False else                     eng.activation(
                        mk.BB2[:, 12288 + 128 * h:12288 + 128 * (h + 1)],
                        mk.ps_b[:, 1024 * half + 128 * hh:
                                1024 * half + 128 * (hh + 1)], AF.Identity)
                eng.drain()
                eng.sem_inc(S("trev"), 1)
            for dms in range(4):
                eng.wait_ge(S("b_attn"), 8 * gci + 2 * dms + 2)
                eng.activation(mk.BF3[:, 512 * dms:512 * (dms + 1)],
                               mk.ps_f[4 + dms % 2][:, :], AF.Identity)
                eng.drain()
                eng.sem_inc(S("b_aev"), 2)
            st = SM_ST + 8 * p
            eng.wait_ge(S("b_aev"), 8 * (gci + 1))
            eng.activation(mk.BF0[:, 4096:6144], mk.BF3[:, :], AF.Square,
                           accum_out=SM[:, st + 1:st + 2])
            eng.drain()
            eng.sem_inc(S("st_sq"), 1)
            eng.wait_ge(S("st_var"), self.stb + 2 * gci + 2)
            eng.activation(SM[:, st + 5:st + 6], SM[:, st + 3:st + 4], AF.Sqrt)
            eng.drain()
            eng.sem_inc(S("st_sqr"), 1)
            eng.wait_ge(S("b_lstat"), gci + 1)
            eng.activation(mk.BF0[:, 4096:6144], mk.BF3[:, :], AF.Identity,
                           bias=SM[:, st + 4:st + 5], scale=SM[:, st + 6:st + 7])
            eng.drain()
            eng.sem_inc(S("b_lnorm"), 1)
            eng.wait_ge(S("b_ln"), gci + 1)
            if gci >= 2:
                eng.wait_ge(S(f"sp{p}"), 16 * (self.spb + gci // 2))
            eng.dma_start(out=mk.xmid[128 * gci:128 * (gci + 1), :],
                          in_=mk.BF3[:, :]).then_inc(S(f"sp{p}"), 16)
        eng.wait_ge(S("sp0"), 16 * (self.spb + (self.nck + 1) // 2))
        eng.wait_ge(S("sp1"), 16 * (self.spb + self.nck // 2))

    def gpsimd(self, eng):
        mk = self.mk
        S = mk.sem
        if "deq" in mk.phases:
            eng.wait_ge(S("agw"), 3)
            for (gr, lin) in ((mk.w1gr, mk.w1g), (mk.w2gr, mk.w2g)):
                for r in range(mk.n_cores):
                    eng.dma_start(out=lin[:, 704 * r:704 * (r + 1)],
                                  in_=gr[r, :, :]).then_inc(S("c_wready"), 16)


class PhC(Phase):

    def plan(self):
        mk = self.mk
        for n in ["c_ln", "c_ab", "c_aev", "c_bev", "c_h", "c_w3m", "c_oev",
                  "c_res", "c_w120", "c_w121",
                  "c_w30", "c_w31", "c_w32", "c_w33"]:
            mk.sem(n)
        n0, n1 = (self.mk.nchunks + 1) // 2, self.mk.nchunks // 2
        self.fb = (256 * n0, 256 * n1)
        self.nck = mk.nchunks
        self.ng = mk.ngc
        self.cpc = mk.cpc
        n0, n1 = mk.xb
        self.xb = (2 * n0, 2 * n1)
        self.stb = 3 * self.nck
        self.stb_stat = self.nck
        self.trb = 6 * self.nck
        self.spb = (mk.nsg * ((NCHK * (mk.sg_tok // 512)) // 2)
                    + (self.nck + 1) // 2,
                    mk.nsg * ((NCHK * (mk.sg_tok // 512)) // 2)
                    + self.nck // 2)

    def xcnt(self, gci):
        return self.xb[gci % 2] + gci // 2 + 1

    @property
    def cxb(self):
        return 4 if "B" in self.mk.phases else 0

    @property
    def bf3b(self):
        return (FCHK + 2 * NCHK) if "deq" in self.mk.phases else 0

    def ln3_ev(self, gci):
        h, ci = divmod(gci, 4)
        return (ci + 1) if h == 0 else (8 * h - 3 + ci)

    def ln4_ev(self, g, tc):
        if g < self.ng - 1:
            return 8 * g + 9 + tc
        return 8 * g + 5 + tc

    def w3_tile_dma(self, eng, g, il):
        mk = self.mk
        S = mk.sem
        i = g * 4 * FCHK + il
        if i >= 4:
            eng.wait_ge(S("c_w3m"), i - 3)
        dms, fc = divmod(il, FCHK)
        eng.dma_start(
            out=mk.BB3[:, 512 * (i % 4):512 * (i % 4 + 1)]
            .rearrange("p (r i) -> p r i", r=2),
            in_=mk.w3g[2 * dms:2 * dms + 2, 128 * fc:128 * (fc + 1), :]
            .rearrange("r f i -> f r i")
            ).then_inc(S(f"c_w3{i % 4}"), 16)

    def lnslot(self, gci):
        return self.mk.BB1[:, 8192 + 2048 * (gci % 4):
                           8192 + 2048 * (gci % 4) + 2048]

    def row(self, tc):
        mk = self.mk
        return (mk.BF0[:, 0:2048] if tc == 0 else
                mk.BF0[:, 2048:4096] if tc == 1 else
                mk.BF1[:, :] if tc == 2 else mk.BF2[:, :])

    def slab_dma(self, eng, g, fcp, gated):
        mk = self.mk
        S = mk.sem
        pw = fcp % 2
        if gated:
            eng.wait_ge(S("c_ab"), g * 2 * FCHK + 4 * (fcp - 1))
        for (wt, off) in ((mk.w1g, 0), (mk.w2g, 8192)):
            eng.dma_start(
                out=mk.BB2[:, off + 4096 * pw:off + 4096 * pw + 4096]
                .rearrange("p (c f) -> p c f", c=NCHK),
                in_=wt.rearrange("(c p) f -> p c f", p=128)
                [:, :, 256 * fcp:256 * (fcp + 1)]
                ).then_inc(S(f"c_w12{pw}"), 16)

    def sync(self, eng):
        mk = self.mk
        S = mk.sem
        if "deq" in mk.phases:
            eng.wait_ge(S("c_wready"), 16 * 16)
        for fcp in range(2):
            self.slab_dma(eng, 0, fcp, gated=False)
        for g in range(self.ng):
            for il in (0, 2):
                self.w3_tile_dma(eng, g, il)
            for fcp in range(2, FCHK // 2):
                self.slab_dma(eng, g, fcp, gated=True)
            if g + 1 < self.ng:
                for fcp in range(2):
                    eng.wait_ge(S("c_ab"),
                                g * 2 * FCHK + 4 * (FCHK // 2 - 1 + fcp))
                    self.slab_dma(eng, g + 1, fcp, gated=False)
            for il in range(4, 4 * FCHK, 2):
                self.w3_tile_dma(eng, g, il)
            for tc in range(self.cpc):
                gci = g * self.cpc + tc
                p = gci % 2
                eng.wait_ge(S("c_res"), gci + 1)
                if gci >= 2:
                    eng.wait_ge(S(f"sp{p}"), 16 * (self.spb[p] + gci // 2))
                eng.dma_start(out=mk.outp[128 * gci:128 * (gci + 1), :],
                              in_=self.row(tc)).then_inc(S(f"sp{p}"), 16)

    def t_trans(self, eng, gci):
        mk = self.mk
        S = mk.sem
        eng.wait_ge(S("c_ln"), gci + 1)
        lns = self.lnslot(gci)
        for s in range(4):
            b = (0, 2, 1, 3)[s]
            if gci * 4 + s >= 4:
                eng.wait_ge(S("trev"), self.trb + gci * 4 + s - 3)
            for k in range(4):
                dc = b * 4 + k
                mm = eng.transpose(
                    mk.ps_b[:, 512 * b + 128 * k:512 * b + 128 * k + 128],
                    lns[:, 128 * dc:128 * (dc + 1)],
                    mk.IDT[:, :])
            mm.then_inc(S("tr"), 1)

    def tensor(self, eng):
        mk = self.mk
        S = mk.sem
        for g in range(self.ng):
            if g == 0:
                for ci in range(self.cpc):
                    self.t_trans(eng, ci)
            eng.wait_ge(S("trev"), self.trb + (g + 1) * self.cpc * 4)
            for fc in range(FCHK):
                fcp = fc // 2
                pw = fcp % 2
                f2 = fc % 2
                ba, bb = (4, 5) if fc % 2 == 0 else (0, 1)
                if f2 == 0:
                    eng.wait_ge(S(f"c_w12{pw}"),
                                32 * (g * (FCHK // 2) // 2 + fcp // 2 + 1))
                if fc == 0:
                    if g >= 1:
                        eng.wait_ge(S("c_aev"), g * FCHK - 1)
                        eng.wait_ge(S("c_bev"), g * FCHK - 1)
                elif fc == 1:
                    if g >= 1:
                        eng.wait_ge(S("c_oev"), 16 * g)
                        eng.wait_ge(S("c_aev"), g * FCHK)
                        eng.wait_ge(S("c_bev"), g * FCHK)
                else:
                    eng.wait_ge(S("c_aev"), g * FCHK + fc - 1)
                    eng.wait_ge(S("c_bev"), g * FCHK + fc - 1)
                for dc in range(NCHK):
                    eng.matmul(mk.ps_f[ba][:, :],
                               mk.BB2[:, 4096 * pw + 256 * dc + 128 * f2:
                                      4096 * pw + 256 * dc + 128 * (f2 + 1)],
                               mk.BB1[:, 512 * dc:512 * (dc + 1)],
                               start=(dc == 0), stop=(dc == NCHK - 1))
                for dc in range(NCHK):
                    mm = eng.matmul(mk.ps_f[bb][:, :],
                                    mk.BB2[:, 8192 + 4096 * pw + 256 * dc + 128 * f2:
                                           8192 + 4096 * pw + 256 * dc + 128 * (f2 + 1)],
                                    mk.BB1[:, 512 * dc:512 * (dc + 1)],
                                    start=(dc == 0), stop=(dc == NCHK - 1))
                mm.then_inc(S("c_ab"), 2)
            eng.wait_ge(S("c_h"), (g + 1) * FCHK)
            for dms in range(4):
                if g * 4 + dms >= 1:
                    eng.wait_ge(S("c_oev"), 4 * (g * 4 + dms))
                for fc in range(FCHK):
                    i = (g * 4 + dms) * FCHK + fc
                    eng.wait_ge(S(f"c_w3{i % 4}"), 16 * (i // 4 + 1))
                    for tc in range(4):
                        mm = eng.matmul(
                            mk.ps_f[tc][:, :],
                            mk.BB0[:, 512 * fc + 128 * tc:
                                   512 * fc + 128 * (tc + 1)],
                            mk.BB3[:, 512 * (i % 4):512 * (i % 4) + 512],
                            start=(fc == 0), stop=(fc == FCHK - 1))
                    mm.then_inc(S("c_w3m"), 1)
                if g < self.ng - 1:
                    self.t_trans(eng, (g + 1) * self.cpc + dms)

    def xslot(self, gci):
        return self.mk.XB[:, 2048 * (gci % 2):2048 * (gci % 2) + 2048]

    def v_ln3(self, eng, gci):
        mk = self.mk
        S = mk.sem
        SM = mk.SM
        p = gci % 2
        st = SM_ST + 8 * p
        xs = self.xslot(gci)
        ev = self.ln3_ev(gci)
        eng.wait_ge(S("b_initl"), 16 * (gci + 1 + self.cxb))
        if gci >= 2:
            eng.wait_ge(S("c_ln"), gci - 1)
        eng.reduce_sum(SM[:, st:st + 1], xs, AX.X)
        eng.wait_ge(S("st_sq"), self.stb + ev)
        eng.drain()
        self.v_stats(eng, st, ev)

    def v_stats(self, eng, st, ev):
        mk = self.mk
        S = mk.sem
        SM = mk.SM
        eng.tensor_scalar_mul(SM[:, st + 2:st + 3], SM[:, st:st + 1],
                              1.0 / D)
        eng.tensor_scalar_mul(SM[:, st + 3:st + 4],
                              SM[:, st + 1:st + 2], 1.0 / D)
        eng.drain()
        eng.tensor_mul(SM[:, st + 7:st + 8], SM[:, st + 2:st + 3],
                       SM[:, st + 2:st + 3])
        eng.drain()
        eng.tensor_sub(SM[:, st + 3:st + 4], SM[:, st + 3:st + 4],
                       SM[:, st + 7:st + 8])
        eng.drain()
        eng.tensor_scalar_add(SM[:, st + 3:st + 4],
                              SM[:, st + 3:st + 4], EPS)
        eng.drain()
        eng.sem_inc(S("st_var"), 1)
        eng.wait_ge(S("st_sqr"), self.stb + ev)
        eng.reciprocal(SM[:, st + 6:st + 7], SM[:, st + 5:st + 6])
        eng.drain()
        eng.tensor_scalar(SM[:, st + 4:st + 5], SM[:, st + 2:st + 3],
                          SM[:, st + 6:st + 7], -1.0, ALU.mult, ALU.mult)
        eng.drain()
        eng.sem_inc(S("st_stat"), 1)

    def v_trev(self, eng, gci):
        mk = self.mk
        S = mk.sem
        ci = gci % 4
        for s in range(4):
            b = (0, 2, 1, 3)[s]
            eng.wait_ge(S("tr"), self.trb + gci * 4 + s + 1)
            for k in range(4):
                dc = b * 4 + k
                eng.tensor_copy(
                    mk.BB1[:, 512 * dc + 128 * ci:
                           512 * dc + 128 * (ci + 1)],
                    mk.ps_b[:, 512 * b + 128 * k:512 * b + 128 * k + 128])
            eng.drain()
            eng.sem_inc(S("trev"), 1)

    def vector(self, eng):
        mk = self.mk
        S = mk.sem
        SM = mk.SM
        for ci in range(self.cpc):
            self.v_ln3(eng, ci)
        for ci in range(self.cpc):
            self.v_trev(eng, ci)
        for g in range(self.ng):
            for fc in range(FCHK):
                eng.wait_ge(S("c_ab"), g * 2 * FCHK + 2 * (fc + 1))
                eng.tensor_copy(mk.CB[:, 512:1024],
                                mk.ps_f[5 if fc % 2 == 0 else 1][:, :])
                eng.drain()
                eng.sem_inc(S("c_bev"), 1)
                eng.wait_ge(S("c_aev"), g * FCHK + fc + 1)
                eng.tensor_mul(mk.BB0[:, 512 * fc:512 * (fc + 1)],
                               mk.CB[:, 0:512], mk.CB[:, 512:1024])
                eng.drain()
                eng.sem_inc(S("c_h"), 1)
            if g < self.ng - 1:
                for ci in range(self.cpc):
                    self.v_ln3(eng, (g + 1) * self.cpc + ci)
            for dms in range(4):
                eng.wait_ge(S("c_w3m"), (g * 4 + dms + 1) * FCHK)
                if dms == 0 and g >= 1:
                    eng.wait_ge(S("sp0"), 16 * (self.spb[0] + 2 * g))
                    eng.wait_ge(S("sp1"), 16 * (self.spb[1] + 2 * g))
                for tc in range(4):
                    eng.tensor_copy(self.row(tc)[:, 512 * dms:512 * (dms + 1)],
                                    mk.ps_f[tc][:, :])
                eng.drain()
                eng.sem_inc(S("c_oev"), 4)
                if g < self.ng - 1 and dms >= 1:
                    self.v_trev(eng, (g + 1) * self.cpc + dms - 1)
            if g < self.ng - 1:
                self.v_trev(eng, (g + 1) * self.cpc + 3)
            for tc in range(self.cpc):
                gci = g * self.cpc + tc
                p = tc % 2
                st = SM_ST + 8 * p
                ev = self.ln4_ev(g, tc)
                row = self.row(tc)
                eng.reduce_sum(SM[:, st:st + 1], row, AX.X)
                eng.wait_ge(S("st_sq"), self.stb + ev)
                eng.drain()
                self.v_stats(eng, st, ev)
                eng.wait_ge(S("b_lnorm"), self.nck + gci + 1)
                eng.wait_ge(S("d_out"), 16 * (self.bf3b + gci + 1))
                eng.tensor_add(row, mk.BF0[:, 4096:6144], mk.BF3[:, :])
                eng.drain()
                eng.sem_inc(S("c_res"), 1)

    def s_ln3(self, eng, gci):
        mk = self.mk
        S = mk.sem
        SM = mk.SM
        p = gci % 2
        st = SM_ST + 8 * p
        xs = self.xslot(gci)
        ev = self.ln3_ev(gci)
        eng.wait_ge(S("b_initl"), 16 * (gci + 1 + self.cxb))
        eng.activation(mk.BF0[:, 4096:6144], xs, AF.Square,
                       accum_out=SM[:, st + 1:st + 2])
        eng.drain()
        eng.sem_inc(S("st_sq"), 1)
        eng.wait_ge(S("st_var"), self.stb + ev)
        eng.activation(SM[:, st + 5:st + 6], SM[:, st + 3:st + 4],
                       AF.Sqrt)
        eng.drain()
        eng.sem_inc(S("st_sqr"), 1)
        eng.wait_ge(S("st_stat"), self.stb_stat + ev)
        if gci >= 4:
            eng.wait_ge(S("tr"), self.trb + 4 * (gci - 3))
        eng.activation(self.lnslot(gci), xs, AF.Identity,
                       bias=SM[:, st + 4:st + 5],
                       scale=SM[:, st + 6:st + 7])
        eng.drain()
        eng.sem_inc(S("c_ln"), 1)
        if gci + 2 < self.nck:
            self.xload(eng, gci + 2)

    def xload(self, eng, gci):
        mk = self.mk
        S = mk.sem
        if gci >= 1:
            eng.wait_ge(S("b_initl"), 16 * (gci + self.cxb))
        if gci >= 2:
            eng.wait_ge(S("c_ln"), gci - 1)
        eng.dma_start(out=self.xslot(gci),
                      in_=mk.xmid[128 * gci:128 * (gci + 1), :]
                      ).then_inc(S("b_initl"), 16)

    def scalar(self, eng):
        mk = self.mk
        S = mk.sem
        SM = mk.SM
        for gci in range(2):
            self.xload(eng, gci)
        for ci in range(self.cpc):
            self.s_ln3(eng, ci)
        for g in range(self.ng):
            for fc in range(FCHK):
                eng.wait_ge(S("c_ab"), g * 2 * FCHK + 2 * fc + 2)
                if fc >= 1:
                    eng.wait_ge(S("c_h"), g * FCHK + fc)
                eng.activation(mk.CB[:, 0:512],
                               mk.ps_f[4 if fc % 2 == 0 else 0][:, :], AF.Silu)
                eng.drain()
                eng.sem_inc(S("c_aev"), 1)
            if g < self.ng - 1:
                for ci in range(self.cpc):
                    self.s_ln3(eng, (g + 1) * self.cpc + ci)
            for tc in range(self.cpc):
                gci = g * self.cpc + tc
                p = tc % 2
                st = SM_ST + 8 * p
                ev = self.ln4_ev(g, tc)
                row = self.row(tc)
                eng.wait_ge(S("c_oev"), 16 * (g + 1))
                eng.wait_ge(S("c_res"), gci)
                eng.dma_start(out=mk.BF3[:, :],
                              in_=mk.xmid[128 * gci:128 * (gci + 1), :]
                              ).then_inc(S("d_out"), 16)
                eng.activation(mk.BF0[:, 4096:6144], row, AF.Square,
                               accum_out=SM[:, st + 1:st + 2])
                eng.drain()
                eng.sem_inc(S("st_sq"), 1)
                eng.wait_ge(S("st_var"), self.stb + ev)
                eng.activation(SM[:, st + 5:st + 6], SM[:, st + 3:st + 4],
                               AF.Sqrt)
                eng.drain()
                eng.sem_inc(S("st_sqr"), 1)
                eng.wait_ge(S("st_stat"), self.stb_stat + ev)
                eng.activation(mk.BF0[:, 4096:6144], row, AF.Identity,
                               bias=SM[:, st + 4:st + 5],
                               scale=SM[:, st + 6:st + 7])
                eng.drain()
                eng.sem_inc(S("b_lnorm"), 1)
        eng.wait_ge(S("sp0"), 16 * (self.spb[0] + (self.nck + 1) // 2))
        eng.wait_ge(S("sp1"), 16 * (self.spb[1] + self.nck // 2))

    def gpsimd(self, eng):
        mk = self.mk
        S = mk.sem
        for g in range(self.ng):
            for il in range(1, 4 * FCHK, 2):
                self.w3_tile_dma(eng, g, il)
        eng.wait_ge(S("sp0"), 16 * (self.spb[0] + (self.nck + 1) // 2))
        eng.wait_ge(S("sp1"), 16 * (self.spb[1] + self.nck // 2))


from concourse.bass_utils import run_bass_kernel_spmd


def kernel(L0_attn_x, L0_attn_log_decay, L0_attn_Wq, L0_attn_Wk, L0_attn_Wv,
           L0_attn_Wo, L0_attn_Wg1, L0_attn_Wg2, L0_attn_norm_w,
           L0_ffn_weight1, L0_ffn_weight2, L0_ffn_weight3):
    NCORES = 8
    TPC = 4096
    x = np.ascontiguousarray(
        np.asarray(L0_attn_x, dtype=np.float32).reshape(NCORES * TPC, D))
    ld = np.ascontiguousarray(
        np.asarray(L0_attn_log_decay, dtype=np.float32).reshape(16, 1))
    Wq = np.ascontiguousarray(np.asarray(L0_attn_Wq, dtype=np.float32))
    Wk = np.ascontiguousarray(np.asarray(L0_attn_Wk, dtype=np.float32))
    Wv = np.ascontiguousarray(np.asarray(L0_attn_Wv, dtype=np.float32))
    Wo = np.ascontiguousarray(np.asarray(L0_attn_Wo, dtype=np.float32))
    Wg1 = np.ascontiguousarray(np.asarray(L0_attn_Wg1, dtype=np.float32))
    Wg2 = np.ascontiguousarray(np.asarray(L0_attn_Wg2, dtype=np.float32))
    nw = np.ascontiguousarray(
        np.asarray(L0_attn_norm_w, dtype=np.float32).reshape(NCHK, 128))
    w1 = np.asarray(L0_ffn_weight1).view(ml_dtypes.bfloat16) \
        if np.asarray(L0_ffn_weight1).dtype != ml_dtypes.bfloat16 \
        else np.asarray(L0_ffn_weight1)
    w2 = np.asarray(L0_ffn_weight2)
    w3 = np.asarray(L0_ffn_weight3)
    w1 = np.ascontiguousarray(np.asarray(w1).reshape(DFF, D))
    w2 = np.ascontiguousarray(np.asarray(w2).reshape(DFF, D))
    w3 = np.ascontiguousarray(np.asarray(w3).reshape(D, DFF))
    w1t = np.ascontiguousarray(w1.T)
    w2t = np.ascontiguousarray(w2.T)
    w3t = np.ascontiguousarray(w3.T)

    mk = MK(tpc=TPC, n_cores=NCORES)
    nc = mk.build()

    in_maps = []
    for c in range(NCORES):
        m = {
            "x": np.ascontiguousarray(x[TPC * c:TPC * (c + 1)]),
            "log_decay": ld,
            "Wq": Wq, "Wk": Wk, "Wv": Wv, "Wo": Wo,
            "Wg1": Wg1, "Wg2": Wg2, "norm_w": nw,
            "w1s": np.ascontiguousarray(w1[704 * c:704 * (c + 1)]),
            "w2s": np.ascontiguousarray(w2[704 * c:704 * (c + 1)]),
            "w3s": np.ascontiguousarray(w3[256 * c:256 * (c + 1)]),
            "w1ts": np.ascontiguousarray(w1t[:, 704 * c:704 * (c + 1)]),
            "w2ts": np.ascontiguousarray(w2t[:, 704 * c:704 * (c + 1)]),
            "w3ts": np.ascontiguousarray(w3t[:, 256 * c:256 * (c + 1)]),
        }
        m.update(host_consts(c))
        in_maps.append(m)

    res = run_bass_kernel_spmd(nc, in_maps, core_ids=list(range(NCORES)))
    out = np.concatenate([np.asarray(res.results[c]["out"], dtype=np.float32)
                          .reshape(TPC, D) for c in range(NCORES)], axis=0)
    return out.reshape(4, 8192, D)


# version 11
# speedup vs baseline: 1.1376x; 1.0048x over previous
import numpy as np
import ml_dtypes
from contextlib import ExitStack

import concourse.bass as bass
import concourse.mybir as mybir

F32 = mybir.dt.float32
BF16 = mybir.dt.bfloat16
I32 = mybir.dt.int32
AF = mybir.ActivationFunctionType
ALU = mybir.AluOpType
AX = mybir.AxisListType

D = 2048
H = 16
HD = 128
DFF = 5632
NCHK = 16
FCHK = 44
EPS = 1e-5

NF4 = np.array([-1.0, -0.6962, -0.5251, -0.3949, -0.2844, -0.1848, -0.0911,
                0.0, 0.0796, 0.1609, 0.2461, 0.3379, 0.4407, 0.5626, 0.723,
                1.0], dtype=ml_dtypes.bfloat16).astype(np.float32)
NF4_MID = [float(x) for x in ((NF4[:-1] + NF4[1:]) / 2)]
NF4_DELTA = [float(x) for x in (NF4[1:] - NF4[:-1])]


def host_consts(core_id):
    c = {}
    c["ident"] = np.eye(128, dtype=np.float32).astype(ml_dtypes.bfloat16)
    c["identf"] = np.eye(128, dtype=np.float32)
    c["ones1"] = np.ones((1, 128), dtype=np.float32)
    c["onesc"] = np.ones((128, 1), dtype=np.float32)
    c["pos_127"] = (127 - np.arange(128, dtype=np.float32)).reshape(128, 1)
    i = np.arange(128)
    dT = (i[None, :] - i[:, None]).astype(np.float32)
    dT[dT < 0] = 1e9
    c["diffT"] = dT
    ho = np.zeros((128, H, H), dtype=np.float32)
    for h in range(H):
        ho[:, h, h] = 1.0
    c["headones"] = ho.reshape(128, H * H)
    eb = np.zeros((2, 128), dtype=np.float32)
    eb[0, 0:64] = 1.0
    eb[1, 64:128] = 1.0
    c["ebase2"] = eb
    c["posmat"] = np.tile(np.arange(1, 129, dtype=np.float32), (128, 4))
    m8 = np.zeros((1, 8), dtype=np.float32)
    if core_id % 2 == 1:
        m8[0, core_id - 1] = 1.0
    c["mask8"] = m8
    return c


SM_LDB = 0
SM_CD128 = 16
SM_CDSUP = 32
SM_KDP = 48
SM_KDCD = 64
SM_NW = 192
SM_HO = 208
SM_M8B = 464
SM_ST = 472
SM_ARG = 492
SM_ROW = 640
SM_MEAN = 920
SM_P127 = 960
SM_ONEC = 961
SM_RMS = 962
SM_KARG = 476


class MK:
    def __init__(self, tpc=4096, n_cores=8, phases=None):
        self.tpc = tpc
        self.n_cores = n_cores
        self.sg_tok = min(1024, tpc)
        self.cps = self.sg_tok // 128
        self.nsg = tpc // self.sg_tok
        self.gb_tok = min(512, tpc)
        self.ngb = tpc // self.gb_tok
        self.cpb = self.gb_tok // 128
        self.gc_tok = min(512, tpc)
        self.ngc = tpc // self.gc_tok
        self.cpc = self.gc_tok // 128
        self.nchunks = tpc // 128
        self.phases = phases or ["setup", "deq", "A", "AGstate", "B", "C"]
        self.nc = bass.Bass("TRN2", target_bir_lowering=False)
        self.stack = ExitStack()
        self.declare()

    def declare(self):
        nc = self.nc
        t = self.tpc
        self.inp = {}
        for name, shape, dt in [
            ("x", [t, D], F32),
            ("log_decay", [16, 1], F32),
            ("Wq", [D, D], F32), ("Wk", [D, D], F32), ("Wv", [D, D], F32),
            ("Wo", [D, D], F32), ("Wg1", [D, HD], F32), ("Wg2", [HD, D], F32),
            ("norm_w", [NCHK, 128], F32),
            ("w1s", [DFF // 8, D], BF16), ("w2s", [DFF // 8, D], BF16),
            ("w3s", [D // 8, DFF], BF16),
            ("w1ts", [D, DFF // 8], BF16), ("w2ts", [D, DFF // 8], BF16),
            ("w3ts", [DFF, D // 8], BF16),
        ]:
            self.inp[name] = nc.dram_tensor(name, shape, dt, kind="ExternalInput").ap()
        for name, arr in host_consts(0).items():
            dtv = BF16 if arr.dtype == ml_dtypes.bfloat16 else F32
            self.inp[name] = nc.dram_tensor(
                name, list(arr.shape), dtv, kind="ExternalInput").ap()
        self.outp = nc.dram_tensor("out", [t, D], F32, kind="ExternalOutput").ap()

        dr = lambda n, s, d: nc.dram_tensor(n, s, d, kind="Internal").ap()
        self.wq_bf = dr("wq_bf", [NCHK, 128, D], BF16)
        self.wk_bf = dr("wk_bf", [NCHK, 128, D], BF16)
        self.wv_bf = dr("wv_bf", [NCHK, 128, D], BF16)
        self.wo_bf = dr("wo_bf", [NCHK, 128, D], BF16)
        self.wg2d = dr("wg2d", [HD, D], BF16)
        self.q_sp = dr("q_sp", [NCHK, 128, t], BF16)
        self.k_sp = dr("k_sp", [t, D], BF16)
        self.v_sp = dr("v_sp", [t, D], BF16)
        self.xmid = dr("xmid", [t, D], F32)
        self.ag_a_in = dr("ag_a_in", [128, D], F32)
        self.ag_a_out = dr("ag_a_out", [self.n_cores, 128, D], F32)
        self.mean_ag_in = dr("mean_ag_in", [1, 4], F32)
        self.mean_ag_out = dr("mean_ag_out", [self.n_cores, 1, 4], F32)
        self.dum_in = dr("dum_in", [1, 4], F32)
        self.dum_out = dr("dum_out", [self.n_cores, 1, 4], F32)
        self.sflat = dr("sflat", [2, 22528], F32)
        self.w1sh = dr("w1sh", [D, DFF // 8], BF16)
        self.w2sh = dr("w2sh", [D, DFF // 8], BF16)
        self.w3sh = dr("w3sh", [DFF, D // 8], BF16)
        self.w1gr = dr("w1gr", [8, D, DFF // 8], BF16)
        self.w2gr = dr("w2gr", [8, D, DFF // 8], BF16)
        self.w1g = dr("w1g", [D, DFF], BF16)
        self.w2g = dr("w2g", [D, DFF], BF16)
        self.w3g = dr("w3g", [8, DFF, D // 8], BF16)

        sb = lambda n, s, d: self.stack.enter_context(nc.sbuf_tensor(n, s, d))
        self.BB0 = sb("BB0", [128, 32768], BF16)
        self.BB1 = sb("BB1", [128, 16384], BF16)
        self.BB2 = sb("BB2", [128, 16384], BF16)
        self.BB3 = sb("BB3", [128, 4096], BF16)
        self.G1R = sb("G1R", [128, 4096], BF16)
        self.CB = sb("CB", [128, 4096], BF16)
        self.IDT = sb("IDT", [128, 128], BF16)
        self.BF0 = sb("BF0", [128, 6144], F32)
        self.BF1 = sb("BF1", [128, 2048], F32)
        self.BF2 = sb("BF2", [128, 2048], F32)
        self.BF3 = sb("BF3", [128, 2048], F32)
        self.SM = sb("SM", [128, 1024], F32)
        self.ITMP = sb("ITMP", [128, 256], I32)
        self.XB = nc.virtual_sbuf_tensor("XB", [128, 4096], F32)
        nc._tensor(self.XB.name, self.XB.shape, self.XB.dtype, type="SB")
        mx = nc.lookup_mloc(self.XB)
        mx.allocated = True
        mx.addr = nc.lookup_mloc(self.BB0).addr + 49152
        self.ps_f = [self.stack.enter_context(
            nc.psum_tensor(f"psf{i}", [128, 512], F32)) for i in range(6)]
        self.ps_b = self.stack.enter_context(
            nc.psum_tensor("psb", [128, 2048], BF16))
        self.bar = self.stack.enter_context(nc.semaphore("bar"))
        self._sems = {}

    def sem(self, name):
        if name not in self._sems:
            self._sems[name] = self.stack.enter_context(self.nc.semaphore(name))
        return self._sems[name]

    @property
    def xb(self):
        return ((self.nchunks + 1) // 2, self.nchunks // 2)

    def build(self):
        nc = self.nc
        reg = {"setup": PhSetup, "deq": PhDeq, "A": PhA, "AGstate": PhAGState,
               "B": PhB, "C": PhC}
        self.ph_objs = [reg[p](self) for p in self.phases]
        groups = []
        i = 0
        while i < len(self.phases):
            if (self.phases[i] == "setup" and i + 1 < len(self.phases)
                    and self.phases[i + 1] == "deq"):
                groups.append([i, i + 1])
                i += 2
            else:
                groups.append([i])
                i += 1

        def drive(engname):
            def cb(eng):
                for gi, idxs in enumerate(groups):
                    eng.wait_ge(self.bar, 5 * gi)
                    for ix in idxs:
                        getattr(self.ph_objs[ix], engname)(eng)
                    eng.sem_inc(self.bar, 1)
            return cb

        with nc.Block() as block:
            block.sync(drive("sync"))
            block.tensor(drive("tensor"))
            block.vector(drive("vector"))
            block.scalar(drive("scalar"))
            block.gpsimd(drive("gpsimd"))
        return nc


class Phase:
    def __init__(self, mk: MK):
        self.mk = mk
        self.nc = mk.nc
        self.plan()

    def plan(self): pass
    def sync(self, eng): pass
    def tensor(self, eng): pass
    def vector(self, eng): pass
    def scalar(self, eng): pass
    def gpsimd(self, eng): pass


class PhSetup(Phase):

    def plan(self):
        mk = self.mk
        self.sl = mk.sem("su_l")
        self.sg_ = mk.sem("su_g")
        self.sc_ = mk.sem("su_c")

    def scalar(self, eng):
        mk, i = self.mk, self.mk.inp
        SM = mk.SM
        eng.dma_start(out=SM[0:1, SM_ROW:SM_ROW + 16],
                      in_=i["log_decay"].rearrange("h o -> o h")).then_inc(self.sl, 16)
        eng.dma_start(out=SM[0:1, SM_ROW + 16:SM_ROW + 144],
                      in_=i["ones1"][:, :]).then_inc(self.sl, 16)
        eng.dma_start(out=SM[0:2, SM_ROW + 144:SM_ROW + 272],
                      in_=i["ebase2"][:, :]).then_inc(self.sl, 16)
        eng.dma_start(out=SM[0:1, SM_ROW + 272:SM_ROW + 280],
                      in_=i["mask8"][:, :]).then_inc(self.sl, 16)
        eng.dma_start(out=SM[:, SM_P127:SM_P127 + 1],
                      in_=i["pos_127"][:, :]).then_inc(self.sl, 16)
        eng.dma_start(out=SM[:, SM_ONEC:SM_ONEC + 1],
                      in_=i["onesc"][:, :]).then_inc(self.sl, 16)
        eng.dma_start(out=SM[:, SM_NW:SM_NW + 16].rearrange("p (c o) -> p c o", c=NCHK),
                      in_=i["norm_w"].rearrange("c p -> p c").unsqueeze(2),
                      allow_slow_non_contiguous=True).then_inc(self.sl, 16)
        eng.dma_start(out=SM[:, SM_HO:SM_HO + 256],
                      in_=i["headones"][:, :]).then_inc(self.sl, 16)
        eng.dma_start(out=mk.BF1[:, 0:128], in_=i["diffT"][:, :]).then_inc(self.sl, 16)
        eng.dma_start(out=mk.IDT[:, :], in_=i["ident"][:, :]).then_inc(self.sl, 16)
        eng.wait_ge(self.sc_, 2)
        eng.activation(SM[:, SM_CD128:SM_CD128 + 16], SM[:, SM_LDB:SM_LDB + 16],
                       AF.Exp, scale=128.0)
        eng.activation(SM[:, SM_CDSUP:SM_CDSUP + 16], SM[:, SM_LDB:SM_LDB + 16],
                       AF.Exp, scale=float(128 * mk.cps))
        eng.activation(SM[:, SM_KDP:SM_KDP + 16], SM[:, SM_KARG:SM_KARG + 16], AF.Exp)
        for j in range(mk.cps):
            eng.activation(SM[:, SM_KDCD + 16 * j:SM_KDCD + 16 * (j + 1)],
                           SM[:, SM_ARG + 16 * j:SM_ARG + 16 * (j + 1)], AF.Exp)
        eng.wait_ge(self.sc_, 3)
        for h in range(H):
            eng.activation(mk.CB[:, 2048 + 128 * h:2048 + 128 * (h + 1)],
                           mk.BF2[:, 128 * h:128 * (h + 1)], AF.Exp)
        eng.drain()
        eng.sem_inc(self.sc_, 1)

    def tensor(self, eng):
        mk = self.mk
        SM = mk.SM
        eng.wait_ge(self.sl, 16 * 10)
        eng.matmul(mk.ps_f[0][:, 0:16], SM[0:1, SM_ROW + 16:SM_ROW + 144],
                   SM[0:1, SM_ROW:SM_ROW + 16], start=True, stop=False)
        mm = eng.matmul(mk.ps_f[0][:, 16:24], SM[0:1, SM_ROW + 16:SM_ROW + 144],
                        SM[0:1, SM_ROW + 272:SM_ROW + 280], start=False, stop=True)
        mm.then_inc(self.sc_, 1)

    def vector(self, eng):
        mk = self.mk
        SM = mk.SM
        eng.wait_ge(self.sc_, 1)
        eng.tensor_copy(SM[:, SM_LDB:SM_LDB + 16], mk.ps_f[0][:, 0:16])
        eng.tensor_copy(SM[:, SM_M8B:SM_M8B + 8], mk.ps_f[0][:, 16:24])
        eng.drain()
        eng.tensor_scalar_mul(SM[:, SM_KARG:SM_KARG + 16], SM[:, SM_LDB:SM_LDB + 16],
                              SM[:, SM_P127:SM_P127 + 1])
        eng.drain()
        for j in range(mk.cps):
            eng.tensor_scalar_mul(SM[:, SM_ARG + 16 * j:SM_ARG + 16 * (j + 1)],
                                  SM[:, SM_LDB:SM_LDB + 16],
                                  float(128 * (mk.cps - 1 - j)))
        eng.drain()
        for j in range(mk.cps):
            eng.tensor_add(SM[:, SM_ARG + 16 * j:SM_ARG + 16 * (j + 1)],
                           SM[:, SM_ARG + 16 * j:SM_ARG + 16 * (j + 1)],
                           SM[:, SM_KARG:SM_KARG + 16])
        eng.drain()
        eng.sem_inc(self.sc_, 1)
        for h in range(H):
            eng.tensor_scalar_mul(mk.BF2[:, 128 * h:128 * (h + 1)],
                                  mk.BF1[:, 0:128],
                                  SM[:, SM_LDB + h:SM_LDB + h + 1])
        eng.drain()
        eng.sem_inc(self.sc_, 1)

    def gpsimd(self, eng):
        mk, i = self.mk, self.mk.inp
        eng.collective_compute(
            "AllGather", ALU.bypass, replica_groups=[list(range(mk.n_cores))],
            ins=[mk.dum_in], outs=[mk.dum_out]).then_inc(self.sg_, 1)
        eng.dma_start(out=mk.CB[:, 0:2048].rearrange("p (c g) -> p c g", c=NCHK),
                      in_=i["Wg1"].rearrange("(c p) g -> p c g", p=128)
                      ).then_inc(self.sg_, 16)
        for name, dst in [("Wq", mk.wq_bf), ("Wk", mk.wk_bf), ("Wv", mk.wv_bf),
                          ("Wo", mk.wo_bf)]:
            eng.dma_start(out=dst[:, :, :],
                          in_=i[name].rearrange("(c p) e -> c p e", p=128)
                          ).then_inc(self.sg_, 16)
        eng.dma_start(out=mk.wg2d[:, :], in_=i["Wg2"][:, :]).then_inc(self.sg_, 16)
        eng.wait_ge(self.sg_, 16 * 6)


class PhA(Phase):

    CUT = 9
    CUT0 = 9

    def plan(self):
        mk = self.mk
        for n in ["x0", "x1", "a_ln", "tr", "trev", "a_mm", "a_ev2",
                  "sp0", "sp1", "a_spk0", "a_spk1", "a_spv0", "a_spv1",
                  "a_kvk0", "a_kvk1", "a_kvv0", "a_kvv1",
                  "a_kt", "a_st", "a_g1t", "a_g1s", "a_mrg",
                  "st_sq", "st_var", "st_sqr", "st_stat", "ag_a_d", "ag_a",
                  "a_w0", "a_w1", "a_w2", "a_w3"]:
            mk.sem(n)
        self.h2 = mk.sg_tok // 512
        self.nsh = mk.sg_tok // 512
        self.gq = NCHK * self.nsh
        self.gkv = mk.cps * 4
        self.gsg = self.gq + 2 * self.gkv
        self.sp_sg = self.gq + 2 * mk.cps

    def spill_done_wait(self, eng, sg):
        S = self.mk.sem
        for par in range(2):
            eng.wait_ge(S(f"sp{par}"), 16 * ((sg + 1) * self.gq // 2))
            eng.wait_ge(S(f"a_spk{par}"), 16 * ((sg + 1) * self.mk.cps // 2))
            eng.wait_ge(S(f"a_spv{par}"), 16 * ((sg + 1) * self.mk.cps // 2))

    def sync(self, eng):
        mk = self.mk
        S = mk.sem
        for sg in range(mk.nsg):
            for ci in range(mk.cps):
                gci = sg * mk.cps + ci
                if gci >= 2:
                    eng.wait_ge(S("a_ln"), gci - 1)
                eng.dma_start(out=mk.BF0[:, 2048 * (gci % 2):2048 * (gci % 2) + 2048],
                              in_=mk.inp["x"][128 * gci:128 * (gci + 1), :]
                              ).then_inc(S(f"x{gci % 2}"), 16)
            for wi, wsrc in enumerate([mk.wq_bf, mk.wk_bf, mk.wv_bf][:max(0, self.CUT - 2)]):
                eng.wait_ge(S("a_mm"), sg * self.gsg
                            + (0 if wi == 0 else self.gq + (wi - 1) * self.gkv))
                for dg in range(4):
                    eng.dma_start(out=mk.BB0[:, 8192 * dg:8192 * (dg + 1)]
                                  .rearrange("p (c e) -> p c e", c=4),
                                  in_=wsrc[4 * dg:4 * (dg + 1), :, :]
                                  .rearrange("c p e -> p c e")
                                  ).then_inc(S(f"a_w{dg}"), 16)
            if self.CUT < 6:
                continue
            self.spill_done_wait(eng, sg)
            if sg > 0:
                eng.wait_ge(S("a_st"), sg * mk.cps)
            for ci in range(mk.cps):
                if ci >= 2:
                    eng.wait_ge(S("a_st"), sg * mk.cps + ci - 1)
                gci = sg * mk.cps + ci
                n = sg * (mk.cps // 2) + ci // 2 + 1
                eng.dma_start(
                    out=mk.BB2[:, 4096 * (ci % 2):4096 * (ci % 2) + 2048],
                    in_=mk.k_sp[128 * gci:128 * (gci + 1), :]
                    ).then_inc(S(f"a_kvk{ci % 2}"), 16)
                eng.dma_start(
                    out=mk.BB2[:, 4096 * (ci % 2) + 2048:4096 * (ci % 2) + 4096],
                    in_=mk.v_sp[128 * gci:128 * (gci + 1), :]
                    ).then_inc(S(f"a_kvv{ci % 2}"), 16)

    def scalar(self, eng):
        mk = self.mk
        SM = mk.SM
        S = mk.sem
        for sg in range(mk.nsg):
            for ci in range(mk.cps):
                gci = sg * mk.cps + ci
                st = SM_ST + 8 * (gci % 2)
                xs = mk.BF0[:, 2048 * (gci % 2):2048 * (gci % 2) + 2048]
                eng.wait_ge(S(f"x{gci % 2}"), 16 * (gci // 2 + 1))
                if self.CUT0 >= 2:
                    eng.activation(mk.BF0[:, 4096:6144], xs, AF.Square,
                                   accum_out=SM[:, st + 1:st + 2])
                    eng.drain()
                eng.sem_inc(S("st_sq"), 1)
                eng.wait_ge(S("st_var"), gci + 1)
                if self.CUT0 >= 2:
                    eng.activation(SM[:, st + 5:st + 6], SM[:, st + 3:st + 4],
                                   AF.Sqrt)
                    eng.drain()
                eng.sem_inc(S("st_sqr"), 1)
                eng.wait_ge(S("st_stat"), gci + 1)
                if ci == 0 and sg > 0:
                    eng.wait_ge(S("a_st"), sg * mk.cps)
                if gci >= 1:
                    eng.wait_ge(S("tr"), 4 * gci)
                if self.CUT0 >= 3:
                    eng.activation(mk.BB3[:, 0:2048], xs, AF.Identity,
                                   bias=SM[:, st + 4:st + 5],
                                   scale=SM[:, st + 6:st + 7])
                    eng.drain()
                eng.sem_inc(S("a_ln"), 1)
            for ns in range(self.h2 if self.CUT >= 2 else 0):
                eng.wait_ge(S("a_g1t"), sg * self.h2 + ns + 1)
                eng.activation(mk.G1R[:, sg * mk.sg_tok + 512 * ns:
                                      sg * mk.sg_tok + 512 * ns + 512],
                               mk.ps_f[4 + ns % 2][:, :], AF.Identity)
                eng.drain()
                eng.sem_inc(S("a_g1s"), 1)
            for g in range(self.gq if self.CUT >= 3 else 0):
                ec, ns = divmod(g, self.nsh)
                eng.wait_ge(S("a_mm"), sg * self.gsg + g + 1)
                if g >= 2:
                    eng.wait_ge(S(f"sp{g % 2}"),
                                16 * (self.gq // 2 * sg + (g - 2) // 2 + 1))
                slot = mk.BB3[:, 512 * (g % 2):512 * (g % 2) + 512]
                eng.activation(slot, mk.ps_f[4 + g % 2][:, :], AF.Silu)
                eng.drain()
                eng.sem_inc(S("a_ev2"), 1)
                eng.dma_start(out=mk.q_sp[ec, :, sg * mk.sg_tok + 512 * ns:
                                          sg * mk.sg_tok + 512 * ns + 512],
                              in_=slot).then_inc(S(f"sp{g % 2}"), 16)
            for pi, (sp, tag) in enumerate(
                    [(mk.k_sp, "k"), (mk.v_sp, "v")][:max(0, self.CUT - 3)]):
                base = self.gq + pi * self.gkv
                for g in range(self.gkv):
                    tc, es = divmod(g, 4)
                    eng.wait_ge(S("a_mm"), sg * self.gsg + base + g + 1)
                    if g == 0:
                        if pi == 0:
                            for par in range(2):
                                eng.wait_ge(S(f"sp{par}"),
                                            16 * ((sg + 1) * self.gq // 2))
                        else:
                            for par in range(2):
                                eng.wait_ge(S(f"a_spk{par}"),
                                            16 * ((sg + 1) * mk.cps // 2))
                    if tc >= 2 and es == 0:
                        eng.wait_ge(S(f"a_sp{tag}{tc % 2}"),
                                    16 * (mk.cps // 2 * sg + (tc - 2) // 2 + 1))
                    slot = mk.BB3[:, 2048 * (tc % 2):2048 * (tc % 2) + 2048]
                    eng.activation(slot[:, 512 * es:512 * es + 512],
                                   mk.ps_f[4 + g % 2][:, :], AF.Silu)
                    eng.drain()
                    eng.sem_inc(S("a_ev2"), 1)
                    if es == 3:
                        gci = sg * mk.cps + tc
                        eng.dma_start(out=sp[128 * gci:128 * (gci + 1), :],
                                      in_=slot).then_inc(S(f"a_sp{tag}{tc % 2}"), 16)

    def vector(self, eng):
        mk = self.mk
        SM = mk.SM
        S = mk.sem
        for sg in range(mk.nsg):
            for ci in range(mk.cps):
                gci = sg * mk.cps + ci
                st = SM_ST + 8 * (gci % 2)
                xs = mk.BF0[:, 2048 * (gci % 2):2048 * (gci % 2) + 2048]
                eng.wait_ge(S(f"x{gci % 2}"), 16 * (gci // 2 + 1))
                if gci >= 2:
                    eng.wait_ge(S("a_ln"), gci - 1)
                if self.CUT0 >= 2:
                    eng.reduce_sum(SM[:, st:st + 1], xs, AX.X)
                eng.wait_ge(S("st_sq"), gci + 1)
                eng.drain()
                if self.CUT0 >= 2:
                    eng.tensor_scalar_mul(SM[:, st + 2:st + 3], SM[:, st:st + 1],
                                          1.0 / D)
                    eng.tensor_scalar_mul(SM[:, st + 3:st + 4],
                                          SM[:, st + 1:st + 2], 1.0 / D)
                    eng.drain()
                    eng.tensor_mul(SM[:, st + 7:st + 8], SM[:, st + 2:st + 3],
                                   SM[:, st + 2:st + 3])
                    eng.drain()
                    eng.tensor_sub(SM[:, st + 3:st + 4], SM[:, st + 3:st + 4],
                                   SM[:, st + 7:st + 8])
                    eng.drain()
                    eng.tensor_scalar_add(SM[:, st + 3:st + 4],
                                          SM[:, st + 3:st + 4], EPS)
                    eng.drain()
                eng.sem_inc(S("st_var"), 1)
                eng.wait_ge(S("st_sqr"), gci + 1)
                if self.CUT0 >= 2:
                    eng.reciprocal(SM[:, st + 6:st + 7], SM[:, st + 5:st + 6])
                    eng.drain()
                    eng.tensor_scalar(SM[:, st + 4:st + 5], SM[:, st + 2:st + 3],
                                      SM[:, st + 6:st + 7], -1.0,
                                      ALU.mult, ALU.mult)
                    eng.drain()
                eng.sem_inc(S("st_stat"), 1)
                eng.wait_ge(S("tr"), gci * 4 + 4)
                if self.CUT0 >= 5:
                    for dc in range(NCHK):
                        eng.tensor_copy(
                            mk.BB1[:, 1024 * dc + 128 * ci:
                                   1024 * dc + 128 * ci + 128],
                            mk.ps_b[:, 128 * dc:128 * dc + 128])
                    eng.drain()
                eng.sem_inc(S("trev"), 4)
            for ci in range(mk.cps if self.CUT >= 6 else 0):
                n = sg * (mk.cps // 2) + ci // 2 + 1
                eng.wait_ge(S(f"a_kvk{ci % 2}"), 16 * n)
                if ci >= 1:
                    eng.wait_ge(S("a_st"), sg * mk.cps + ci)
                kv = mk.BB2[:, 4096 * (ci % 2):4096 * (ci % 2) + 2048]
                kd = SM[:, SM_KDCD + 16 * ci:SM_KDCD + 16 * (ci + 1)]
                eng.tensor_tensor(
                    mk.BB3[:, 0:2048].rearrange("p (h e) -> p h e", h=H),
                    kv.rearrange("p (h e) -> p h e", h=H),
                    kd.unsqueeze(2).broadcast_to([128, H, HD]),
                    ALU.mult)
                eng.drain()
                eng.sem_inc(S("a_kt"), 1)
            if self.CUT < 6:
                continue
            eng.wait_ge(S("a_st"), (sg + 1) * mk.cps)
            if sg == 0:
                eng.memset(mk.BF1[:, :], 0.0)
                eng.drain()
            eng.tensor_tensor(
                mk.BF1[:, :].rearrange("p (h e) -> p h e", h=H),
                mk.BF1[:, :].rearrange("p (h e) -> p h e", h=H),
                SM[:, SM_CDSUP:SM_CDSUP + 16].unsqueeze(2)
                .broadcast_to([128, H, HD]), ALU.mult)
            eng.drain()
            for b in range(4):
                eng.tensor_add(mk.BF1[:, 512 * b:512 * b + 512],
                               mk.BF1[:, 512 * b:512 * b + 512], mk.ps_f[b][:, :])
            eng.drain()
            eng.sem_inc(S("a_mrg"), 1)

    def tensor(self, eng):
        mk = self.mk
        S = mk.sem
        for sg in range(mk.nsg):
            for ci in range(mk.cps):
                gci = sg * mk.cps + ci
                eng.wait_ge(S("a_ln"), gci + 1)
                for s in range(4):
                    b = (0, 2, 1, 3)[s]
                    if gci * 4 + s >= 4:
                        eng.wait_ge(S("trev"), gci * 4 + s - 3)
                    for k in range(4):
                        dc = b * 4 + k
                        mm = eng.transpose(
                            mk.ps_b[:, 512 * b + 128 * k:512 * b + 128 * k + 128],
                            mk.BB3[:, 128 * dc:128 * dc + 128], mk.IDT[:, :])
                    mm.then_inc(S("tr"), 1)
            eng.wait_ge(S("trev"), ((sg + 1) * mk.cps) * 4)
            if sg > 0 and self.CUT >= 3:
                eng.wait_ge(S("a_ev2"), sg * self.gsg)
            for ns in range(self.h2 if self.CUT >= 2 else 0):
                for dc in range(NCHK):
                    mm = eng.matmul(mk.ps_f[4 + ns % 2][:, :],
                               mk.CB[:, 128 * dc:128 * dc + 128],
                               mk.BB1[:, 1024 * dc + 512 * ns:
                                      1024 * dc + 512 * ns + 512],
                               start=(dc == 0), stop=(dc == NCHK - 1))
                mm.then_inc(S("a_g1t"), 1)
            eng.wait_ge(S("a_g1s"), (sg + 1) * self.h2 if self.CUT >= 2 else 0)
            for g in range(self.gq if self.CUT >= 3 else 0):
                ec, ns = divmod(g, self.nsh)
                if sg * self.gsg + g - 1 > 0:
                    eng.wait_ge(S("a_ev2"), sg * self.gsg + g - 1)
                for dc in range(NCHK):
                    if g == 0:
                        eng.wait_ge(S(f"a_w{dc // 4}"), 16 * (3 * sg + 1))
                    mm = eng.matmul(mk.ps_f[4 + g % 2][:, :],
                               mk.BB0[:, 2048 * dc + 128 * ec:
                                      2048 * dc + 128 * ec + 128],
                               mk.BB1[:, 1024 * dc + 512 * ns:
                                      1024 * dc + 512 * ns + 512],
                               start=(dc == 0), stop=(dc == NCHK - 1))
                mm.then_inc(S("a_mm"), 1)
            for g in range(self.gkv if self.CUT >= 4 else 0):
                tc, es = divmod(g, 4)
                eng.wait_ge(S("a_ev2"), sg * self.gsg + self.gq + g - 1)
                for dc in range(NCHK):
                    if g == 0:
                        eng.wait_ge(S(f"a_w{dc // 4}"), 16 * (3 * sg + 2))
                    mm = eng.matmul(mk.ps_f[4 + g % 2][:, :],
                               mk.BB1[:, 1024 * dc + 128 * tc:
                                      1024 * dc + 128 * tc + 128],
                               mk.BB0[:, 2048 * dc + 512 * es:
                                      2048 * dc + 512 * es + 512],
                               start=(dc == 0), stop=(dc == NCHK - 1))
                mm.then_inc(S("a_mm"), 1)
            for g in range(self.gkv if self.CUT >= 5 else 0):
                tc, es = divmod(g, 4)
                eng.wait_ge(S("a_ev2"),
                            sg * self.gsg + self.gq + self.gkv + g - 1)
                for dc in range(NCHK):
                    if g == 0:
                        eng.wait_ge(S(f"a_w{dc // 4}"), 16 * (3 * sg + 3))
                    mm = eng.matmul(mk.ps_f[4 + g % 2][:, :],
                               mk.BB1[:, 1024 * dc + 128 * tc:
                                      1024 * dc + 128 * tc + 128],
                               mk.BB0[:, 2048 * dc + 512 * es:
                                      2048 * dc + 512 * es + 512],
                               start=(dc == 0), stop=(dc == NCHK - 1))
                mm.then_inc(S("a_mm"), 1)
            if sg > 0 and self.CUT >= 6:
                eng.wait_ge(S("a_mrg"), sg)
            for ci in range(mk.cps if self.CUT >= 6 else 0):
                eng.wait_ge(S("a_kt"), sg * mk.cps + ci + 1)
                eng.wait_ge(S(f"a_kvv{ci % 2}"),
                            16 * (sg * (mk.cps // 2) + ci // 2 + 1))
                kt = mk.BB3[:, 0:2048]
                vv = mk.BB2[:, 4096 * (ci % 2) + 2048:4096 * (ci % 2) + 4096]
                for h in range(H):
                    mm = eng.matmul(
                        mk.ps_f[h // 4][:, 128 * (h % 4):128 * (h % 4) + 128],
                        kt[:, 128 * h:128 * h + 128],
                        vv[:, 128 * h:128 * h + 128],
                        start=(ci == 0 and h % 4 == 0),
                        stop=(ci == mk.cps - 1 and h % 4 == 3))
                mm.then_inc(S("a_st"), 1)

    def gpsimd(self, eng):
        if self.CUT < 7:
            return
        mk = self.mk
        S = mk.sem
        eng.wait_ge(S("a_mrg"), mk.nsg)
        eng.dma_start(out=mk.ag_a_in[:, :], in_=mk.BF1[:, :]).then_inc(S("ag_a_d"), 16)
        eng.wait_ge(S("ag_a_d"), 16)
        eng.collective_compute(
            "AllGather", ALU.bypass,
            replica_groups=[list(range(mk.n_cores))],
            ins=[mk.ag_a_in], outs=[mk.ag_a_out],
        ).then_inc(S("ag_a"), 1)


class PhAGState(Phase):

    def plan(self):
        for n in ["ag_a", "ags_l", "ags_v"]:
            self.mk.sem(n)

    def gpsimd(self, eng):
        mk = self.mk
        S = mk.sem
        eng.wait_ge(S("ag_a"), 1)
        for r in range(mk.n_cores):
            eng.wait_ge(S("ags_v"), r)
            eng.dma_start(out=mk.BF0[:, 0:2048], in_=mk.ag_a_out[r, :, :]
                          ).then_inc(S("ags_l"), 16)
        eng.wait_ge(S("ags_v"), mk.n_cores)

    def vector(self, eng):
        mk = self.mk
        S = mk.sem
        eng.memset(mk.BF1[:, :], 0.0)
        eng.drain()
        eng.sem_inc(S("ags_v"), 0) if False else None
        for r in range(mk.n_cores):
            eng.wait_ge(S("ags_l"), 16 * (r + 1))
            eng.scalar_tensor_tensor(mk.BF1[:, :], mk.BF0[:, 0:2048],
                                     mk.SM[:, SM_M8B + r:SM_M8B + r + 1],
                                     mk.BF1[:, :], ALU.mult, ALU.add)
            eng.drain()
            eng.sem_inc(S("ags_v"), 1)


class PhDeq(Phase):

    def plan(self):
        mk = self.mk
        for n in ["d_s1", "d_part", "d_agm", "d_mld",
                  "d_mn", "d_bnc", "d_arl", "d_art", "d_arv",
                  "d_wt", "d_stg", "d_exp", "d_eev", "d_fin", "d_out",
                  "agw", "c_wready", "su_l"]:
            mk.sem(n)
        self.NB = 180224

    def sync(self, eng):
        mk = self.mk
        S = mk.sem
        i = mk.inp
        for wi, name in enumerate(["w1s", "w2s", "w3s"]):
            if wi > 0:
                eng.wait_ge(S("d_s1"), wi)
            sh = i[name]
            flat = sh.rearrange("a b -> (a b)").rearrange("(s q) -> s q", q=16384)
            eng.dma_start(out=mk.BB1[0:88, :], in_=flat).then_inc(S("d_wt"), 16)
        eng.wait_ge(S("d_bnc"), 16 * 3)
        for a in range(2):
            for t in range(2):
                gi = 2 * a + t
                if gi >= 1:
                    eng.wait_ge(S("d_art"), gi)
                eng.dma_start(
                    out=mk.BF0[0:128, 4000 + 88 * t:4000 + 88 * (t + 1)],
                    in_=mk.sflat[a, :].rearrange("(x y) -> x y", y=88)
                    [128 * t:128 * (t + 1), :]).then_inc(S("d_arl"), 16)
        for fc in range(FCHK):
            if fc > 0:
                eng.wait_ge(S("d_fin"), fc)
            eng.dma_start(out=mk.BB3[:, 0:256],
                          in_=i["w3ts"][128 * fc:128 * (fc + 1), :]
                          ).then_inc(S("d_wt"), 16)
        for wi, name in enumerate(["w1ts", "w2ts"]):
            eng.wait_ge(S("d_bnc"), 16 * (5 + 2 * wi))
            for a in range(2):
                for t in range(6):
                    gi = 4 + wi * 12 + a * 6 + t
                    rows = 128 if t < 5 else 64
                    eng.wait_ge(S("d_art"), gi)
                    eng.dma_start(
                        out=mk.BF0[0:rows, 4000 + 32 * (t % 2):
                                   4000 + 32 * (t % 2) + 32],
                        in_=mk.sflat[a, :].rearrange("(x y) -> x y", y=32)
                        [128 * t:128 * t + rows, :]).then_inc(S("d_arl"), 16)
            for dc in range(NCHK):
                ti = FCHK + wi * NCHK + dc
                eng.wait_ge(S("d_fin"), ti)
                eng.dma_start(out=mk.BB3[:, 0:704],
                              in_=i[name][128 * dc:128 * (dc + 1), :]
                              ).then_inc(S("d_wt"), 16)

    def tensor(self, eng):
        mk = self.mk
        S = mk.sem
        SM = mk.SM
        for wi in range(3):
            eng.wait_ge(S("d_s1"), wi + 1)
            mm = eng.matmul(mk.ps_f[0][0:1, wi:wi + 1],
                            SM[0:88, 940 + wi:941 + wi],
                            SM[0:88, SM_ONEC:SM_ONEC + 1],
                            start=(wi == 0), stop=(wi == 2))
        mm.then_inc(S("d_part"), 1)
        eng.wait_ge(S("d_mld"), 16)
        for wi in range(3):
            mm = eng.matmul(mk.ps_f[1][0:1, wi:wi + 1],
                            SM[0:8, 924 + wi:925 + wi],
                            SM[0:8, SM_ONEC:SM_ONEC + 1],
                            start=(wi == 0), stop=(wi == 2))
        mm.then_inc(S("d_mn"), 1)
        eng.wait_ge(S("d_mn"), 2)
        mm = eng.matmul(mk.ps_f[1][:, 8:11], SM[0:1, SM_ROW + 16:SM_ROW + 144],
                        SM[0:1, 928:931], start=True, stop=True)
        mm.then_inc(S("d_mn"), 1)
        eng.wait_ge(S("su_l"), 16 * 11)
        for a in range(2):
            for t in range(2):
                gi = 2 * a + t
                eng.wait_ge(S("d_arl"), 16 * (gi + 1))
                mm = eng.transpose(mk.ps_f[2][0:88, 0:128],
                                   mk.BF0[0:128, 4000 + 88 * t:4000 + 88 * (t + 1)],
                                   SM[0:128, 472:600])
                mm.then_inc(S("d_art"), 1)
                eng.wait_ge(S("d_arv"), gi + 1)
        for fc in range(FCHK):
            eng.wait_ge(S("d_stg"), 32 * (fc + 1))
            eng.wait_ge(S("d_eev"), 2 * fc)
            mm = eng.matmul(mk.ps_f[4 + fc % 2][:, :],
                            SM[0:2, SM_ROW + 144:SM_ROW + 272],
                            mk.BF0[0:2, 1408:1920],
                            start=True, stop=True)
            mm.then_inc(S("d_exp"), 1)
        for wi in range(2):
            for a in range(2):
                for t in range(6):
                    gi = 4 + wi * 12 + a * 6 + t
                    rows = 128 if t < 5 else 64
                    eng.wait_ge(S("d_arl"), 16 * (gi + 1))
                    mm = eng.transpose(
                        mk.ps_f[2][0:32, 0:rows],
                        mk.BF0[0:rows, 4000 + 32 * (t % 2):
                               4000 + 32 * (t % 2) + 32],
                        SM[0:rows, 472:472 + rows])
                    mm.then_inc(S("d_art"), 1)
                    eng.wait_ge(S("d_arv"), gi + 1)
            for dc in range(NCHK):
                ti = FCHK + wi * NCHK + dc
                eng.wait_ge(S("d_stg"), 32 * (ti + 1))
                for rnd in range(2):
                    eng.wait_ge(S("d_eev"), 2 * ti + rnd)
                    off = 1408 + 704 * rnd
                    eng.matmul(mk.ps_f[4][:, 0:512],
                               SM[0:2, SM_ROW + 144:SM_ROW + 272],
                               mk.BF0[0:2, off:off + 512],
                               start=True, stop=True)
                    mm = eng.matmul(mk.ps_f[5][:, 0:192],
                                    SM[0:2, SM_ROW + 144:SM_ROW + 272],
                                    mk.BF0[0:2, off + 512:off + 704],
                                    start=True, stop=True)
                    mm.then_inc(S("d_exp"), 1)

    def vector(self, eng):
        mk = self.mk
        S = mk.sem
        SM = mk.SM
        BF1, BF2 = mk.BF1, mk.BF2
        for wi in range(3):
            eng.wait_ge(S("d_wt"), 16 * (wi + 1))
            eng.tensor_reduce(BF1[0:88, 256 * wi:256 * (wi + 1)],
                              mk.BB1[0:88, :].rearrange("s (b e) -> s b e", e=64),
                              AX.X, ALU.max, apply_absolute_value=True)
            eng.drain()
            eng.reduce_sum(SM[0:88, 940 + wi:941 + wi],
                           BF1[0:88, 256 * wi:256 * (wi + 1)], AX.X)
            eng.drain()
            eng.sem_inc(S("d_s1"), 1)
        eng.wait_ge(S("d_part"), 1)
        eng.memset(SM[0:1, 920:924], 0.0)
        eng.drain()
        eng.tensor_copy(SM[0:1, 920:923], mk.ps_f[0][0:1, 0:3])
        eng.drain()
        eng.sem_inc(S("d_part"), 1)
        eng.wait_ge(S("d_mn"), 1)
        eng.tensor_scalar_mul(SM[0:1, 928:931], mk.ps_f[1][0:1, 0:3],
                              1.0 / self.NB)
        eng.drain()
        eng.sem_inc(S("d_mn"), 1)
        eng.wait_ge(S("d_mn"), 3)
        eng.tensor_copy(SM[:, 936:939], mk.ps_f[1][:, 8:11])
        eng.drain()
        eng.sem_inc(S("d_mn"), 1)
        eng.wait_ge(S("su_c"), 4)
        for wi in range(3):
            s1 = BF1[0:88, 256 * wi:256 * (wi + 1)]
            eng.tensor_scalar(BF2[0:88, 0:256], s1, SM[0:88, 936 + wi:937 + wi],
                              None, ALU.subtract)
            eng.drain()
            eng.tensor_reduce(SM[0:88, 944:945], BF2[0:88, 0:256], AX.X,
                              ALU.max, apply_absolute_value=True)
            eng.drain()
            eng.reciprocal(SM[0:88, 945:946], SM[0:88, 944:945])
            eng.drain()
            eng.tensor_scalar_mul(SM[0:88, 946:947], SM[0:88, 945:946], 128.0)
            eng.drain()
            eng.reciprocal(SM[0:88, 947:948], SM[0:88, 946:947])
            eng.drain()
            eng.tensor_scalar_mul(BF2[0:88, 256:512], BF2[0:88, 0:256],
                                  SM[0:88, 946:947])
            eng.drain()
            eng.tensor_copy(mk.ITMP[0:88, :], BF2[0:88, 256:512])
            eng.drain()
            eng.tensor_copy(BF2[0:88, 256:512], mk.ITMP[0:88, :])
            eng.drain()
            eng.tensor_scalar_min(BF2[0:88, 256:512], BF2[0:88, 256:512], 127.0)
            eng.drain()
            eng.tensor_scalar_max(BF2[0:88, 256:512], BF2[0:88, 256:512], -128.0)
            eng.drain()
            eng.tensor_scalar_mul(mk.BB3[0:88, 3584:3840], BF2[0:88, 256:512],
                                  SM[0:88, 947:948])
            eng.drain()
            eng.tensor_scalar(BF2[0:88, 512 + 256 * wi:768 + 256 * wi],
                              mk.BB3[0:88, 3584:3840],
                              SM[0:88, 936 + wi:937 + wi], None, ALU.add)
            eng.drain()
            eng.reciprocal(BF2[0:88, 1280 + 256 * wi:1536 + 256 * wi], s1)
            eng.drain()
            eng.sem_inc(S("d_s1"), 1)
        for a in range(2):
            for t in range(2):
                gi = 2 * a + t
                eng.wait_ge(S("d_art"), gi + 1)
                eng.tensor_copy(
                    BF1[0:88, 768 + 256 * a + 128 * t:768 + 256 * a + 128 * (t + 1)],
                    mk.ps_f[2][0:88, 0:128])
                eng.drain()
                eng.sem_inc(S("d_arv"), 1)
        for fc in range(FCHK):
            eng.wait_ge(S("d_exp"), fc + 1)
            eng.tensor_copy(mk.BF3[:, 0:512], mk.ps_f[4 + fc % 2][:, :])
            eng.drain()
            eng.sem_inc(S("d_eev"), 2)
            self.emit_tile(eng, fc, 256)
        for wi in range(2):
            for a in range(2):
                for t in range(6):
                    gi = 4 + wi * 12 + a * 6 + t
                    cols = 128 if t < 5 else 64
                    eng.wait_ge(S("d_art"), gi + 1)
                    eng.tensor_copy(
                        mk.BF0[0:32, 704 * a + 128 * t:704 * a + 128 * t + cols],
                        mk.ps_f[2][0:32, 0:cols])
                    eng.drain()
                    eng.sem_inc(S("d_arv"), 1)
            for dc in range(NCHK):
                ti = FCHK + wi * NCHK + dc
                for rnd in range(2):
                    eng.wait_ge(S("d_exp"), FCHK + wi * 2 * NCHK + 2 * dc
                                + rnd + 1)
                    eng.tensor_copy(mk.BF3[:, 704 * rnd:704 * rnd + 512],
                                    mk.ps_f[4][:, 0:512])
                    eng.tensor_copy(mk.BF3[:, 704 * rnd + 512:704 * rnd + 704],
                                    mk.ps_f[5][:, 0:192])
                    eng.drain()
                    eng.sem_inc(S("d_eev"), 1)
                self.emit_tile(eng, ti, 704)
        eng.wait_ge(S("d_out"), 16 * (FCHK + 2 * NCHK))

    def emit_tile(self, eng, ti, W):
        mk = self.mk
        S = mk.sem
        if True:
            eng.wait_ge(S("d_wt"), 16 * (ti + 4))
            if ti >= 1:
                eng.wait_ge(S("d_out"), 16 * ti)
            wt = mk.BB3[:, 0:W]
            sc = mk.BB3[:, W:2 * W]
            acc = mk.BB3[:, 2 * W:3 * W]
            ind = mk.BB3[:, 3 * W:4 * W]
            eng.tensor_tensor(sc, wt, mk.BF3[:, 0:W], ALU.mult)
            eng.drain()
            eng.tensor_scalar(acc, sc, NF4_MID[0], NF4_DELTA[0],
                              ALU.is_gt, ALU.mult)
            eng.drain()
            for j in range(1, 15):
                eng.tensor_scalar(ind, sc, NF4_MID[j], None, ALU.is_gt)
                eng.drain()
                eng.scalar_tensor_tensor(acc, ind, NF4_DELTA[j], acc,
                                         ALU.mult, ALU.add)
                eng.drain()
            eng.scalar_tensor_tensor(ind, acc, -1.0, mk.BF3[:, W:2 * W],
                                     ALU.add, ALU.mult)
            eng.drain()
            eng.sem_inc(S("d_fin"), 1)

    def scalar(self, eng):
        mk = self.mk
        S = mk.sem
        SM = mk.SM
        i = mk.inp
        eng.dma_start(out=SM[:, 472:600], in_=i["identf"][:, :]
                      ).then_inc(S("su_l"), 16)
        eng.wait_ge(S("d_part"), 2)
        eng.dma_start(out=mk.mean_ag_in[:, :], in_=SM[0:1, 920:924]
                      ).then_inc(S("d_bnc"), 16)
        eng.wait_ge(S("d_s1"), 6)
        eng.dma_start(out=mk.sflat[0, :].rearrange("(a b) -> a b", b=256),
                      in_=mk.BF2[0:88, 1792:2048]).then_inc(S("d_bnc"), 16)
        eng.dma_start(out=mk.sflat[1, :].rearrange("(a b) -> a b", b=256),
                      in_=mk.BF2[0:88, 1024:1280]).then_inc(S("d_bnc"), 16)
        for fc in range(FCHK):
            eng.wait_ge(S("d_fin"), fc + 1)
            eng.dma_start(out=mk.w3sh[128 * fc:128 * (fc + 1), :],
                          in_=mk.BB3[:, 768:1024]).then_inc(S("d_out"), 16)
        for wi in range(2):
            eng.wait_ge(S("d_arv"), 4 + wi * 12)
            eng.dma_start(out=mk.sflat[0, :].rearrange("(a b) -> a b", b=256),
                          in_=mk.BF2[0:88, 1280 + 256 * wi:1536 + 256 * wi]
                          ).then_inc(S("d_bnc"), 16)
            eng.dma_start(out=mk.sflat[1, :].rearrange("(a b) -> a b", b=256),
                          in_=mk.BF2[0:88, 512 + 256 * wi:768 + 256 * wi]
                          ).then_inc(S("d_bnc"), 16)
            sh = mk.w1sh if wi == 0 else mk.w2sh
            for dc in range(NCHK):
                ti = FCHK + wi * NCHK + dc
                eng.wait_ge(S("d_fin"), ti + 1)
                eng.dma_start(out=sh[128 * dc:128 * (dc + 1), :],
                              in_=mk.BB3[:, 2112:2816]).then_inc(S("d_out"), 16)

    def gpsimd(self, eng):
        mk = self.mk
        S = mk.sem
        eng.wait_ge(S("d_bnc"), 16)
        eng.collective_compute(
            "AllGather", ALU.bypass, replica_groups=[list(range(mk.n_cores))],
            ins=[mk.mean_ag_in], outs=[mk.mean_ag_out]).then_inc(S("d_agm"), 1)
        eng.wait_ge(S("d_agm"), 1)
        eng.dma_start(out=mk.SM[0:8, 924:928],
                      in_=mk.mean_ag_out.rearrange("r o w -> (r o) w")
                      ).then_inc(S("d_mld"), 16)
        for fc in range(FCHK):
            eng.wait_ge(S("d_arv"), 4)
            if fc >= 1:
                eng.wait_ge(S("d_exp"), fc)
            eng.dma_start(out=mk.BF0[0:2, 1408:1664],
                          in_=mk.BF1[2 * fc:2 * fc + 2, 768:1024]
                          ).then_inc(S("d_stg"), 16)
            eng.dma_start(out=mk.BF0[0:2, 1664:1920],
                          in_=mk.BF1[2 * fc:2 * fc + 2, 1024:1280]
                          ).then_inc(S("d_stg"), 16)
        for wi in range(2):
            for dc in range(NCHK):
                ti = FCHK + wi * NCHK + dc
                eng.wait_ge(S("d_arv"), 4 + (wi + 1) * 12)
                eng.wait_ge(S("d_exp"), FCHK + wi * 2 * NCHK + 2 * dc)
                eng.dma_start(out=mk.BF0[0:2, 1408:2112],
                              in_=mk.BF0[2 * dc:2 * dc + 2, 0:704]
                              ).then_inc(S("d_stg"), 16)
                eng.dma_start(out=mk.BF0[0:2, 2112:2816],
                              in_=mk.BF0[2 * dc:2 * dc + 2, 704:1408]
                              ).then_inc(S("d_stg"), 16)
        eng.wait_ge(S("d_out"), 16 * (FCHK + 2 * NCHK))
        for (src, dst) in ((mk.w1sh, mk.w1gr), (mk.w2sh, mk.w2gr),
                           (mk.w3sh, mk.w3g)):
            eng.collective_compute(
                "AllGather", ALU.bypass,
                replica_groups=[list(range(mk.n_cores))],
                ins=[src], outs=[dst]).then_inc(S("agw"), 1)



class PhB(Phase):

    def plan(self):
        mk = self.mk
        for n in ["b_wo", "b_gate", "b_gev", "b_sc", "b_msk", "b_qt", "b_o",
                  "b_kt", "b_stA", "b_stB", "b_sevA", "b_sevB", "b_og",
                  "b_on", "b_attn", "b_aev", "b_ln", "b_Sbf", "b_init",
                  "b_q0", "b_q1", "b_k0", "b_k1", "b_v0",
                  "b_v1", "b_f0", "b_f1",
                  "b_initl", "b_nw2", "b_lstat", "b_lnorm"]:
            mk.sem(n)
        self.nck = mk.nchunks
        mkk = self.mk
        self.xb0, self.xb1 = mkk.xb
        self.trb = 4 * self.nck
        self.stb = self.nck
        self.spb = mkk.nsg * ((NCHK * (mkk.sg_tok // 512)) // 2)

    def sync(self, eng):
        mk = self.mk
        S = mk.sem
        for ec in range(NCHK):
            eng.dma_start(out=mk.BB0[:, 2048 * ec:2048 * (ec + 1)],
                          in_=mk.wo_bf[ec, :, :]).then_inc(S("b_wo"), 16)
        eng.wait_ge(S("b_init"), 2)
        for gci in range(self.nck):
            p = gci % 2
            if gci >= 2:
                eng.wait_ge(S("b_ln"), gci - 1)
            eng.dma_start(out=mk.BF0[:, 2048 * p:2048 * p + 2048],
                          in_=mk.inp["x"][128 * gci:128 * (gci + 1), :]
                          ).then_inc(S(f"x{p}"), 16)
            if gci >= 2:
                eng.wait_ge(S("b_qt"), gci - 1)
            eng.dma_start(out=mk.BB1[:, 12288 + 2048 * p:12288 + 2048 * p + 2048]
                          .rearrange("p (h e) -> p h e", h=H),
                          in_=mk.q_sp[:, :, 128 * gci:128 * (gci + 1)]
                          .rearrange("h p e -> p h e")).then_inc(S(f"b_q{p}"), 16)
            if gci >= 2:
                eng.wait_ge(S("b_kt"), gci - 1)
            eng.dma_start(out=mk.BB1[:, 2048 * p:2048 * p + 2048],
                          in_=mk.k_sp[128 * gci:128 * (gci + 1), :]
                          ).then_inc(S(f"b_k{p}"), 16)
            if gci >= 2:
                eng.wait_ge(S("b_stB"), gci - 1)
            eng.dma_start(out=mk.BB1[:, 4096 + 2048 * p:4096 + 2048 * p + 2048],
                          in_=mk.v_sp[128 * gci:128 * (gci + 1), :]
                          ).then_inc(S(f"b_v{p}"), 16)
            if gci >= 2:
                eng.wait_ge(S("b_sc"), gci - 1)
            for h in range(H):
                eng.dma_start(
                    out=mk.BB1[:, 8192 + 2048 * p + 128 * h:
                               8192 + 2048 * p + 128 * (h + 1)],
                    in_=mk.k_sp[128 * gci:128 * (gci + 1),
                                128 * h:128 * (h + 1)],
                    transpose=True).then_inc(S(f"b_f{p}"), 16)

    def tensor(self, eng):
        mk = self.mk
        S = mk.sem
        eng.wait_ge(S("b_initl"), 16 * 4)
        for sl in range(4):
            mm = eng.matmul(mk.ps_f[sl][:, :],
                            mk.SM[0:1, SM_ROW + 16:SM_ROW + 144],
                            mk.BF0[0:1, 512 * sl:512 * (sl + 1)],
                            start=True, stop=True)
        mm.then_inc(S("b_nw2"), 1)
        eng.wait_ge(S("b_init"), 2)
        eng.wait_ge(S("b_wo"), 16 * NCHK)
        for gci in range(self.nck):
            p = gci % 2
            if gci >= 1:
                eng.wait_ge(S("b_aev"), 8 * gci)
            for es in range(4):
                mm = eng.matmul(mk.ps_f[4 + es % 2][:, :],
                                mk.G1R[:, 128 * gci:128 * (gci + 1)],
                                mk.BB2[:, 4096 + 512 * es:4096 + 512 * (es + 1)],
                                start=True, stop=True)
                mm.then_inc(S("b_gate"), 1)
                if es >= 1:
                    eng.wait_ge(S("b_gev"), 4 * gci + es)
            eng.wait_ge(S(f"b_f{p}"), 16 * H * (gci // 2 + 1))
            eng.wait_ge(S(f"b_q{p}"), 16 * (gci // 2 + 1))
            if gci >= 1:
                eng.wait_ge(S("b_og"), 4 * gci)
            for h in range(H):
                mm = eng.matmul(mk.ps_f[h // 4][:, 128 * (h % 4):128 * (h % 4) + 128],
                                mk.BB1[:, 8192 + 2048 * p + 128 * h:
                                       8192 + 2048 * p + 128 * (h + 1)],
                                mk.BB1[:, 12288 + 2048 * p + 128 * h:
                                       12288 + 2048 * p + 128 * (h + 1)],
                                start=(h % 4 == 0), stop=(h % 4 == 3))
            mm.then_inc(S("b_sc"), 1)
            eng.wait_ge(S("b_msk"), gci + 1)
            eng.wait_ge(S("b_qt"), gci + 1)
            eng.wait_ge(S("b_Sbf"), gci + 1)
            eng.wait_ge(S(f"b_v{p}"), 16 * (gci // 2 + 1))
            for h in range(H):
                eng.matmul(mk.ps_f[h // 4][:, 128 * (h % 4):128 * (h % 4) + 128],
                           mk.BB2[:, 2048 + 128 * h:2048 + 128 * (h + 1)],
                           mk.BB1[:, 4096 + 2048 * p + 128 * h:
                                  4096 + 2048 * p + 128 * (h + 1)],
                           start=(h % 4 == 0), stop=False)
                mm = eng.matmul(
                    mk.ps_f[h // 4][:, 128 * (h % 4):128 * (h % 4) + 128],
                    mk.BB2[:, 14336 + 128 * h:14336 + 128 * (h + 1)],
                    mk.BB2[:, 128 * h:128 * (h + 1)],
                    start=False, stop=(h % 4 == 3))
            mm.then_inc(S("b_o"), 1)
            eng.wait_ge(S("b_kt"), gci + 1)
            eng.wait_ge(S("b_gev"), 4 * (gci + 1))
            for half in range(2):
                if half == 1:
                    eng.wait_ge(S("b_sevA"), gci + 1)
                for hh in range(8):
                    h = 8 * half + hh
                    bank = 4 + hh // 4
                    eng.matmul(mk.ps_f[bank][:, 128 * (hh % 4):128 * (hh % 4) + 128],
                               mk.BF2[:, 128 * h:128 * (h + 1)],
                               mk.BF1[:, 128 * h:128 * (h + 1)],
                               start=(hh % 4 == 0), stop=False)
                    mm = eng.matmul(
                        mk.ps_f[bank][:, 128 * (hh % 4):128 * (hh % 4) + 128],
                        mk.BB3[:, 128 * h:128 * (h + 1)],
                        mk.BB1[:, 4096 + 2048 * p + 128 * h:
                               4096 + 2048 * p + 128 * (h + 1)],
                        start=False, stop=(hh % 4 == 3))
                mm.then_inc(S("b_stA" if half == 0 else "b_stB"), 1)
            eng.wait_ge(S("b_on"), gci + 1)
            for half in range(2):
                if half == 0 and gci >= 1:
                    eng.wait_ge(S("trev"), self.trb + 2 * gci)
                if half == 1:
                    eng.wait_ge(S("trev"), self.trb + 2 * gci + 1)
                for hh in range(8):
                    h = 8 * half + hh
                    mm = eng.transpose(
                        mk.ps_b[:, 1024 * half + 128 * hh:
                                1024 * half + 128 * (hh + 1)],
                        mk.BB2[:, 6144 + 128 * h:6144 + 128 * (h + 1)],
                        mk.IDT[:, :])
                mm.then_inc(S("tr"), 1)
            eng.wait_ge(S("b_sevB"), gci + 1)
            eng.wait_ge(S("trev"), self.trb + 2 * (gci + 1))
            for dms in range(4):
                if dms >= 2:
                    eng.wait_ge(S("b_aev"), 8 * gci + 2 * (dms - 1))
                for half in range(2):
                    for ec in range(8 * half, 8 * half + 8):
                        mm = eng.matmul(
                            mk.ps_f[4 + dms % 2][:, :],
                            mk.BB2[:, 12288 + 128 * ec:12288 + 128 * (ec + 1)],
                            mk.BB0[:, 2048 * ec + 512 * dms:
                                   2048 * ec + 512 * (dms + 1)],
                            start=(ec == 0), stop=(ec == NCHK - 1))
                    mm.then_inc(S("b_attn"), 1)

    def vector(self, eng):
        mk = self.mk
        S = mk.sem
        SM = mk.SM
        eng.wait_ge(S("b_initl"), 16 * 4)
        for h in range(H):
            eng.tensor_scalar_mul(mk.BF2[:, 128 * h:128 * (h + 1)],
                                  mk.BF0[:, 2048:2176],
                                  SM[:, SM_CD128 + h:SM_CD128 + h + 1])
        for h in range(H):
            eng.tensor_scalar_mul(mk.BF3[:, 128 * h:128 * (h + 1)],
                                  mk.BF0[:, 4096:4224],
                                  SM[:, SM_LDB + h:SM_LDB + h + 1])
        eng.drain()
        eng.sem_inc(S("b_init"), 1)
        for gci in range(self.nck):
            p = gci % 2
            if gci == 0:
                eng.wait_ge(S("b_init"), 1)
            else:
                eng.wait_ge(S("b_sevB"), gci)
                eng.wait_ge(S("b_o"), gci)
            eng.tensor_copy(mk.BB2[:, 0:2048], mk.BF1[:, :])
            eng.drain()
            eng.sem_inc(S("b_Sbf"), 1)
            eng.wait_ge(S("b_sc"), gci + 1)
            for b in range(4):
                eng.tensor_tensor(
                    mk.BB2[:, 2048 + 512 * b:2048 + 512 * (b + 1)],
                    mk.ps_f[b][:, :],
                    mk.CB[:, 2048 + 512 * b:2048 + 512 * (b + 1)],
                    ALU.mult)
            eng.drain()
            eng.sem_inc(S("b_msk"), 1)
            eng.wait_ge(S(f"b_q{p}"), 16 * (gci // 2 + 1))
            eng.tensor_mul(mk.BB2[:, 14336:16384],
                           mk.BB1[:, 12288 + 2048 * p:12288 + 2048 * p + 2048],
                           mk.CB[:, 0:2048])
            eng.drain()
            eng.sem_inc(S("b_qt"), 1)
            eng.wait_ge(S(f"b_k{p}"), 16 * (gci // 2 + 1))
            if gci >= 1:
                eng.wait_ge(S("b_stB"), gci)
            eng.tensor_tensor(
                mk.BB3[:, 0:2048].rearrange("p (h e) -> p h e", h=H),
                mk.BB1[:, 2048 * p:2048 * p + 2048]
                .rearrange("p (h e) -> p h e", h=H),
                SM[:, SM_KDP:SM_KDP + 16].unsqueeze(2).broadcast_to([128, H, HD]),
                ALU.mult)
            eng.drain()
            eng.sem_inc(S("b_kt"), 1)
            eng.wait_ge(S("b_o"), gci + 1)
            eng.wait_ge(S("b_gev"), 4 * (gci + 1))
            if gci >= 1:
                pp = (gci - 1) % 2
                eng.wait_ge(S(f"sp{pp}"),
                            16 * (self.spb + (gci - 1) // 2 + 1))
            for b in range(4):
                eng.tensor_tensor(mk.BF3[:, 512 * b:512 * (b + 1)],
                                  mk.ps_f[b][:, :],
                                  mk.BB2[:, 8192 + 512 * b:8192 + 512 * (b + 1)],
                                  ALU.mult)
            eng.drain()
            eng.sem_inc(S("b_og"), 4)
            rs = SM_RMS + 16 * p
            eng.wait_ge(S("st_sq"), self.stb + 2 * gci + 1)
            eng.tensor_reduce(SM[:, rs:rs + 16],
                              mk.BF0[:, 4096:6144].rearrange(
                                  "p (h e) -> p h e", h=H),
                              AX.X, ALU.add)
            eng.drain()
            eng.tensor_scalar(SM[:, rs:rs + 16], SM[:, rs:rs + 16],
                              1.0 / HD, EPS, ALU.mult, ALU.add)
            eng.drain()
            eng.sem_inc(S("st_var"), 1)
            eng.wait_ge(S("st_sqr"), self.stb + 2 * gci + 1)
            eng.reciprocal(SM[:, rs:rs + 16], SM[:, rs:rs + 16])
            eng.drain()
            eng.tensor_tensor(
                mk.BF3[:, :].rearrange("p (h e) -> p h e", h=H),
                mk.BF3[:, :].rearrange("p (h e) -> p h e", h=H),
                SM[:, rs:rs + 16].unsqueeze(2).broadcast_to([128, H, HD]),
                ALU.mult)
            eng.drain()
            eng.tensor_mul(mk.BB2[:, 6144:8192], mk.BF3[:, :],
                           mk.BB2[:, 10240:12288])
            eng.drain()
            eng.sem_inc(S("b_on"), 1)
            eng.wait_ge(S("b_aev"), 8 * (gci + 1))
            st = SM_ST + 8 * p
            eng.reduce_sum(SM[:, st:st + 1], mk.BF3[:, :], AX.X)
            eng.wait_ge(S("st_sq"), self.stb + 2 * gci + 2)
            eng.drain()
            eng.tensor_scalar_mul(SM[:, st + 2:st + 3], SM[:, st:st + 1], 1.0 / D)
            eng.tensor_scalar_mul(SM[:, st + 3:st + 4], SM[:, st + 1:st + 2],
                                  1.0 / D)
            eng.drain()
            eng.tensor_mul(SM[:, st + 7:st + 8], SM[:, st + 2:st + 3],
                           SM[:, st + 2:st + 3])
            eng.drain()
            eng.tensor_sub(SM[:, st + 3:st + 4], SM[:, st + 3:st + 4],
                           SM[:, st + 7:st + 8])
            eng.drain()
            eng.tensor_scalar_add(SM[:, st + 3:st + 4], SM[:, st + 3:st + 4], EPS)
            eng.drain()
            eng.sem_inc(S("st_var"), 1)
            eng.wait_ge(S("st_sqr"), self.stb + 2 * gci + 2)
            eng.reciprocal(SM[:, st + 6:st + 7], SM[:, st + 5:st + 6])
            eng.drain()
            eng.tensor_scalar(SM[:, st + 4:st + 5], SM[:, st + 2:st + 3],
                              SM[:, st + 6:st + 7], -1.0, ALU.mult, ALU.mult)
            eng.drain()
            eng.sem_inc(S("b_lstat"), 1)
            eng.wait_ge(S("b_lnorm"), gci + 1)
            xbase = self.xb0 if p == 0 else self.xb1
            eng.wait_ge(S(f"x{p}"), 16 * (xbase + gci // 2 + 1))
            eng.tensor_add(mk.BF3[:, :], mk.BF0[:, 4096:6144],
                           mk.BF0[:, 2048 * p:2048 * p + 2048])
            eng.drain()
            eng.sem_inc(S("b_ln"), 1)

    def scalar(self, eng):
        mk = self.mk
        S = mk.sem
        SM = mk.SM
        i = mk.inp
        eng.dma_start(out=mk.BF0[:, 4096:4224], in_=i["posmat"][:, 0:128]
                      ).then_inc(S("b_initl"), 16)
        eng.dma_start(out=mk.BF0[:, 2048:2176], in_=i["identf"][:, :]
                      ).then_inc(S("b_initl"), 16)
        eng.dma_start(out=mk.BF0[0:1, 0:2048],
                      in_=i["norm_w"].rearrange("(o c) p -> o (c p)", o=1)
                      ).then_inc(S("b_initl"), 16)
        eng.dma_start(out=mk.BB2[:, 4096:6144], in_=mk.wg2d[:, :]
                      ).then_inc(S("b_initl"), 16)
        eng.wait_ge(S("b_init"), 1)
        for h in range(H):
            eng.activation(mk.CB[:, 128 * h:128 * (h + 1)],
                           mk.BF3[:, 128 * h:128 * (h + 1)], AF.Exp)
        eng.drain()
        eng.wait_ge(S("b_nw2"), 1)
        for sl in range(4):
            eng.activation(mk.BB2[:, 10240 + 512 * sl:10240 + 512 * (sl + 1)],
                           mk.ps_f[sl][:, :], AF.Identity)
        eng.drain()
        eng.sem_inc(S("b_init"), 1)
        for gci in range(self.nck):
            p = gci % 2
            for es in range(4):
                eng.wait_ge(S("b_gate"), 4 * gci + es + 1)
                eng.activation(mk.BB2[:, 8192 + 512 * es:8192 + 512 * (es + 1)],
                               mk.ps_f[4 + es % 2][:, :], AF.Sigmoid)
                eng.drain()
                eng.sem_inc(S("b_gev"), 1)
            eng.wait_ge(S("b_stA"), gci + 1)
            eng.activation(mk.BF1[:, 0:512], mk.ps_f[4][:, :], AF.Identity)
            eng.activation(mk.BF1[:, 512:1024], mk.ps_f[5][:, :], AF.Identity)
            eng.drain()
            eng.sem_inc(S("b_sevA"), 1)
            eng.wait_ge(S("b_stB"), gci + 1)
            eng.activation(mk.BF1[:, 1024:1536], mk.ps_f[4][:, :], AF.Identity)
            eng.activation(mk.BF1[:, 1536:2048], mk.ps_f[5][:, :], AF.Identity)
            eng.drain()
            eng.sem_inc(S("b_sevB"), 1)
            eng.wait_ge(S("b_og"), 4 * (gci + 1))
            eng.activation(mk.BF0[:, 4096:6144], mk.BF3[:, :], AF.Square)
            eng.drain()
            eng.sem_inc(S("st_sq"), 1)
            rs = SM_RMS + 16 * p
            eng.wait_ge(S("st_var"), self.stb + 2 * gci + 1)
            eng.activation(SM[:, rs:rs + 16], SM[:, rs:rs + 16], AF.Sqrt)
            eng.drain()
            eng.sem_inc(S("st_sqr"), 1)
            for half in range(2):
                eng.wait_ge(S("tr"), self.trb + 2 * gci + half + 1)
                for hh in range(8):
                    h = 8 * half + hh
                    eng.tensor_copy(
                        mk.BB2[:, 12288 + 128 * h:12288 + 128 * (h + 1)],
                        mk.ps_b[:, 1024 * half + 128 * hh:
                                1024 * half + 128 * (hh + 1)])                         if False else                     eng.activation(
                        mk.BB2[:, 12288 + 128 * h:12288 + 128 * (h + 1)],
                        mk.ps_b[:, 1024 * half + 128 * hh:
                                1024 * half + 128 * (hh + 1)], AF.Identity)
                eng.drain()
                eng.sem_inc(S("trev"), 1)
            for dms in range(4):
                eng.wait_ge(S("b_attn"), 8 * gci + 2 * dms + 2)
                eng.activation(mk.BF3[:, 512 * dms:512 * (dms + 1)],
                               mk.ps_f[4 + dms % 2][:, :], AF.Identity)
                eng.drain()
                eng.sem_inc(S("b_aev"), 2)
            st = SM_ST + 8 * p
            eng.wait_ge(S("b_aev"), 8 * (gci + 1))
            eng.activation(mk.BF0[:, 4096:6144], mk.BF3[:, :], AF.Square,
                           accum_out=SM[:, st + 1:st + 2])
            eng.drain()
            eng.sem_inc(S("st_sq"), 1)
            eng.wait_ge(S("st_var"), self.stb + 2 * gci + 2)
            eng.activation(SM[:, st + 5:st + 6], SM[:, st + 3:st + 4], AF.Sqrt)
            eng.drain()
            eng.sem_inc(S("st_sqr"), 1)
            eng.wait_ge(S("b_lstat"), gci + 1)
            eng.activation(mk.BF0[:, 4096:6144], mk.BF3[:, :], AF.Identity,
                           bias=SM[:, st + 4:st + 5], scale=SM[:, st + 6:st + 7])
            eng.drain()
            eng.sem_inc(S("b_lnorm"), 1)
            eng.wait_ge(S("b_ln"), gci + 1)
            if gci >= 2:
                eng.wait_ge(S(f"sp{p}"), 16 * (self.spb + gci // 2))
            eng.dma_start(out=mk.xmid[128 * gci:128 * (gci + 1), :],
                          in_=mk.BF3[:, :]).then_inc(S(f"sp{p}"), 16)
        eng.wait_ge(S("sp0"), 16 * (self.spb + (self.nck + 1) // 2))
        eng.wait_ge(S("sp1"), 16 * (self.spb + self.nck // 2))

    def gpsimd(self, eng):
        mk = self.mk
        S = mk.sem
        if "deq" in mk.phases:
            eng.wait_ge(S("agw"), 3)
            for (gr, lin) in ((mk.w1gr, mk.w1g), (mk.w2gr, mk.w2g)):
                for r in range(mk.n_cores):
                    eng.dma_start(out=lin[:, 704 * r:704 * (r + 1)],
                                  in_=gr[r, :, :]).then_inc(S("c_wready"), 16)


class PhC(Phase):

    def plan(self):
        mk = self.mk
        for n in ["c_ln", "c_ab", "c_aev", "c_bev", "c_h", "c_w3m", "c_oev",
                  "c_res", "c_w120", "c_w121",
                  "c_w30", "c_w31", "c_w32", "c_w33"]:
            mk.sem(n)
        n0, n1 = (self.mk.nchunks + 1) // 2, self.mk.nchunks // 2
        self.fb = (256 * n0, 256 * n1)
        self.nck = mk.nchunks
        self.ng = mk.ngc
        self.cpc = mk.cpc
        n0, n1 = mk.xb
        self.xb = (2 * n0, 2 * n1)
        self.stb = 3 * self.nck
        self.stb_stat = self.nck
        self.trb = 6 * self.nck
        self.spb = (mk.nsg * ((NCHK * (mk.sg_tok // 512)) // 2)
                    + (self.nck + 1) // 2,
                    mk.nsg * ((NCHK * (mk.sg_tok // 512)) // 2)
                    + self.nck // 2)

    def xcnt(self, gci):
        return self.xb[gci % 2] + gci // 2 + 1

    @property
    def cxb(self):
        return 4 if "B" in self.mk.phases else 0

    @property
    def bf3b(self):
        return (FCHK + 2 * NCHK) if "deq" in self.mk.phases else 0

    def ln3_ev(self, gci):
        h, ci = divmod(gci, 4)
        return (ci + 1) if h == 0 else (8 * h - 3 + ci)

    def ln4_ev(self, g, tc):
        if g < self.ng - 1:
            return 8 * g + 9 + tc
        return 8 * g + 5 + tc

    def w3_tile_dma(self, eng, g, il):
        mk = self.mk
        S = mk.sem
        i = g * 4 * FCHK + il
        if i >= 4:
            eng.wait_ge(S("c_w3m"), i - 3)
        dms, fc = divmod(il, FCHK)
        eng.dma_start(
            out=mk.BB3[:, 512 * (i % 4):512 * (i % 4 + 1)]
            .rearrange("p (r i) -> p r i", r=2),
            in_=mk.w3g[2 * dms:2 * dms + 2, 128 * fc:128 * (fc + 1), :]
            .rearrange("r f i -> f r i")
            ).then_inc(S(f"c_w3{i % 4}"), 16)

    def lnslot(self, gci):
        return self.mk.BB1[:, 8192 + 2048 * (gci % 4):
                           8192 + 2048 * (gci % 4) + 2048]

    def row(self, tc):
        mk = self.mk
        return (mk.BF0[:, 0:2048] if tc == 0 else
                mk.BF0[:, 2048:4096] if tc == 1 else
                mk.BF1[:, :] if tc == 2 else mk.BF2[:, :])

    def slab_dma(self, eng, g, fcp, gated):
        mk = self.mk
        S = mk.sem
        pw = fcp % 2
        if gated:
            eng.wait_ge(S("c_ab"), g * 2 * FCHK + 4 * (fcp - 1))
        for (wt, off) in ((mk.w1g, 0), (mk.w2g, 8192)):
            eng.dma_start(
                out=mk.BB2[:, off + 4096 * pw:off + 4096 * pw + 4096]
                .rearrange("p (c f) -> p c f", c=NCHK),
                in_=wt.rearrange("(c p) f -> p c f", p=128)
                [:, :, 256 * fcp:256 * (fcp + 1)]
                ).then_inc(S(f"c_w12{pw}"), 16)

    def sync(self, eng):
        mk = self.mk
        S = mk.sem
        if "deq" in mk.phases:
            eng.wait_ge(S("c_wready"), 16 * 16)
        for fcp in range(2):
            self.slab_dma(eng, 0, fcp, gated=False)
        for g in range(self.ng):
            for il in (0, 2):
                self.w3_tile_dma(eng, g, il)
            for fcp in range(2, FCHK // 2):
                self.slab_dma(eng, g, fcp, gated=True)
            if g + 1 < self.ng:
                for fcp in range(2):
                    eng.wait_ge(S("c_ab"),
                                g * 2 * FCHK + 4 * (FCHK // 2 - 1 + fcp))
                    self.slab_dma(eng, g + 1, fcp, gated=False)
            for il in range(4, 4 * FCHK, 2):
                self.w3_tile_dma(eng, g, il)
            for tc in range(self.cpc):
                gci = g * self.cpc + tc
                p = gci % 2
                eng.wait_ge(S("c_res"), gci + 1)
                if gci >= 2:
                    eng.wait_ge(S(f"sp{p}"), 16 * (self.spb[p] + gci // 2))
                eng.dma_start(out=mk.outp[128 * gci:128 * (gci + 1), :],
                              in_=self.row(tc)).then_inc(S(f"sp{p}"), 16)

    def t_trans(self, eng, gci):
        mk = self.mk
        S = mk.sem
        eng.wait_ge(S("c_ln"), gci + 1)
        lns = self.lnslot(gci)
        for s in range(4):
            b = (0, 2, 1, 3)[s]
            if gci * 4 + s >= 4:
                eng.wait_ge(S("trev"), self.trb + gci * 4 + s - 3)
            for k in range(4):
                dc = b * 4 + k
                mm = eng.transpose(
                    mk.ps_b[:, 512 * b + 128 * k:512 * b + 128 * k + 128],
                    lns[:, 128 * dc:128 * (dc + 1)],
                    mk.IDT[:, :])
            mm.then_inc(S("tr"), 1)

    def tensor(self, eng):
        mk = self.mk
        S = mk.sem
        for g in range(self.ng):
            if g == 0:
                for ci in range(self.cpc):
                    self.t_trans(eng, ci)
            eng.wait_ge(S("trev"), self.trb + (g + 1) * self.cpc * 4)
            for fc in range(FCHK):
                fcp = fc // 2
                pw = fcp % 2
                f2 = fc % 2
                ba, bb = (4, 5) if fc % 2 == 0 else (0, 1)
                if f2 == 0:
                    eng.wait_ge(S(f"c_w12{pw}"),
                                32 * (g * (FCHK // 2) // 2 + fcp // 2 + 1))
                if fc == 0:
                    if g >= 1:
                        eng.wait_ge(S("c_aev"), g * FCHK - 1)
                        eng.wait_ge(S("c_bev"), g * FCHK - 1)
                elif fc == 1:
                    if g >= 1:
                        eng.wait_ge(S("c_oev"), 16 * g)
                        eng.wait_ge(S("c_aev"), g * FCHK)
                        eng.wait_ge(S("c_bev"), g * FCHK)
                else:
                    eng.wait_ge(S("c_aev"), g * FCHK + fc - 1)
                    eng.wait_ge(S("c_bev"), g * FCHK + fc - 1)
                for dc in range(NCHK):
                    eng.matmul(mk.ps_f[ba][:, :],
                               mk.BB2[:, 4096 * pw + 256 * dc + 128 * f2:
                                      4096 * pw + 256 * dc + 128 * (f2 + 1)],
                               mk.BB1[:, 512 * dc:512 * (dc + 1)],
                               start=(dc == 0), stop=(dc == NCHK - 1))
                for dc in range(NCHK):
                    mm = eng.matmul(mk.ps_f[bb][:, :],
                                    mk.BB2[:, 8192 + 4096 * pw + 256 * dc + 128 * f2:
                                           8192 + 4096 * pw + 256 * dc + 128 * (f2 + 1)],
                                    mk.BB1[:, 512 * dc:512 * (dc + 1)],
                                    start=(dc == 0), stop=(dc == NCHK - 1))
                mm.then_inc(S("c_ab"), 2)
            eng.wait_ge(S("c_h"), (g + 1) * FCHK)
            for dms in range(4):
                if g * 4 + dms >= 1:
                    eng.wait_ge(S("c_oev"), 4 * (g * 4 + dms))
                for fc in range(FCHK):
                    i = (g * 4 + dms) * FCHK + fc
                    eng.wait_ge(S(f"c_w3{i % 4}"), 16 * (i // 4 + 1))
                    for tc in range(4):
                        mm = eng.matmul(
                            mk.ps_f[tc][:, :],
                            mk.BB0[:, 512 * fc + 128 * tc:
                                   512 * fc + 128 * (tc + 1)],
                            mk.BB3[:, 512 * (i % 4):512 * (i % 4) + 512],
                            start=(fc == 0), stop=(fc == FCHK - 1))
                    mm.then_inc(S("c_w3m"), 1)
                if g < self.ng - 1:
                    self.t_trans(eng, (g + 1) * self.cpc + dms)

    def xslot(self, gci):
        return self.mk.XB[:, 2048 * (gci % 2):2048 * (gci % 2) + 2048]

    def v_ln3(self, eng, gci):
        mk = self.mk
        S = mk.sem
        SM = mk.SM
        p = gci % 2
        st = SM_ST + 8 * p
        xs = self.xslot(gci)
        ev = self.ln3_ev(gci)
        eng.wait_ge(S("b_initl"), 16 * (gci + 1 + self.cxb))
        if gci >= 2:
            eng.wait_ge(S("c_ln"), gci - 1)
        eng.reduce_sum(SM[:, st:st + 1], xs, AX.X)
        eng.wait_ge(S("st_sq"), self.stb + ev)
        eng.drain()
        self.v_stats(eng, st, ev)

    def v_stats(self, eng, st, ev):
        mk = self.mk
        S = mk.sem
        SM = mk.SM
        eng.tensor_scalar_mul(SM[:, st + 2:st + 3], SM[:, st:st + 1],
                              1.0 / D)
        eng.tensor_scalar_mul(SM[:, st + 3:st + 4],
                              SM[:, st + 1:st + 2], 1.0 / D)
        eng.drain()
        eng.tensor_mul(SM[:, st + 7:st + 8], SM[:, st + 2:st + 3],
                       SM[:, st + 2:st + 3])
        eng.drain()
        eng.tensor_sub(SM[:, st + 3:st + 4], SM[:, st + 3:st + 4],
                       SM[:, st + 7:st + 8])
        eng.drain()
        eng.tensor_scalar_add(SM[:, st + 3:st + 4],
                              SM[:, st + 3:st + 4], EPS)
        eng.drain()
        eng.sem_inc(S("st_var"), 1)
        eng.wait_ge(S("st_sqr"), self.stb + ev)
        eng.reciprocal(SM[:, st + 6:st + 7], SM[:, st + 5:st + 6])
        eng.drain()
        eng.tensor_scalar(SM[:, st + 4:st + 5], SM[:, st + 2:st + 3],
                          SM[:, st + 6:st + 7], -1.0, ALU.mult, ALU.mult)
        eng.drain()
        eng.sem_inc(S("st_stat"), 1)

    def v_trev(self, eng, gci):
        mk = self.mk
        S = mk.sem
        ci = gci % 4
        for s in range(4):
            b = (0, 2, 1, 3)[s]
            eng.wait_ge(S("tr"), self.trb + gci * 4 + s + 1)
            for k in range(4):
                dc = b * 4 + k
                eng.tensor_copy(
                    mk.BB1[:, 512 * dc + 128 * ci:
                           512 * dc + 128 * (ci + 1)],
                    mk.ps_b[:, 512 * b + 128 * k:512 * b + 128 * k + 128])
            eng.drain()
            eng.sem_inc(S("trev"), 1)

    def vector(self, eng):
        mk = self.mk
        S = mk.sem
        SM = mk.SM
        for ci in range(self.cpc):
            self.v_ln3(eng, ci)
        for ci in range(self.cpc):
            self.v_trev(eng, ci)
        for g in range(self.ng):
            for fc in range(FCHK):
                eng.wait_ge(S("c_ab"), g * 2 * FCHK + 2 * (fc + 1))
                eng.tensor_copy(mk.CB[:, 512:1024],
                                mk.ps_f[5 if fc % 2 == 0 else 1][:, :])
                eng.drain()
                eng.sem_inc(S("c_bev"), 1)
                eng.wait_ge(S("c_aev"), g * FCHK + fc + 1)
                eng.tensor_mul(mk.BB0[:, 512 * fc:512 * (fc + 1)],
                               mk.CB[:, 0:512], mk.CB[:, 512:1024])
                eng.drain()
                eng.sem_inc(S("c_h"), 1)
            if g < self.ng - 1:
                for ci in range(self.cpc):
                    self.v_ln3(eng, (g + 1) * self.cpc + ci)
            for dms in range(4):
                eng.wait_ge(S("c_w3m"), (g * 4 + dms + 1) * FCHK)
                if dms == 0 and g >= 1:
                    eng.wait_ge(S("sp0"), 16 * (self.spb[0] + 2 * g))
                    eng.wait_ge(S("sp1"), 16 * (self.spb[1] + 2 * g))
                for tc in range(4):
                    eng.tensor_copy(self.row(tc)[:, 512 * dms:512 * (dms + 1)],
                                    mk.ps_f[tc][:, :])
                eng.drain()
                eng.sem_inc(S("c_oev"), 4)
                if g < self.ng - 1 and dms >= 1:
                    self.v_trev(eng, (g + 1) * self.cpc + dms - 1)
            if g < self.ng - 1:
                self.v_trev(eng, (g + 1) * self.cpc + 3)
            for tc in range(self.cpc):
                gci = g * self.cpc + tc
                p = tc % 2
                st = SM_ST + 8 * p
                ev = self.ln4_ev(g, tc)
                row = self.row(tc)
                eng.reduce_sum(SM[:, st:st + 1], row, AX.X)
                eng.wait_ge(S("st_sq"), self.stb + ev)
                eng.drain()
                self.v_stats(eng, st, ev)
                eng.wait_ge(S("b_lnorm"), self.nck + gci + 1)
                eng.wait_ge(S("d_out"), 16 * (self.bf3b + gci + 1))
                eng.tensor_add(row, mk.BF0[:, 4096:6144], mk.BF3[:, :])
                eng.drain()
                eng.sem_inc(S("c_res"), 1)

    def s_ln3(self, eng, gci):
        mk = self.mk
        S = mk.sem
        SM = mk.SM
        p = gci % 2
        st = SM_ST + 8 * p
        xs = self.xslot(gci)
        ev = self.ln3_ev(gci)
        eng.wait_ge(S("b_initl"), 16 * (gci + 1 + self.cxb))
        eng.activation(mk.BF0[:, 4096:6144], xs, AF.Square,
                       accum_out=SM[:, st + 1:st + 2])
        eng.drain()
        eng.sem_inc(S("st_sq"), 1)
        eng.wait_ge(S("st_var"), self.stb + ev)
        eng.activation(SM[:, st + 5:st + 6], SM[:, st + 3:st + 4],
                       AF.Sqrt)
        eng.drain()
        eng.sem_inc(S("st_sqr"), 1)
        eng.wait_ge(S("st_stat"), self.stb_stat + ev)
        if gci >= 4:
            eng.wait_ge(S("tr"), self.trb + 4 * (gci - 3))
        eng.activation(self.lnslot(gci), xs, AF.Identity,
                       bias=SM[:, st + 4:st + 5],
                       scale=SM[:, st + 6:st + 7])
        eng.drain()
        eng.sem_inc(S("c_ln"), 1)
        if gci + 2 < self.nck:
            self.xload(eng, gci + 2)

    def xload(self, eng, gci):
        mk = self.mk
        S = mk.sem
        if gci >= 1:
            eng.wait_ge(S("b_initl"), 16 * (gci + self.cxb))
        if gci >= 2:
            eng.wait_ge(S("c_ln"), gci - 1)
        eng.dma_start(out=self.xslot(gci),
                      in_=mk.xmid[128 * gci:128 * (gci + 1), :]
                      ).then_inc(S("b_initl"), 16)

    def scalar(self, eng):
        mk = self.mk
        S = mk.sem
        SM = mk.SM
        for gci in range(2):
            self.xload(eng, gci)
        for ci in range(self.cpc):
            self.s_ln3(eng, ci)
        for g in range(self.ng):
            for fc in range(FCHK):
                eng.wait_ge(S("c_ab"), g * 2 * FCHK + 2 * fc + 2)
                if fc >= 1:
                    eng.wait_ge(S("c_h"), g * FCHK + fc)
                eng.activation(mk.CB[:, 0:512],
                               mk.ps_f[4 if fc % 2 == 0 else 0][:, :], AF.Silu)
                eng.drain()
                eng.sem_inc(S("c_aev"), 1)
            if g < self.ng - 1:
                for ci in range(self.cpc):
                    self.s_ln3(eng, (g + 1) * self.cpc + ci)
            for tc in range(self.cpc):
                gci = g * self.cpc + tc
                p = tc % 2
                st = SM_ST + 8 * p
                ev = self.ln4_ev(g, tc)
                row = self.row(tc)
                eng.wait_ge(S("c_oev"), 16 * (g + 1))
                eng.wait_ge(S("c_res"), gci)
                eng.dma_start(out=mk.BF3[:, :],
                              in_=mk.xmid[128 * gci:128 * (gci + 1), :]
                              ).then_inc(S("d_out"), 16)
                eng.activation(mk.BF0[:, 4096:6144], row, AF.Square,
                               accum_out=SM[:, st + 1:st + 2])
                eng.drain()
                eng.sem_inc(S("st_sq"), 1)
                eng.wait_ge(S("st_var"), self.stb + ev)
                eng.activation(SM[:, st + 5:st + 6], SM[:, st + 3:st + 4],
                               AF.Sqrt)
                eng.drain()
                eng.sem_inc(S("st_sqr"), 1)
                eng.wait_ge(S("st_stat"), self.stb_stat + ev)
                eng.activation(mk.BF0[:, 4096:6144], row, AF.Identity,
                               bias=SM[:, st + 4:st + 5],
                               scale=SM[:, st + 6:st + 7])
                eng.drain()
                eng.sem_inc(S("b_lnorm"), 1)
        eng.wait_ge(S("sp0"), 16 * (self.spb[0] + (self.nck + 1) // 2))
        eng.wait_ge(S("sp1"), 16 * (self.spb[1] + self.nck // 2))

    def gpsimd(self, eng):
        mk = self.mk
        S = mk.sem
        for g in range(self.ng):
            for il in range(1, 4 * FCHK, 2):
                self.w3_tile_dma(eng, g, il)
        eng.wait_ge(S("sp0"), 16 * (self.spb[0] + (self.nck + 1) // 2))
        eng.wait_ge(S("sp1"), 16 * (self.spb[1] + self.nck // 2))


from concourse.bass_utils import run_bass_kernel_spmd


def kernel(L0_attn_x, L0_attn_log_decay, L0_attn_Wq, L0_attn_Wk, L0_attn_Wv,
           L0_attn_Wo, L0_attn_Wg1, L0_attn_Wg2, L0_attn_norm_w,
           L0_ffn_weight1, L0_ffn_weight2, L0_ffn_weight3):
    NCORES = 8
    TPC = 4096
    x = np.ascontiguousarray(
        np.asarray(L0_attn_x, dtype=np.float32).reshape(NCORES * TPC, D))
    ld = np.ascontiguousarray(
        np.asarray(L0_attn_log_decay, dtype=np.float32).reshape(16, 1))
    Wq = np.ascontiguousarray(np.asarray(L0_attn_Wq, dtype=np.float32))
    Wk = np.ascontiguousarray(np.asarray(L0_attn_Wk, dtype=np.float32))
    Wv = np.ascontiguousarray(np.asarray(L0_attn_Wv, dtype=np.float32))
    Wo = np.ascontiguousarray(np.asarray(L0_attn_Wo, dtype=np.float32))
    Wg1 = np.ascontiguousarray(np.asarray(L0_attn_Wg1, dtype=np.float32))
    Wg2 = np.ascontiguousarray(np.asarray(L0_attn_Wg2, dtype=np.float32))
    nw = np.ascontiguousarray(
        np.asarray(L0_attn_norm_w, dtype=np.float32).reshape(NCHK, 128))
    w1 = np.asarray(L0_ffn_weight1).view(ml_dtypes.bfloat16) \
        if np.asarray(L0_ffn_weight1).dtype != ml_dtypes.bfloat16 \
        else np.asarray(L0_ffn_weight1)
    w2 = np.asarray(L0_ffn_weight2)
    w3 = np.asarray(L0_ffn_weight3)
    w1 = np.ascontiguousarray(np.asarray(w1).reshape(DFF, D))
    w2 = np.ascontiguousarray(np.asarray(w2).reshape(DFF, D))
    w3 = np.ascontiguousarray(np.asarray(w3).reshape(D, DFF))
    w1t = np.ascontiguousarray(w1.T)
    w2t = np.ascontiguousarray(w2.T)
    w3t = np.ascontiguousarray(w3.T)

    mk = MK(tpc=TPC, n_cores=NCORES)
    nc = mk.build()

    in_maps = []
    for c in range(NCORES):
        m = {
            "x": np.ascontiguousarray(x[TPC * c:TPC * (c + 1)]),
            "log_decay": ld,
            "Wq": Wq, "Wk": Wk, "Wv": Wv, "Wo": Wo,
            "Wg1": Wg1, "Wg2": Wg2, "norm_w": nw,
            "w1s": np.ascontiguousarray(w1[704 * c:704 * (c + 1)]),
            "w2s": np.ascontiguousarray(w2[704 * c:704 * (c + 1)]),
            "w3s": np.ascontiguousarray(w3[256 * c:256 * (c + 1)]),
            "w1ts": np.ascontiguousarray(w1t[:, 704 * c:704 * (c + 1)]),
            "w2ts": np.ascontiguousarray(w2t[:, 704 * c:704 * (c + 1)]),
            "w3ts": np.ascontiguousarray(w3t[:, 256 * c:256 * (c + 1)]),
        }
        m.update(host_consts(c))
        in_maps.append(m)

    res = run_bass_kernel_spmd(nc, in_maps, core_ids=list(range(NCORES)))
    out = np.concatenate([np.asarray(res.results[c]["out"], dtype=np.float32)
                          .reshape(TPC, D) for c in range(NCORES)], axis=0)
    return out.reshape(4, 8192, D)


# version 12
# speedup vs baseline: 1.1457x; 1.0071x over previous
import numpy as np
import ml_dtypes
from contextlib import ExitStack

import concourse.bass as bass
import concourse.mybir as mybir

F32 = mybir.dt.float32
BF16 = mybir.dt.bfloat16
I32 = mybir.dt.int32
AF = mybir.ActivationFunctionType
ALU = mybir.AluOpType
AX = mybir.AxisListType

D = 2048
H = 16
HD = 128
DFF = 5632
NCHK = 16
FCHK = 44
EPS = 1e-5

NF4 = np.array([-1.0, -0.6962, -0.5251, -0.3949, -0.2844, -0.1848, -0.0911,
                0.0, 0.0796, 0.1609, 0.2461, 0.3379, 0.4407, 0.5626, 0.723,
                1.0], dtype=ml_dtypes.bfloat16).astype(np.float32)
NF4_MID = [float(x) for x in ((NF4[:-1] + NF4[1:]) / 2)]
NF4_DELTA = [float(x) for x in (NF4[1:] - NF4[:-1])]


def host_consts(core_id):
    c = {}
    c["ident"] = np.eye(128, dtype=np.float32).astype(ml_dtypes.bfloat16)
    c["identf"] = np.eye(128, dtype=np.float32)
    c["ones1"] = np.ones((1, 128), dtype=np.float32)
    c["onesc"] = np.ones((128, 1), dtype=np.float32)
    c["pos_127"] = (127 - np.arange(128, dtype=np.float32)).reshape(128, 1)
    i = np.arange(128)
    dT = (i[None, :] - i[:, None]).astype(np.float32)
    dT[dT < 0] = 1e9
    c["diffT"] = dT
    ho = np.zeros((128, H, H), dtype=np.float32)
    for h in range(H):
        ho[:, h, h] = 1.0
    c["headones"] = ho.reshape(128, H * H)
    eb = np.zeros((2, 128), dtype=np.float32)
    eb[0, 0:64] = 1.0
    eb[1, 64:128] = 1.0
    c["ebase2"] = eb
    c["posmat"] = np.tile(np.arange(1, 129, dtype=np.float32), (128, 4))
    m8 = np.zeros((1, 8), dtype=np.float32)
    if core_id % 2 == 1:
        m8[0, core_id - 1] = 1.0
    c["mask8"] = m8
    return c


SM_LDB = 0
SM_CD128 = 16
SM_CDSUP = 32
SM_KDP = 48
SM_KDCD = 64
SM_NW = 192
SM_HO = 208
SM_M8B = 464
SM_ST = 472
SM_ARG = 492
SM_ROW = 640
SM_MEAN = 920
SM_P127 = 960
SM_ONEC = 961
SM_RMS = 962
SM_KARG = 476


class MK:
    def __init__(self, tpc=4096, n_cores=8, phases=None):
        self.tpc = tpc
        self.n_cores = n_cores
        self.sg_tok = min(1024, tpc)
        self.cps = self.sg_tok // 128
        self.nsg = tpc // self.sg_tok
        self.gb_tok = min(512, tpc)
        self.ngb = tpc // self.gb_tok
        self.cpb = self.gb_tok // 128
        self.gc_tok = min(512, tpc)
        self.ngc = tpc // self.gc_tok
        self.cpc = self.gc_tok // 128
        self.nchunks = tpc // 128
        self.phases = phases or ["setup", "deq", "A", "AGstate", "B", "C"]
        self.nc = bass.Bass("TRN2", target_bir_lowering=False)
        self.stack = ExitStack()
        self.declare()

    def declare(self):
        nc = self.nc
        t = self.tpc
        self.inp = {}
        for name, shape, dt in [
            ("x", [t, D], F32),
            ("log_decay", [16, 1], F32),
            ("Wq", [D, D], F32), ("Wk", [D, D], F32), ("Wv", [D, D], F32),
            ("Wo", [D, D], F32), ("Wg1", [D, HD], F32), ("Wg2", [HD, D], F32),
            ("norm_w", [NCHK, 128], F32),
            ("w1s", [DFF // 8, D], BF16), ("w2s", [DFF // 8, D], BF16),
            ("w3s", [D // 8, DFF], BF16),
            ("w1ts", [D, DFF // 8], BF16), ("w2ts", [D, DFF // 8], BF16),
            ("w3ts", [DFF, D // 8], BF16),
        ]:
            self.inp[name] = nc.dram_tensor(name, shape, dt, kind="ExternalInput").ap()
        for name, arr in host_consts(0).items():
            dtv = BF16 if arr.dtype == ml_dtypes.bfloat16 else F32
            self.inp[name] = nc.dram_tensor(
                name, list(arr.shape), dtv, kind="ExternalInput").ap()
        self.outp = nc.dram_tensor("out", [t, D], F32, kind="ExternalOutput").ap()

        dr = lambda n, s, d: nc.dram_tensor(n, s, d, kind="Internal").ap()
        self.wq_bf = dr("wq_bf", [NCHK, 128, D], BF16)
        self.wk_bf = dr("wk_bf", [NCHK, 128, D], BF16)
        self.wv_bf = dr("wv_bf", [NCHK, 128, D], BF16)
        self.wo_bf = dr("wo_bf", [NCHK, 128, D], BF16)
        self.wg2d = dr("wg2d", [HD, D], BF16)
        self.q_sp = dr("q_sp", [NCHK, 128, t], BF16)
        self.k_sp = dr("k_sp", [t, D], BF16)
        self.v_sp = dr("v_sp", [t, D], BF16)
        self.xmid = dr("xmid", [t, D], F32)
        self.ag_a_in = dr("ag_a_in", [128, D], F32)
        self.ag_a_out = dr("ag_a_out", [self.n_cores, 128, D], F32)
        self.mean_ag_in = dr("mean_ag_in", [1, 4], F32)
        self.mean_ag_out = dr("mean_ag_out", [self.n_cores, 1, 4], F32)
        self.dum_in = dr("dum_in", [1, 4], F32)
        self.dum_out = dr("dum_out", [self.n_cores, 1, 4], F32)
        self.sflat = dr("sflat", [2, 22528], F32)
        self.w1sh = dr("w1sh", [D, DFF // 8], BF16)
        self.w2sh = dr("w2sh", [D, DFF // 8], BF16)
        self.w3sh = dr("w3sh", [DFF, D // 8], BF16)
        self.w1gr = dr("w1gr", [8, D, DFF // 8], BF16)
        self.w2gr = dr("w2gr", [8, D, DFF // 8], BF16)
        self.w1g = dr("w1g", [D, DFF], BF16)
        self.w2g = dr("w2g", [D, DFF], BF16)
        self.w3g = dr("w3g", [8, DFF, D // 8], BF16)

        sb = lambda n, s, d: self.stack.enter_context(nc.sbuf_tensor(n, s, d))
        self.BB0 = sb("BB0", [128, 32768], BF16)
        self.BB1 = sb("BB1", [128, 16384], BF16)
        self.BB2 = sb("BB2", [128, 16384], BF16)
        self.BB3 = sb("BB3", [128, 4096], BF16)
        self.G1R = sb("G1R", [128, 4096], BF16)
        self.CB = sb("CB", [128, 4096], BF16)
        self.IDT = sb("IDT", [128, 128], BF16)
        self.BF0 = sb("BF0", [128, 6144], F32)
        self.BF1 = sb("BF1", [128, 2048], F32)
        self.BF2 = sb("BF2", [128, 2048], F32)
        self.BF3 = sb("BF3", [128, 2048], F32)
        self.SM = sb("SM", [128, 1024], F32)
        self.ITMP = sb("ITMP", [128, 256], I32)
        self.XB = nc.virtual_sbuf_tensor("XB", [128, 4096], F32)
        nc._tensor(self.XB.name, self.XB.shape, self.XB.dtype, type="SB")
        mx = nc.lookup_mloc(self.XB)
        mx.allocated = True
        mx.addr = nc.lookup_mloc(self.BB0).addr + 49152
        self.ps_f = [self.stack.enter_context(
            nc.psum_tensor(f"psf{i}", [128, 512], F32)) for i in range(6)]
        self.ps_b = self.stack.enter_context(
            nc.psum_tensor("psb", [128, 2048], BF16))
        self.bar = self.stack.enter_context(nc.semaphore("bar"))
        self._sems = {}

    def sem(self, name):
        if name not in self._sems:
            self._sems[name] = self.stack.enter_context(self.nc.semaphore(name))
        return self._sems[name]

    @property
    def xb(self):
        return ((self.nchunks + 1) // 2, self.nchunks // 2)

    def build(self):
        nc = self.nc
        reg = {"setup": PhSetup, "deq": PhDeq, "A": PhA, "AGstate": PhAGState,
               "B": PhB, "C": PhC}
        self.ph_objs = [reg[p](self) for p in self.phases]
        groups = []
        i = 0
        while i < len(self.phases):
            if (self.phases[i] == "setup" and i + 1 < len(self.phases)
                    and self.phases[i + 1] == "deq"):
                groups.append([i, i + 1])
                i += 2
            else:
                groups.append([i])
                i += 1

        def drive(engname):
            def cb(eng):
                for gi, idxs in enumerate(groups):
                    eng.wait_ge(self.bar, 5 * gi)
                    for ix in idxs:
                        getattr(self.ph_objs[ix], engname)(eng)
                    eng.sem_inc(self.bar, 1)
            return cb

        with nc.Block() as block:
            block.sync(drive("sync"))
            block.tensor(drive("tensor"))
            block.vector(drive("vector"))
            block.scalar(drive("scalar"))
            block.gpsimd(drive("gpsimd"))
        return nc


class Phase:
    def __init__(self, mk: MK):
        self.mk = mk
        self.nc = mk.nc
        self.plan()

    def plan(self): pass
    def sync(self, eng): pass
    def tensor(self, eng): pass
    def vector(self, eng): pass
    def scalar(self, eng): pass
    def gpsimd(self, eng): pass


class PhSetup(Phase):

    def plan(self):
        mk = self.mk
        self.sl = mk.sem("su_l")
        self.sg_ = mk.sem("su_g")
        self.sc_ = mk.sem("su_c")

    def scalar(self, eng):
        mk, i = self.mk, self.mk.inp
        SM = mk.SM
        eng.dma_start(out=SM[0:1, SM_ROW:SM_ROW + 16],
                      in_=i["log_decay"].rearrange("h o -> o h")).then_inc(self.sl, 16)
        eng.dma_start(out=SM[0:1, SM_ROW + 16:SM_ROW + 144],
                      in_=i["ones1"][:, :]).then_inc(self.sl, 16)
        eng.dma_start(out=SM[0:2, SM_ROW + 144:SM_ROW + 272],
                      in_=i["ebase2"][:, :]).then_inc(self.sl, 16)
        eng.dma_start(out=SM[0:1, SM_ROW + 272:SM_ROW + 280],
                      in_=i["mask8"][:, :]).then_inc(self.sl, 16)
        eng.dma_start(out=SM[:, SM_P127:SM_P127 + 1],
                      in_=i["pos_127"][:, :]).then_inc(self.sl, 16)
        eng.dma_start(out=SM[:, SM_ONEC:SM_ONEC + 1],
                      in_=i["onesc"][:, :]).then_inc(self.sl, 16)
        eng.dma_start(out=SM[:, SM_NW:SM_NW + 16].rearrange("p (c o) -> p c o", c=NCHK),
                      in_=i["norm_w"].rearrange("c p -> p c").unsqueeze(2),
                      allow_slow_non_contiguous=True).then_inc(self.sl, 16)
        eng.dma_start(out=SM[:, SM_HO:SM_HO + 256],
                      in_=i["headones"][:, :]).then_inc(self.sl, 16)
        eng.dma_start(out=mk.BF1[:, 0:128], in_=i["diffT"][:, :]).then_inc(self.sl, 16)
        eng.dma_start(out=mk.IDT[:, :], in_=i["ident"][:, :]).then_inc(self.sl, 16)
        eng.wait_ge(self.sc_, 2)
        eng.activation(SM[:, SM_CD128:SM_CD128 + 16], SM[:, SM_LDB:SM_LDB + 16],
                       AF.Exp, scale=128.0)
        eng.activation(SM[:, SM_CDSUP:SM_CDSUP + 16], SM[:, SM_LDB:SM_LDB + 16],
                       AF.Exp, scale=float(128 * mk.cps))
        eng.activation(SM[:, SM_KDP:SM_KDP + 16], SM[:, SM_KARG:SM_KARG + 16], AF.Exp)
        for j in range(mk.cps):
            eng.activation(SM[:, SM_KDCD + 16 * j:SM_KDCD + 16 * (j + 1)],
                           SM[:, SM_ARG + 16 * j:SM_ARG + 16 * (j + 1)], AF.Exp)
        eng.wait_ge(self.sc_, 3)
        for h in range(H):
            eng.activation(mk.CB[:, 2048 + 128 * h:2048 + 128 * (h + 1)],
                           mk.BF2[:, 128 * h:128 * (h + 1)], AF.Exp)
        eng.drain()
        eng.sem_inc(self.sc_, 1)

    def tensor(self, eng):
        mk = self.mk
        SM = mk.SM
        eng.wait_ge(self.sl, 16 * 10)
        eng.matmul(mk.ps_f[0][:, 0:16], SM[0:1, SM_ROW + 16:SM_ROW + 144],
                   SM[0:1, SM_ROW:SM_ROW + 16], start=True, stop=False)
        mm = eng.matmul(mk.ps_f[0][:, 16:24], SM[0:1, SM_ROW + 16:SM_ROW + 144],
                        SM[0:1, SM_ROW + 272:SM_ROW + 280], start=False, stop=True)
        mm.then_inc(self.sc_, 1)

    def vector(self, eng):
        mk = self.mk
        SM = mk.SM
        eng.wait_ge(self.sc_, 1)
        eng.tensor_copy(SM[:, SM_LDB:SM_LDB + 16], mk.ps_f[0][:, 0:16])
        eng.tensor_copy(SM[:, SM_M8B:SM_M8B + 8], mk.ps_f[0][:, 16:24])
        eng.drain()
        eng.tensor_scalar_mul(SM[:, SM_KARG:SM_KARG + 16], SM[:, SM_LDB:SM_LDB + 16],
                              SM[:, SM_P127:SM_P127 + 1])
        eng.drain()
        for j in range(mk.cps):
            eng.tensor_scalar_mul(SM[:, SM_ARG + 16 * j:SM_ARG + 16 * (j + 1)],
                                  SM[:, SM_LDB:SM_LDB + 16],
                                  float(128 * (mk.cps - 1 - j)))
        eng.drain()
        for j in range(mk.cps):
            eng.tensor_add(SM[:, SM_ARG + 16 * j:SM_ARG + 16 * (j + 1)],
                           SM[:, SM_ARG + 16 * j:SM_ARG + 16 * (j + 1)],
                           SM[:, SM_KARG:SM_KARG + 16])
        eng.drain()
        eng.sem_inc(self.sc_, 1)
        for h in range(H):
            eng.tensor_scalar_mul(mk.BF2[:, 128 * h:128 * (h + 1)],
                                  mk.BF1[:, 0:128],
                                  SM[:, SM_LDB + h:SM_LDB + h + 1])
        eng.drain()
        eng.sem_inc(self.sc_, 1)

    def gpsimd(self, eng):
        mk, i = self.mk, self.mk.inp
        eng.collective_compute(
            "AllGather", ALU.bypass, replica_groups=[list(range(mk.n_cores))],
            ins=[mk.dum_in], outs=[mk.dum_out]).then_inc(self.sg_, 1)
        if "deq" in mk.phases:
            eng.wait_ge(mk.sem("d_bnc"), 16)
            eng.collective_compute(
                "AllGather", ALU.bypass,
                replica_groups=[list(range(mk.n_cores))],
                ins=[mk.mean_ag_in], outs=[mk.mean_ag_out]
                ).then_inc(mk.sem("d_agm"), 1)
        eng.dma_start(out=mk.CB[:, 0:2048].rearrange("p (c g) -> p c g", c=NCHK),
                      in_=i["Wg1"].rearrange("(c p) g -> p c g", p=128)
                      ).then_inc(self.sg_, 16)
        for name, dst in [("Wq", mk.wq_bf), ("Wk", mk.wk_bf), ("Wv", mk.wv_bf),
                          ("Wo", mk.wo_bf)]:
            eng.dma_start(out=dst[:, :, :],
                          in_=i[name].rearrange("(c p) e -> c p e", p=128)
                          ).then_inc(self.sg_, 16)
        eng.dma_start(out=mk.wg2d[:, :], in_=i["Wg2"][:, :]).then_inc(self.sg_, 16)
        eng.wait_ge(self.sg_, 16 * 6)


class PhA(Phase):

    CUT = 9
    CUT0 = 9

    def plan(self):
        mk = self.mk
        for n in ["x0", "x1", "a_ln", "tr", "trev", "a_mm", "a_ev2",
                  "sp0", "sp1", "a_spk0", "a_spk1", "a_spv0", "a_spv1",
                  "a_kvk0", "a_kvk1", "a_kvv0", "a_kvv1",
                  "a_kt", "a_st", "a_g1t", "a_g1s", "a_mrg",
                  "st_sq", "st_var", "st_sqr", "st_stat", "ag_a_d", "ag_a",
                  "a_w0", "a_w1", "a_w2", "a_w3"]:
            mk.sem(n)
        self.h2 = mk.sg_tok // 512
        self.nsh = mk.sg_tok // 512
        self.gq = NCHK * self.nsh
        self.gkv = mk.cps * 4
        self.gsg = self.gq + 2 * self.gkv
        self.sp_sg = self.gq + 2 * mk.cps

    def spill_done_wait(self, eng, sg):
        S = self.mk.sem
        for par in range(2):
            eng.wait_ge(S(f"sp{par}"), 16 * ((sg + 1) * self.gq // 2))
            eng.wait_ge(S(f"a_spk{par}"), 16 * ((sg + 1) * self.mk.cps // 2))
            eng.wait_ge(S(f"a_spv{par}"), 16 * ((sg + 1) * self.mk.cps // 2))

    def sync(self, eng):
        mk = self.mk
        S = mk.sem
        for sg in range(mk.nsg):
            for ci in range(mk.cps):
                gci = sg * mk.cps + ci
                if gci >= 2:
                    eng.wait_ge(S("a_ln"), gci - 1)
                eng.dma_start(out=mk.BF0[:, 2048 * (gci % 2):2048 * (gci % 2) + 2048],
                              in_=mk.inp["x"][128 * gci:128 * (gci + 1), :]
                              ).then_inc(S(f"x{gci % 2}"), 16)
            for wi, wsrc in enumerate([mk.wq_bf, mk.wk_bf, mk.wv_bf][:max(0, self.CUT - 2)]):
                eng.wait_ge(S("a_mm"), sg * self.gsg
                            + (0 if wi == 0 else self.gq + (wi - 1) * self.gkv))
                for dg in range(4):
                    eng.dma_start(out=mk.BB0[:, 8192 * dg:8192 * (dg + 1)]
                                  .rearrange("p (c e) -> p c e", c=4),
                                  in_=wsrc[4 * dg:4 * (dg + 1), :, :]
                                  .rearrange("c p e -> p c e")
                                  ).then_inc(S(f"a_w{dg}"), 16)
            if self.CUT < 6:
                continue
            self.spill_done_wait(eng, sg)
            if sg > 0:
                eng.wait_ge(S("a_st"), sg * mk.cps)
            for ci in range(mk.cps):
                if ci >= 2:
                    eng.wait_ge(S("a_st"), sg * mk.cps + ci - 1)
                gci = sg * mk.cps + ci
                n = sg * (mk.cps // 2) + ci // 2 + 1
                eng.dma_start(
                    out=mk.BB2[:, 4096 * (ci % 2):4096 * (ci % 2) + 2048],
                    in_=mk.k_sp[128 * gci:128 * (gci + 1), :]
                    ).then_inc(S(f"a_kvk{ci % 2}"), 16)
                eng.dma_start(
                    out=mk.BB2[:, 4096 * (ci % 2) + 2048:4096 * (ci % 2) + 4096],
                    in_=mk.v_sp[128 * gci:128 * (gci + 1), :]
                    ).then_inc(S(f"a_kvv{ci % 2}"), 16)

    def scalar(self, eng):
        mk = self.mk
        SM = mk.SM
        S = mk.sem
        for sg in range(mk.nsg):
            for ci in range(mk.cps):
                gci = sg * mk.cps + ci
                st = SM_ST + 8 * (gci % 2)
                xs = mk.BF0[:, 2048 * (gci % 2):2048 * (gci % 2) + 2048]
                eng.wait_ge(S(f"x{gci % 2}"), 16 * (gci // 2 + 1))
                if self.CUT0 >= 2:
                    eng.activation(mk.BF0[:, 4096:6144], xs, AF.Square,
                                   accum_out=SM[:, st + 1:st + 2])
                    eng.drain()
                eng.sem_inc(S("st_sq"), 1)
                eng.wait_ge(S("st_var"), gci + 1)
                if self.CUT0 >= 2:
                    eng.activation(SM[:, st + 5:st + 6], SM[:, st + 3:st + 4],
                                   AF.Sqrt)
                    eng.drain()
                eng.sem_inc(S("st_sqr"), 1)
                eng.wait_ge(S("st_stat"), gci + 1)
                if ci == 0 and sg > 0:
                    eng.wait_ge(S("a_st"), sg * mk.cps)
                if gci >= 1:
                    eng.wait_ge(S("tr"), 4 * gci)
                if self.CUT0 >= 3:
                    eng.activation(mk.BB3[:, 0:2048], xs, AF.Identity,
                                   bias=SM[:, st + 4:st + 5],
                                   scale=SM[:, st + 6:st + 7])
                    eng.drain()
                eng.sem_inc(S("a_ln"), 1)
            for ns in range(self.h2 if self.CUT >= 2 else 0):
                eng.wait_ge(S("a_g1t"), sg * self.h2 + ns + 1)
                eng.activation(mk.G1R[:, sg * mk.sg_tok + 512 * ns:
                                      sg * mk.sg_tok + 512 * ns + 512],
                               mk.ps_f[4 + ns % 2][:, :], AF.Identity)
                eng.drain()
                eng.sem_inc(S("a_g1s"), 1)
            for g in range(self.gq if self.CUT >= 3 else 0):
                ec, ns = divmod(g, self.nsh)
                eng.wait_ge(S("a_mm"), sg * self.gsg + g + 1)
                if g >= 2:
                    eng.wait_ge(S(f"sp{g % 2}"),
                                16 * (self.gq // 2 * sg + (g - 2) // 2 + 1))
                slot = mk.BB3[:, 512 * (g % 2):512 * (g % 2) + 512]
                eng.activation(slot, mk.ps_f[4 + g % 2][:, :], AF.Silu)
                eng.drain()
                eng.sem_inc(S("a_ev2"), 1)
                eng.dma_start(out=mk.q_sp[ec, :, sg * mk.sg_tok + 512 * ns:
                                          sg * mk.sg_tok + 512 * ns + 512],
                              in_=slot).then_inc(S(f"sp{g % 2}"), 16)
            for pi, (sp, tag) in enumerate(
                    [(mk.k_sp, "k"), (mk.v_sp, "v")][:max(0, self.CUT - 3)]):
                base = self.gq + pi * self.gkv
                for g in range(self.gkv):
                    tc, es = divmod(g, 4)
                    eng.wait_ge(S("a_mm"), sg * self.gsg + base + g + 1)
                    if g == 0:
                        if pi == 0:
                            for par in range(2):
                                eng.wait_ge(S(f"sp{par}"),
                                            16 * ((sg + 1) * self.gq // 2))
                        else:
                            for par in range(2):
                                eng.wait_ge(S(f"a_spk{par}"),
                                            16 * ((sg + 1) * mk.cps // 2))
                    if tc >= 2 and es == 0:
                        eng.wait_ge(S(f"a_sp{tag}{tc % 2}"),
                                    16 * (mk.cps // 2 * sg + (tc - 2) // 2 + 1))
                    slot = mk.BB3[:, 2048 * (tc % 2):2048 * (tc % 2) + 2048]
                    eng.activation(slot[:, 512 * es:512 * es + 512],
                                   mk.ps_f[4 + g % 2][:, :], AF.Silu)
                    eng.drain()
                    eng.sem_inc(S("a_ev2"), 1)
                    if es == 3:
                        gci = sg * mk.cps + tc
                        eng.dma_start(out=sp[128 * gci:128 * (gci + 1), :],
                                      in_=slot).then_inc(S(f"a_sp{tag}{tc % 2}"), 16)

    def vector(self, eng):
        mk = self.mk
        SM = mk.SM
        S = mk.sem
        for sg in range(mk.nsg):
            for ci in range(mk.cps):
                gci = sg * mk.cps + ci
                st = SM_ST + 8 * (gci % 2)
                xs = mk.BF0[:, 2048 * (gci % 2):2048 * (gci % 2) + 2048]
                eng.wait_ge(S(f"x{gci % 2}"), 16 * (gci // 2 + 1))
                if gci >= 2:
                    eng.wait_ge(S("a_ln"), gci - 1)
                if self.CUT0 >= 2:
                    eng.reduce_sum(SM[:, st:st + 1], xs, AX.X)
                eng.wait_ge(S("st_sq"), gci + 1)
                eng.drain()
                if self.CUT0 >= 2:
                    eng.tensor_scalar_mul(SM[:, st + 2:st + 3], SM[:, st:st + 1],
                                          1.0 / D)
                    eng.tensor_scalar_mul(SM[:, st + 3:st + 4],
                                          SM[:, st + 1:st + 2], 1.0 / D)
                    eng.drain()
                    eng.tensor_mul(SM[:, st + 7:st + 8], SM[:, st + 2:st + 3],
                                   SM[:, st + 2:st + 3])
                    eng.drain()
                    eng.tensor_sub(SM[:, st + 3:st + 4], SM[:, st + 3:st + 4],
                                   SM[:, st + 7:st + 8])
                    eng.drain()
                    eng.tensor_scalar_add(SM[:, st + 3:st + 4],
                                          SM[:, st + 3:st + 4], EPS)
                    eng.drain()
                eng.sem_inc(S("st_var"), 1)
                eng.wait_ge(S("st_sqr"), gci + 1)
                if self.CUT0 >= 2:
                    eng.reciprocal(SM[:, st + 6:st + 7], SM[:, st + 5:st + 6])
                    eng.drain()
                    eng.tensor_scalar(SM[:, st + 4:st + 5], SM[:, st + 2:st + 3],
                                      SM[:, st + 6:st + 7], -1.0,
                                      ALU.mult, ALU.mult)
                    eng.drain()
                eng.sem_inc(S("st_stat"), 1)
                eng.wait_ge(S("tr"), gci * 4 + 4)
                if self.CUT0 >= 5:
                    for dc in range(NCHK):
                        eng.tensor_copy(
                            mk.BB1[:, 1024 * dc + 128 * ci:
                                   1024 * dc + 128 * ci + 128],
                            mk.ps_b[:, 128 * dc:128 * dc + 128])
                    eng.drain()
                eng.sem_inc(S("trev"), 4)
            for ci in range(mk.cps if self.CUT >= 6 else 0):
                n = sg * (mk.cps // 2) + ci // 2 + 1
                eng.wait_ge(S(f"a_kvk{ci % 2}"), 16 * n)
                if ci >= 1:
                    eng.wait_ge(S("a_st"), sg * mk.cps + ci)
                kv = mk.BB2[:, 4096 * (ci % 2):4096 * (ci % 2) + 2048]
                kd = SM[:, SM_KDCD + 16 * ci:SM_KDCD + 16 * (ci + 1)]
                eng.tensor_tensor(
                    mk.BB3[:, 0:2048].rearrange("p (h e) -> p h e", h=H),
                    kv.rearrange("p (h e) -> p h e", h=H),
                    kd.unsqueeze(2).broadcast_to([128, H, HD]),
                    ALU.mult)
                eng.drain()
                eng.sem_inc(S("a_kt"), 1)
            if self.CUT < 6:
                continue
            eng.wait_ge(S("a_st"), (sg + 1) * mk.cps)
            if sg == 0:
                eng.memset(mk.BF1[:, :], 0.0)
                eng.drain()
            eng.tensor_tensor(
                mk.BF1[:, :].rearrange("p (h e) -> p h e", h=H),
                mk.BF1[:, :].rearrange("p (h e) -> p h e", h=H),
                SM[:, SM_CDSUP:SM_CDSUP + 16].unsqueeze(2)
                .broadcast_to([128, H, HD]), ALU.mult)
            eng.drain()
            for b in range(4):
                eng.tensor_add(mk.BF1[:, 512 * b:512 * b + 512],
                               mk.BF1[:, 512 * b:512 * b + 512], mk.ps_f[b][:, :])
            eng.drain()
            eng.sem_inc(S("a_mrg"), 1)

    def tensor(self, eng):
        mk = self.mk
        S = mk.sem
        for sg in range(mk.nsg):
            for ci in range(mk.cps):
                gci = sg * mk.cps + ci
                eng.wait_ge(S("a_ln"), gci + 1)
                for s in range(4):
                    b = (0, 2, 1, 3)[s]
                    if gci * 4 + s >= 4:
                        eng.wait_ge(S("trev"), gci * 4 + s - 3)
                    for k in range(4):
                        dc = b * 4 + k
                        mm = eng.transpose(
                            mk.ps_b[:, 512 * b + 128 * k:512 * b + 128 * k + 128],
                            mk.BB3[:, 128 * dc:128 * dc + 128], mk.IDT[:, :])
                    mm.then_inc(S("tr"), 1)
            eng.wait_ge(S("trev"), ((sg + 1) * mk.cps) * 4)
            if sg > 0 and self.CUT >= 3:
                eng.wait_ge(S("a_ev2"), sg * self.gsg)
            for ns in range(self.h2 if self.CUT >= 2 else 0):
                for dc in range(NCHK):
                    mm = eng.matmul(mk.ps_f[4 + ns % 2][:, :],
                               mk.CB[:, 128 * dc:128 * dc + 128],
                               mk.BB1[:, 1024 * dc + 512 * ns:
                                      1024 * dc + 512 * ns + 512],
                               start=(dc == 0), stop=(dc == NCHK - 1))
                mm.then_inc(S("a_g1t"), 1)
            eng.wait_ge(S("a_g1s"), (sg + 1) * self.h2 if self.CUT >= 2 else 0)
            for g in range(self.gq if self.CUT >= 3 else 0):
                ec, ns = divmod(g, self.nsh)
                if sg * self.gsg + g - 1 > 0:
                    eng.wait_ge(S("a_ev2"), sg * self.gsg + g - 1)
                for dc in range(NCHK):
                    if g == 0:
                        eng.wait_ge(S(f"a_w{dc // 4}"), 16 * (3 * sg + 1))
                    mm = eng.matmul(mk.ps_f[4 + g % 2][:, :],
                               mk.BB0[:, 2048 * dc + 128 * ec:
                                      2048 * dc + 128 * ec + 128],
                               mk.BB1[:, 1024 * dc + 512 * ns:
                                      1024 * dc + 512 * ns + 512],
                               start=(dc == 0), stop=(dc == NCHK - 1))
                mm.then_inc(S("a_mm"), 1)
            for g in range(self.gkv if self.CUT >= 4 else 0):
                tc, es = divmod(g, 4)
                eng.wait_ge(S("a_ev2"), sg * self.gsg + self.gq + g - 1)
                for dc in range(NCHK):
                    if g == 0:
                        eng.wait_ge(S(f"a_w{dc // 4}"), 16 * (3 * sg + 2))
                    mm = eng.matmul(mk.ps_f[4 + g % 2][:, :],
                               mk.BB1[:, 1024 * dc + 128 * tc:
                                      1024 * dc + 128 * tc + 128],
                               mk.BB0[:, 2048 * dc + 512 * es:
                                      2048 * dc + 512 * es + 512],
                               start=(dc == 0), stop=(dc == NCHK - 1))
                mm.then_inc(S("a_mm"), 1)
            for g in range(self.gkv if self.CUT >= 5 else 0):
                tc, es = divmod(g, 4)
                eng.wait_ge(S("a_ev2"),
                            sg * self.gsg + self.gq + self.gkv + g - 1)
                for dc in range(NCHK):
                    if g == 0:
                        eng.wait_ge(S(f"a_w{dc // 4}"), 16 * (3 * sg + 3))
                    mm = eng.matmul(mk.ps_f[4 + g % 2][:, :],
                               mk.BB1[:, 1024 * dc + 128 * tc:
                                      1024 * dc + 128 * tc + 128],
                               mk.BB0[:, 2048 * dc + 512 * es:
                                      2048 * dc + 512 * es + 512],
                               start=(dc == 0), stop=(dc == NCHK - 1))
                mm.then_inc(S("a_mm"), 1)
            if sg > 0 and self.CUT >= 6:
                eng.wait_ge(S("a_mrg"), sg)
            for ci in range(mk.cps if self.CUT >= 6 else 0):
                eng.wait_ge(S("a_kt"), sg * mk.cps + ci + 1)
                eng.wait_ge(S(f"a_kvv{ci % 2}"),
                            16 * (sg * (mk.cps // 2) + ci // 2 + 1))
                kt = mk.BB3[:, 0:2048]
                vv = mk.BB2[:, 4096 * (ci % 2) + 2048:4096 * (ci % 2) + 4096]
                for h in range(H):
                    mm = eng.matmul(
                        mk.ps_f[h // 4][:, 128 * (h % 4):128 * (h % 4) + 128],
                        kt[:, 128 * h:128 * h + 128],
                        vv[:, 128 * h:128 * h + 128],
                        start=(ci == 0 and h % 4 == 0),
                        stop=(ci == mk.cps - 1 and h % 4 == 3))
                mm.then_inc(S("a_st"), 1)

    def gpsimd(self, eng):
        if self.CUT < 7:
            return
        mk = self.mk
        S = mk.sem
        eng.wait_ge(S("a_mrg"), mk.nsg)
        eng.dma_start(out=mk.ag_a_in[:, :], in_=mk.BF1[:, :]).then_inc(S("ag_a_d"), 16)
        eng.wait_ge(S("ag_a_d"), 16)
        eng.collective_compute(
            "AllGather", ALU.bypass,
            replica_groups=[list(range(mk.n_cores))],
            ins=[mk.ag_a_in], outs=[mk.ag_a_out],
        ).then_inc(S("ag_a"), 1)


class PhAGState(Phase):

    def plan(self):
        for n in ["ag_a", "ags_l", "ags_v"]:
            self.mk.sem(n)

    def gpsimd(self, eng):
        mk = self.mk
        S = mk.sem
        eng.wait_ge(S("ag_a"), 1)
        for r in range(mk.n_cores):
            eng.wait_ge(S("ags_v"), r)
            eng.dma_start(out=mk.BF0[:, 0:2048], in_=mk.ag_a_out[r, :, :]
                          ).then_inc(S("ags_l"), 16)
        eng.wait_ge(S("ags_v"), mk.n_cores)

    def vector(self, eng):
        mk = self.mk
        S = mk.sem
        eng.memset(mk.BF1[:, :], 0.0)
        eng.drain()
        eng.sem_inc(S("ags_v"), 0) if False else None
        for r in range(mk.n_cores):
            eng.wait_ge(S("ags_l"), 16 * (r + 1))
            eng.scalar_tensor_tensor(mk.BF1[:, :], mk.BF0[:, 0:2048],
                                     mk.SM[:, SM_M8B + r:SM_M8B + r + 1],
                                     mk.BF1[:, :], ALU.mult, ALU.add)
            eng.drain()
            eng.sem_inc(S("ags_v"), 1)


class PhDeq(Phase):

    def plan(self):
        mk = self.mk
        for n in ["d_s1", "d_part", "d_agm", "d_mld",
                  "d_mn", "d_bnc", "d_arl", "d_art", "d_arv",
                  "d_wt", "d_stg", "d_exp", "d_eev", "d_fin", "d_out",
                  "agw", "c_wready", "su_l"]:
            mk.sem(n)
        self.NB = 180224

    def sync(self, eng):
        mk = self.mk
        S = mk.sem
        i = mk.inp
        for wi, name in enumerate(["w1s", "w2s", "w3s"]):
            if wi > 0:
                eng.wait_ge(S("d_s1"), wi)
            sh = i[name]
            flat = sh.rearrange("a b -> (a b)").rearrange("(s q) -> s q", q=16384)
            eng.dma_start(out=mk.BB1[0:88, :], in_=flat).then_inc(S("d_wt"), 16)
        eng.wait_ge(S("d_bnc"), 16 * 3)
        for a in range(2):
            for t in range(2):
                gi = 2 * a + t
                if gi >= 1:
                    eng.wait_ge(S("d_art"), gi)
                eng.dma_start(
                    out=mk.BF0[0:128, 4000 + 88 * t:4000 + 88 * (t + 1)],
                    in_=mk.sflat[a, :].rearrange("(x y) -> x y", y=88)
                    [128 * t:128 * (t + 1), :]).then_inc(S("d_arl"), 16)
        for fc in range(FCHK):
            if fc > 0:
                eng.wait_ge(S("d_fin"), fc)
            eng.dma_start(out=mk.BB3[:, 0:256],
                          in_=i["w3ts"][128 * fc:128 * (fc + 1), :]
                          ).then_inc(S("d_wt"), 16)
        for wi, name in enumerate(["w1ts", "w2ts"]):
            eng.wait_ge(S("d_bnc"), 16 * (5 + 2 * wi))
            for a in range(2):
                for t in range(6):
                    gi = 4 + wi * 12 + a * 6 + t
                    rows = 128 if t < 5 else 64
                    eng.wait_ge(S("d_art"), gi)
                    eng.dma_start(
                        out=mk.BF0[0:rows, 4000 + 32 * (t % 2):
                                   4000 + 32 * (t % 2) + 32],
                        in_=mk.sflat[a, :].rearrange("(x y) -> x y", y=32)
                        [128 * t:128 * t + rows, :]).then_inc(S("d_arl"), 16)
            for dc in range(NCHK):
                ti = FCHK + wi * NCHK + dc
                eng.wait_ge(S("d_fin"), ti)
                eng.dma_start(out=mk.BB3[:, 0:704],
                              in_=i[name][128 * dc:128 * (dc + 1), :]
                              ).then_inc(S("d_wt"), 16)

    def tensor(self, eng):
        mk = self.mk
        S = mk.sem
        SM = mk.SM
        for wi in range(3):
            eng.wait_ge(S("d_s1"), wi + 1)
            mm = eng.matmul(mk.ps_f[0][0:1, wi:wi + 1],
                            SM[0:88, 940 + wi:941 + wi],
                            SM[0:88, SM_ONEC:SM_ONEC + 1],
                            start=(wi == 0), stop=(wi == 2))
        mm.then_inc(S("d_part"), 1)
        eng.wait_ge(S("d_mld"), 16)
        for wi in range(3):
            mm = eng.matmul(mk.ps_f[1][0:1, wi:wi + 1],
                            SM[0:8, 924 + wi:925 + wi],
                            SM[0:8, SM_ONEC:SM_ONEC + 1],
                            start=(wi == 0), stop=(wi == 2))
        mm.then_inc(S("d_mn"), 1)
        eng.wait_ge(S("d_mn"), 2)
        mm = eng.matmul(mk.ps_f[1][:, 8:11], SM[0:1, SM_ROW + 16:SM_ROW + 144],
                        SM[0:1, 928:931], start=True, stop=True)
        mm.then_inc(S("d_mn"), 1)
        eng.wait_ge(S("su_l"), 16 * 11)
        for a in range(2):
            for t in range(2):
                gi = 2 * a + t
                eng.wait_ge(S("d_arl"), 16 * (gi + 1))
                mm = eng.transpose(mk.ps_f[2][0:88, 0:128],
                                   mk.BF0[0:128, 4000 + 88 * t:4000 + 88 * (t + 1)],
                                   SM[0:128, 472:600])
                mm.then_inc(S("d_art"), 1)
                eng.wait_ge(S("d_arv"), gi + 1)
        for fc in range(FCHK):
            eng.wait_ge(S("d_stg"), 32 * (fc + 1))
            eng.wait_ge(S("d_eev"), 2 * fc)
            mm = eng.matmul(mk.ps_f[4 + fc % 2][:, :],
                            SM[0:2, SM_ROW + 144:SM_ROW + 272],
                            mk.BF0[0:2, 1408:1920],
                            start=True, stop=True)
            mm.then_inc(S("d_exp"), 1)
        for wi in range(2):
            for a in range(2):
                for t in range(6):
                    gi = 4 + wi * 12 + a * 6 + t
                    rows = 128 if t < 5 else 64
                    eng.wait_ge(S("d_arl"), 16 * (gi + 1))
                    mm = eng.transpose(
                        mk.ps_f[2][0:32, 0:rows],
                        mk.BF0[0:rows, 4000 + 32 * (t % 2):
                               4000 + 32 * (t % 2) + 32],
                        SM[0:rows, 472:472 + rows])
                    mm.then_inc(S("d_art"), 1)
                    eng.wait_ge(S("d_arv"), gi + 1)
            for dc in range(NCHK):
                ti = FCHK + wi * NCHK + dc
                eng.wait_ge(S("d_stg"), 32 * (ti + 1))
                for rnd in range(2):
                    eng.wait_ge(S("d_eev"), 2 * ti + rnd)
                    off = 1408 + 704 * rnd
                    eng.matmul(mk.ps_f[4][:, 0:512],
                               SM[0:2, SM_ROW + 144:SM_ROW + 272],
                               mk.BF0[0:2, off:off + 512],
                               start=True, stop=True)
                    mm = eng.matmul(mk.ps_f[5][:, 0:192],
                                    SM[0:2, SM_ROW + 144:SM_ROW + 272],
                                    mk.BF0[0:2, off + 512:off + 704],
                                    start=True, stop=True)
                    mm.then_inc(S("d_exp"), 1)

    def vector(self, eng):
        mk = self.mk
        S = mk.sem
        SM = mk.SM
        BF1, BF2 = mk.BF1, mk.BF2
        for wi in range(3):
            eng.wait_ge(S("d_wt"), 16 * (wi + 1))
            eng.tensor_reduce(BF1[0:88, 256 * wi:256 * (wi + 1)],
                              mk.BB1[0:88, :].rearrange("s (b e) -> s b e", e=64),
                              AX.X, ALU.max, apply_absolute_value=True)
            eng.drain()
            eng.reduce_sum(SM[0:88, 940 + wi:941 + wi],
                           BF1[0:88, 256 * wi:256 * (wi + 1)], AX.X)
            eng.drain()
            eng.sem_inc(S("d_s1"), 1)
        eng.wait_ge(S("d_part"), 1)
        eng.memset(SM[0:1, 920:924], 0.0)
        eng.drain()
        eng.tensor_copy(SM[0:1, 920:923], mk.ps_f[0][0:1, 0:3])
        eng.drain()
        eng.sem_inc(S("d_part"), 1)
        eng.wait_ge(S("d_mn"), 1)
        eng.tensor_scalar_mul(SM[0:1, 928:931], mk.ps_f[1][0:1, 0:3],
                              1.0 / self.NB)
        eng.drain()
        eng.sem_inc(S("d_mn"), 1)
        eng.wait_ge(S("d_mn"), 3)
        eng.tensor_copy(SM[:, 936:939], mk.ps_f[1][:, 8:11])
        eng.drain()
        eng.sem_inc(S("d_mn"), 1)
        eng.wait_ge(S("su_c"), 4)
        for wi in range(3):
            s1 = BF1[0:88, 256 * wi:256 * (wi + 1)]
            eng.tensor_scalar(BF2[0:88, 0:256], s1, SM[0:88, 936 + wi:937 + wi],
                              None, ALU.subtract)
            eng.drain()
            eng.tensor_reduce(SM[0:88, 944:945], BF2[0:88, 0:256], AX.X,
                              ALU.max, apply_absolute_value=True)
            eng.drain()
            eng.reciprocal(SM[0:88, 945:946], SM[0:88, 944:945])
            eng.drain()
            eng.tensor_scalar_mul(SM[0:88, 946:947], SM[0:88, 945:946], 128.0)
            eng.drain()
            eng.reciprocal(SM[0:88, 947:948], SM[0:88, 946:947])
            eng.drain()
            eng.tensor_scalar_mul(BF2[0:88, 256:512], BF2[0:88, 0:256],
                                  SM[0:88, 946:947])
            eng.drain()
            eng.tensor_copy(mk.ITMP[0:88, :], BF2[0:88, 256:512])
            eng.drain()
            eng.tensor_copy(BF2[0:88, 256:512], mk.ITMP[0:88, :])
            eng.drain()
            eng.tensor_scalar_min(BF2[0:88, 256:512], BF2[0:88, 256:512], 127.0)
            eng.drain()
            eng.tensor_scalar_max(BF2[0:88, 256:512], BF2[0:88, 256:512], -128.0)
            eng.drain()
            eng.tensor_scalar_mul(mk.BB3[0:88, 3584:3840], BF2[0:88, 256:512],
                                  SM[0:88, 947:948])
            eng.drain()
            eng.tensor_scalar(BF2[0:88, 512 + 256 * wi:768 + 256 * wi],
                              mk.BB3[0:88, 3584:3840],
                              SM[0:88, 936 + wi:937 + wi], None, ALU.add)
            eng.drain()
            eng.reciprocal(BF2[0:88, 1280 + 256 * wi:1536 + 256 * wi], s1)
            eng.drain()
            eng.sem_inc(S("d_s1"), 1)
        for a in range(2):
            for t in range(2):
                gi = 2 * a + t
                eng.wait_ge(S("d_art"), gi + 1)
                eng.tensor_copy(
                    BF1[0:88, 768 + 256 * a + 128 * t:768 + 256 * a + 128 * (t + 1)],
                    mk.ps_f[2][0:88, 0:128])
                eng.drain()
                eng.sem_inc(S("d_arv"), 1)
        for fc in range(FCHK):
            eng.wait_ge(S("d_exp"), fc + 1)
            eng.tensor_copy(mk.BF3[:, 0:512], mk.ps_f[4 + fc % 2][:, :])
            eng.drain()
            eng.sem_inc(S("d_eev"), 2)
            self.emit_tile(eng, fc, 256)
        for wi in range(2):
            for a in range(2):
                for t in range(6):
                    gi = 4 + wi * 12 + a * 6 + t
                    cols = 128 if t < 5 else 64
                    eng.wait_ge(S("d_art"), gi + 1)
                    eng.tensor_copy(
                        mk.BF0[0:32, 704 * a + 128 * t:704 * a + 128 * t + cols],
                        mk.ps_f[2][0:32, 0:cols])
                    eng.drain()
                    eng.sem_inc(S("d_arv"), 1)
            for dc in range(NCHK):
                ti = FCHK + wi * NCHK + dc
                for rnd in range(2):
                    eng.wait_ge(S("d_exp"), FCHK + wi * 2 * NCHK + 2 * dc
                                + rnd + 1)
                    eng.tensor_copy(mk.BF3[:, 704 * rnd:704 * rnd + 512],
                                    mk.ps_f[4][:, 0:512])
                    eng.tensor_copy(mk.BF3[:, 704 * rnd + 512:704 * rnd + 704],
                                    mk.ps_f[5][:, 0:192])
                    eng.drain()
                    eng.sem_inc(S("d_eev"), 1)
                self.emit_tile(eng, ti, 704)
        eng.wait_ge(S("d_out"), 16 * (FCHK + 2 * NCHK))

    def emit_tile(self, eng, ti, W):
        mk = self.mk
        S = mk.sem
        if True:
            eng.wait_ge(S("d_wt"), 16 * (ti + 4))
            if ti >= 1:
                eng.wait_ge(S("d_out"), 16 * ti)
            wt = mk.BB3[:, 0:W]
            sc = mk.BB3[:, W:2 * W]
            acc = mk.BB3[:, 2 * W:3 * W]
            ind = mk.BB3[:, 3 * W:4 * W]
            eng.tensor_tensor(sc, wt, mk.BF3[:, 0:W], ALU.mult)
            eng.drain()
            eng.tensor_scalar(acc, sc, NF4_MID[0], NF4_DELTA[0],
                              ALU.is_gt, ALU.mult)
            eng.drain()
            for j in range(1, 15):
                eng.tensor_scalar(ind, sc, NF4_MID[j], None, ALU.is_gt)
                eng.drain()
                eng.scalar_tensor_tensor(acc, ind, NF4_DELTA[j], acc,
                                         ALU.mult, ALU.add)
                eng.drain()
            eng.scalar_tensor_tensor(ind, acc, -1.0, mk.BF3[:, W:2 * W],
                                     ALU.add, ALU.mult)
            eng.drain()
            eng.sem_inc(S("d_fin"), 1)

    def scalar(self, eng):
        mk = self.mk
        S = mk.sem
        SM = mk.SM
        i = mk.inp
        eng.dma_start(out=SM[:, 472:600], in_=i["identf"][:, :]
                      ).then_inc(S("su_l"), 16)
        eng.wait_ge(S("d_part"), 2)
        eng.dma_start(out=mk.mean_ag_in[:, :], in_=SM[0:1, 920:924]
                      ).then_inc(S("d_bnc"), 16)
        eng.wait_ge(S("d_agm"), 1)
        eng.dma_start(out=mk.SM[0:8, 924:928],
                      in_=mk.mean_ag_out.rearrange("r o w -> (r o) w")
                      ).then_inc(S("d_mld"), 16)
        eng.wait_ge(S("d_s1"), 6)
        eng.dma_start(out=mk.sflat[0, :].rearrange("(a b) -> a b", b=256),
                      in_=mk.BF2[0:88, 1792:2048]).then_inc(S("d_bnc"), 16)
        eng.dma_start(out=mk.sflat[1, :].rearrange("(a b) -> a b", b=256),
                      in_=mk.BF2[0:88, 1024:1280]).then_inc(S("d_bnc"), 16)
        for fc in range(FCHK):
            eng.wait_ge(S("d_fin"), fc + 1)
            eng.dma_start(out=mk.w3sh[128 * fc:128 * (fc + 1), :],
                          in_=mk.BB3[:, 768:1024]).then_inc(S("d_out"), 16)
        for wi in range(2):
            eng.wait_ge(S("d_arv"), 4 + wi * 12)
            eng.dma_start(out=mk.sflat[0, :].rearrange("(a b) -> a b", b=256),
                          in_=mk.BF2[0:88, 1280 + 256 * wi:1536 + 256 * wi]
                          ).then_inc(S("d_bnc"), 16)
            eng.dma_start(out=mk.sflat[1, :].rearrange("(a b) -> a b", b=256),
                          in_=mk.BF2[0:88, 512 + 256 * wi:768 + 256 * wi]
                          ).then_inc(S("d_bnc"), 16)
            sh = mk.w1sh if wi == 0 else mk.w2sh
            for dc in range(NCHK):
                ti = FCHK + wi * NCHK + dc
                eng.wait_ge(S("d_fin"), ti + 1)
                eng.dma_start(out=sh[128 * dc:128 * (dc + 1), :],
                              in_=mk.BB3[:, 2112:2816]).then_inc(S("d_out"), 16)

    def gpsimd(self, eng):
        mk = self.mk
        S = mk.sem
        for fc in range(FCHK):
            eng.wait_ge(S("d_arv"), 4)
            if fc >= 1:
                eng.wait_ge(S("d_exp"), fc)
            eng.dma_start(out=mk.BF0[0:2, 1408:1664],
                          in_=mk.BF1[2 * fc:2 * fc + 2, 768:1024]
                          ).then_inc(S("d_stg"), 16)
            eng.dma_start(out=mk.BF0[0:2, 1664:1920],
                          in_=mk.BF1[2 * fc:2 * fc + 2, 1024:1280]
                          ).then_inc(S("d_stg"), 16)
        for wi in range(2):
            for dc in range(NCHK):
                ti = FCHK + wi * NCHK + dc
                eng.wait_ge(S("d_arv"), 4 + (wi + 1) * 12)
                eng.wait_ge(S("d_exp"), FCHK + wi * 2 * NCHK + 2 * dc)
                eng.dma_start(out=mk.BF0[0:2, 1408:2112],
                              in_=mk.BF0[2 * dc:2 * dc + 2, 0:704]
                              ).then_inc(S("d_stg"), 16)
                eng.dma_start(out=mk.BF0[0:2, 2112:2816],
                              in_=mk.BF0[2 * dc:2 * dc + 2, 704:1408]
                              ).then_inc(S("d_stg"), 16)



class PhB(Phase):

    def plan(self):
        mk = self.mk
        for n in ["b_wo", "b_gate", "b_gev", "b_sc", "b_msk", "b_qt", "b_o",
                  "b_kt", "b_stA", "b_stB", "b_sevA", "b_sevB", "b_og",
                  "b_on", "b_attn", "b_aev", "b_ln", "b_Sbf", "b_init",
                  "b_q0", "b_q1", "b_k0", "b_k1", "b_v0",
                  "b_v1", "b_f0", "b_f1",
                  "b_initl", "b_nw2", "b_lstat", "b_lnorm"]:
            mk.sem(n)
        self.nck = mk.nchunks
        mkk = self.mk
        self.xb0, self.xb1 = mkk.xb
        self.trb = 4 * self.nck
        self.stb = self.nck
        self.spb = mkk.nsg * ((NCHK * (mkk.sg_tok // 512)) // 2)

    def sync(self, eng):
        mk = self.mk
        S = mk.sem
        for ec in range(NCHK):
            eng.dma_start(out=mk.BB0[:, 2048 * ec:2048 * (ec + 1)],
                          in_=mk.wo_bf[ec, :, :]).then_inc(S("b_wo"), 16)
        eng.wait_ge(S("b_init"), 2)
        for gci in range(self.nck):
            p = gci % 2
            if gci >= 2:
                eng.wait_ge(S("b_ln"), gci - 1)
            eng.dma_start(out=mk.BF0[:, 2048 * p:2048 * p + 2048],
                          in_=mk.inp["x"][128 * gci:128 * (gci + 1), :]
                          ).then_inc(S(f"x{p}"), 16)
            if gci >= 2:
                eng.wait_ge(S("b_qt"), gci - 1)
            eng.dma_start(out=mk.BB1[:, 12288 + 2048 * p:12288 + 2048 * p + 2048]
                          .rearrange("p (h e) -> p h e", h=H),
                          in_=mk.q_sp[:, :, 128 * gci:128 * (gci + 1)]
                          .rearrange("h p e -> p h e")).then_inc(S(f"b_q{p}"), 16)
            if gci >= 2:
                eng.wait_ge(S("b_kt"), gci - 1)
            eng.dma_start(out=mk.BB1[:, 2048 * p:2048 * p + 2048],
                          in_=mk.k_sp[128 * gci:128 * (gci + 1), :]
                          ).then_inc(S(f"b_k{p}"), 16)
            if gci >= 2:
                eng.wait_ge(S("b_stB"), gci - 1)
            eng.dma_start(out=mk.BB1[:, 4096 + 2048 * p:4096 + 2048 * p + 2048],
                          in_=mk.v_sp[128 * gci:128 * (gci + 1), :]
                          ).then_inc(S(f"b_v{p}"), 16)
            if gci >= 2:
                eng.wait_ge(S("b_sc"), gci - 1)
            for h in range(H):
                eng.dma_start(
                    out=mk.BB1[:, 8192 + 2048 * p + 128 * h:
                               8192 + 2048 * p + 128 * (h + 1)],
                    in_=mk.k_sp[128 * gci:128 * (gci + 1),
                                128 * h:128 * (h + 1)],
                    transpose=True).then_inc(S(f"b_f{p}"), 16)

    def tensor(self, eng):
        mk = self.mk
        S = mk.sem
        eng.wait_ge(S("b_initl"), 16 * 4)
        for sl in range(4):
            mm = eng.matmul(mk.ps_f[sl][:, :],
                            mk.SM[0:1, SM_ROW + 16:SM_ROW + 144],
                            mk.BF0[0:1, 512 * sl:512 * (sl + 1)],
                            start=True, stop=True)
        mm.then_inc(S("b_nw2"), 1)
        eng.wait_ge(S("b_init"), 2)
        eng.wait_ge(S("b_wo"), 16 * NCHK)
        for gci in range(self.nck):
            p = gci % 2
            if gci >= 1:
                eng.wait_ge(S("b_aev"), 8 * gci)
            for es in range(4):
                mm = eng.matmul(mk.ps_f[4 + es % 2][:, :],
                                mk.G1R[:, 128 * gci:128 * (gci + 1)],
                                mk.BB2[:, 4096 + 512 * es:4096 + 512 * (es + 1)],
                                start=True, stop=True)
                mm.then_inc(S("b_gate"), 1)
                if es >= 1:
                    eng.wait_ge(S("b_gev"), 4 * gci + es)
            eng.wait_ge(S(f"b_f{p}"), 16 * H * (gci // 2 + 1))
            eng.wait_ge(S(f"b_q{p}"), 16 * (gci // 2 + 1))
            if gci >= 1:
                eng.wait_ge(S("b_og"), 4 * gci)
            for h in range(H):
                mm = eng.matmul(mk.ps_f[h // 4][:, 128 * (h % 4):128 * (h % 4) + 128],
                                mk.BB1[:, 8192 + 2048 * p + 128 * h:
                                       8192 + 2048 * p + 128 * (h + 1)],
                                mk.BB1[:, 12288 + 2048 * p + 128 * h:
                                       12288 + 2048 * p + 128 * (h + 1)],
                                start=(h % 4 == 0), stop=(h % 4 == 3))
            mm.then_inc(S("b_sc"), 1)
            eng.wait_ge(S("b_msk"), gci + 1)
            eng.wait_ge(S("b_qt"), gci + 1)
            eng.wait_ge(S("b_Sbf"), gci + 1)
            eng.wait_ge(S(f"b_v{p}"), 16 * (gci // 2 + 1))
            for h in range(H):
                eng.matmul(mk.ps_f[h // 4][:, 128 * (h % 4):128 * (h % 4) + 128],
                           mk.BB2[:, 2048 + 128 * h:2048 + 128 * (h + 1)],
                           mk.BB1[:, 4096 + 2048 * p + 128 * h:
                                  4096 + 2048 * p + 128 * (h + 1)],
                           start=(h % 4 == 0), stop=False)
                mm = eng.matmul(
                    mk.ps_f[h // 4][:, 128 * (h % 4):128 * (h % 4) + 128],
                    mk.BB2[:, 14336 + 128 * h:14336 + 128 * (h + 1)],
                    mk.BB2[:, 128 * h:128 * (h + 1)],
                    start=False, stop=(h % 4 == 3))
            mm.then_inc(S("b_o"), 1)
            eng.wait_ge(S("b_kt"), gci + 1)
            eng.wait_ge(S("b_gev"), 4 * (gci + 1))
            for half in range(2):
                if half == 1:
                    eng.wait_ge(S("b_sevA"), gci + 1)
                for hh in range(8):
                    h = 8 * half + hh
                    bank = 4 + hh // 4
                    eng.matmul(mk.ps_f[bank][:, 128 * (hh % 4):128 * (hh % 4) + 128],
                               mk.BF2[:, 128 * h:128 * (h + 1)],
                               mk.BF1[:, 128 * h:128 * (h + 1)],
                               start=(hh % 4 == 0), stop=False)
                    mm = eng.matmul(
                        mk.ps_f[bank][:, 128 * (hh % 4):128 * (hh % 4) + 128],
                        mk.BB3[:, 128 * h:128 * (h + 1)],
                        mk.BB1[:, 4096 + 2048 * p + 128 * h:
                               4096 + 2048 * p + 128 * (h + 1)],
                        start=False, stop=(hh % 4 == 3))
                mm.then_inc(S("b_stA" if half == 0 else "b_stB"), 1)
            eng.wait_ge(S("b_on"), gci + 1)
            for half in range(2):
                if half == 0 and gci >= 1:
                    eng.wait_ge(S("trev"), self.trb + 2 * gci)
                if half == 1:
                    eng.wait_ge(S("trev"), self.trb + 2 * gci + 1)
                for hh in range(8):
                    h = 8 * half + hh
                    mm = eng.transpose(
                        mk.ps_b[:, 1024 * half + 128 * hh:
                                1024 * half + 128 * (hh + 1)],
                        mk.BB2[:, 6144 + 128 * h:6144 + 128 * (h + 1)],
                        mk.IDT[:, :])
                mm.then_inc(S("tr"), 1)
            eng.wait_ge(S("b_sevB"), gci + 1)
            eng.wait_ge(S("trev"), self.trb + 2 * (gci + 1))
            for dms in range(4):
                if dms >= 2:
                    eng.wait_ge(S("b_aev"), 8 * gci + 2 * (dms - 1))
                for half in range(2):
                    for ec in range(8 * half, 8 * half + 8):
                        mm = eng.matmul(
                            mk.ps_f[4 + dms % 2][:, :],
                            mk.BB2[:, 12288 + 128 * ec:12288 + 128 * (ec + 1)],
                            mk.BB0[:, 2048 * ec + 512 * dms:
                                   2048 * ec + 512 * (dms + 1)],
                            start=(ec == 0), stop=(ec == NCHK - 1))
                    mm.then_inc(S("b_attn"), 1)

    def vector(self, eng):
        mk = self.mk
        S = mk.sem
        SM = mk.SM
        eng.wait_ge(S("b_initl"), 16 * 4)
        for h in range(H):
            eng.tensor_scalar_mul(mk.BF2[:, 128 * h:128 * (h + 1)],
                                  mk.BF0[:, 2048:2176],
                                  SM[:, SM_CD128 + h:SM_CD128 + h + 1])
        for h in range(H):
            eng.tensor_scalar_mul(mk.BF3[:, 128 * h:128 * (h + 1)],
                                  mk.BF0[:, 4096:4224],
                                  SM[:, SM_LDB + h:SM_LDB + h + 1])
        eng.drain()
        eng.sem_inc(S("b_init"), 1)
        for gci in range(self.nck):
            p = gci % 2
            if gci == 0:
                eng.wait_ge(S("b_init"), 1)
            else:
                eng.wait_ge(S("b_sevB"), gci)
                eng.wait_ge(S("b_o"), gci)
            eng.tensor_copy(mk.BB2[:, 0:2048], mk.BF1[:, :])
            eng.drain()
            eng.sem_inc(S("b_Sbf"), 1)
            eng.wait_ge(S("b_sc"), gci + 1)
            for b in range(4):
                eng.tensor_tensor(
                    mk.BB2[:, 2048 + 512 * b:2048 + 512 * (b + 1)],
                    mk.ps_f[b][:, :],
                    mk.CB[:, 2048 + 512 * b:2048 + 512 * (b + 1)],
                    ALU.mult)
            eng.drain()
            eng.sem_inc(S("b_msk"), 1)
            eng.wait_ge(S(f"b_q{p}"), 16 * (gci // 2 + 1))
            eng.tensor_mul(mk.BB2[:, 14336:16384],
                           mk.BB1[:, 12288 + 2048 * p:12288 + 2048 * p + 2048],
                           mk.CB[:, 0:2048])
            eng.drain()
            eng.sem_inc(S("b_qt"), 1)
            eng.wait_ge(S(f"b_k{p}"), 16 * (gci // 2 + 1))
            if gci >= 1:
                eng.wait_ge(S("b_stB"), gci)
            eng.tensor_tensor(
                mk.BB3[:, 0:2048].rearrange("p (h e) -> p h e", h=H),
                mk.BB1[:, 2048 * p:2048 * p + 2048]
                .rearrange("p (h e) -> p h e", h=H),
                SM[:, SM_KDP:SM_KDP + 16].unsqueeze(2).broadcast_to([128, H, HD]),
                ALU.mult)
            eng.drain()
            eng.sem_inc(S("b_kt"), 1)
            eng.wait_ge(S("b_o"), gci + 1)
            eng.wait_ge(S("b_gev"), 4 * (gci + 1))
            if gci >= 1:
                pp = (gci - 1) % 2
                eng.wait_ge(S(f"sp{pp}"),
                            16 * (self.spb + (gci - 1) // 2 + 1))
            for b in range(4):
                eng.tensor_tensor(mk.BF3[:, 512 * b:512 * (b + 1)],
                                  mk.ps_f[b][:, :],
                                  mk.BB2[:, 8192 + 512 * b:8192 + 512 * (b + 1)],
                                  ALU.mult)
            eng.drain()
            eng.sem_inc(S("b_og"), 4)
            rs = SM_RMS + 16 * p
            eng.wait_ge(S("st_sq"), self.stb + 2 * gci + 1)
            eng.tensor_reduce(SM[:, rs:rs + 16],
                              mk.BF0[:, 4096:6144].rearrange(
                                  "p (h e) -> p h e", h=H),
                              AX.X, ALU.add)
            eng.drain()
            eng.tensor_scalar(SM[:, rs:rs + 16], SM[:, rs:rs + 16],
                              1.0 / HD, EPS, ALU.mult, ALU.add)
            eng.drain()
            eng.sem_inc(S("st_var"), 1)
            eng.wait_ge(S("st_sqr"), self.stb + 2 * gci + 1)
            eng.reciprocal(SM[:, rs:rs + 16], SM[:, rs:rs + 16])
            eng.drain()
            eng.tensor_tensor(
                mk.BF3[:, :].rearrange("p (h e) -> p h e", h=H),
                mk.BF3[:, :].rearrange("p (h e) -> p h e", h=H),
                SM[:, rs:rs + 16].unsqueeze(2).broadcast_to([128, H, HD]),
                ALU.mult)
            eng.drain()
            eng.tensor_mul(mk.BB2[:, 6144:8192], mk.BF3[:, :],
                           mk.BB2[:, 10240:12288])
            eng.drain()
            eng.sem_inc(S("b_on"), 1)
            eng.wait_ge(S("b_aev"), 8 * (gci + 1))
            st = SM_ST + 8 * p
            eng.reduce_sum(SM[:, st:st + 1], mk.BF3[:, :], AX.X)
            eng.wait_ge(S("st_sq"), self.stb + 2 * gci + 2)
            eng.drain()
            eng.tensor_scalar_mul(SM[:, st + 2:st + 3], SM[:, st:st + 1], 1.0 / D)
            eng.tensor_scalar_mul(SM[:, st + 3:st + 4], SM[:, st + 1:st + 2],
                                  1.0 / D)
            eng.drain()
            eng.tensor_mul(SM[:, st + 7:st + 8], SM[:, st + 2:st + 3],
                           SM[:, st + 2:st + 3])
            eng.drain()
            eng.tensor_sub(SM[:, st + 3:st + 4], SM[:, st + 3:st + 4],
                           SM[:, st + 7:st + 8])
            eng.drain()
            eng.tensor_scalar_add(SM[:, st + 3:st + 4], SM[:, st + 3:st + 4], EPS)
            eng.drain()
            eng.sem_inc(S("st_var"), 1)
            eng.wait_ge(S("st_sqr"), self.stb + 2 * gci + 2)
            eng.reciprocal(SM[:, st + 6:st + 7], SM[:, st + 5:st + 6])
            eng.drain()
            eng.tensor_scalar(SM[:, st + 4:st + 5], SM[:, st + 2:st + 3],
                              SM[:, st + 6:st + 7], -1.0, ALU.mult, ALU.mult)
            eng.drain()
            eng.sem_inc(S("b_lstat"), 1)
            eng.wait_ge(S("b_lnorm"), gci + 1)
            xbase = self.xb0 if p == 0 else self.xb1
            eng.wait_ge(S(f"x{p}"), 16 * (xbase + gci // 2 + 1))
            eng.tensor_add(mk.BF3[:, :], mk.BF0[:, 4096:6144],
                           mk.BF0[:, 2048 * p:2048 * p + 2048])
            eng.drain()
            eng.sem_inc(S("b_ln"), 1)

    def scalar(self, eng):
        mk = self.mk
        S = mk.sem
        SM = mk.SM
        i = mk.inp
        eng.dma_start(out=mk.BF0[:, 4096:4224], in_=i["posmat"][:, 0:128]
                      ).then_inc(S("b_initl"), 16)
        eng.dma_start(out=mk.BF0[:, 2048:2176], in_=i["identf"][:, :]
                      ).then_inc(S("b_initl"), 16)
        eng.dma_start(out=mk.BF0[0:1, 0:2048],
                      in_=i["norm_w"].rearrange("(o c) p -> o (c p)", o=1)
                      ).then_inc(S("b_initl"), 16)
        eng.dma_start(out=mk.BB2[:, 4096:6144], in_=mk.wg2d[:, :]
                      ).then_inc(S("b_initl"), 16)
        eng.wait_ge(S("b_init"), 1)
        for h in range(H):
            eng.activation(mk.CB[:, 128 * h:128 * (h + 1)],
                           mk.BF3[:, 128 * h:128 * (h + 1)], AF.Exp)
        eng.drain()
        eng.wait_ge(S("b_nw2"), 1)
        for sl in range(4):
            eng.activation(mk.BB2[:, 10240 + 512 * sl:10240 + 512 * (sl + 1)],
                           mk.ps_f[sl][:, :], AF.Identity)
        eng.drain()
        eng.sem_inc(S("b_init"), 1)
        for gci in range(self.nck):
            p = gci % 2
            for es in range(4):
                eng.wait_ge(S("b_gate"), 4 * gci + es + 1)
                eng.activation(mk.BB2[:, 8192 + 512 * es:8192 + 512 * (es + 1)],
                               mk.ps_f[4 + es % 2][:, :], AF.Sigmoid)
                eng.drain()
                eng.sem_inc(S("b_gev"), 1)
            eng.wait_ge(S("b_stA"), gci + 1)
            eng.activation(mk.BF1[:, 0:512], mk.ps_f[4][:, :], AF.Identity)
            eng.activation(mk.BF1[:, 512:1024], mk.ps_f[5][:, :], AF.Identity)
            eng.drain()
            eng.sem_inc(S("b_sevA"), 1)
            eng.wait_ge(S("b_stB"), gci + 1)
            eng.activation(mk.BF1[:, 1024:1536], mk.ps_f[4][:, :], AF.Identity)
            eng.activation(mk.BF1[:, 1536:2048], mk.ps_f[5][:, :], AF.Identity)
            eng.drain()
            eng.sem_inc(S("b_sevB"), 1)
            eng.wait_ge(S("b_og"), 4 * (gci + 1))
            eng.activation(mk.BF0[:, 4096:6144], mk.BF3[:, :], AF.Square)
            eng.drain()
            eng.sem_inc(S("st_sq"), 1)
            rs = SM_RMS + 16 * p
            eng.wait_ge(S("st_var"), self.stb + 2 * gci + 1)
            eng.activation(SM[:, rs:rs + 16], SM[:, rs:rs + 16], AF.Sqrt)
            eng.drain()
            eng.sem_inc(S("st_sqr"), 1)
            for half in range(2):
                eng.wait_ge(S("tr"), self.trb + 2 * gci + half + 1)
                for hh in range(8):
                    h = 8 * half + hh
                    eng.tensor_copy(
                        mk.BB2[:, 12288 + 128 * h:12288 + 128 * (h + 1)],
                        mk.ps_b[:, 1024 * half + 128 * hh:
                                1024 * half + 128 * (hh + 1)])                         if False else                     eng.activation(
                        mk.BB2[:, 12288 + 128 * h:12288 + 128 * (h + 1)],
                        mk.ps_b[:, 1024 * half + 128 * hh:
                                1024 * half + 128 * (hh + 1)], AF.Identity)
                eng.drain()
                eng.sem_inc(S("trev"), 1)
            for dms in range(4):
                eng.wait_ge(S("b_attn"), 8 * gci + 2 * dms + 2)
                eng.activation(mk.BF3[:, 512 * dms:512 * (dms + 1)],
                               mk.ps_f[4 + dms % 2][:, :], AF.Identity)
                eng.drain()
                eng.sem_inc(S("b_aev"), 2)
            st = SM_ST + 8 * p
            eng.wait_ge(S("b_aev"), 8 * (gci + 1))
            eng.activation(mk.BF0[:, 4096:6144], mk.BF3[:, :], AF.Square,
                           accum_out=SM[:, st + 1:st + 2])
            eng.drain()
            eng.sem_inc(S("st_sq"), 1)
            eng.wait_ge(S("st_var"), self.stb + 2 * gci + 2)
            eng.activation(SM[:, st + 5:st + 6], SM[:, st + 3:st + 4], AF.Sqrt)
            eng.drain()
            eng.sem_inc(S("st_sqr"), 1)
            eng.wait_ge(S("b_lstat"), gci + 1)
            eng.activation(mk.BF0[:, 4096:6144], mk.BF3[:, :], AF.Identity,
                           bias=SM[:, st + 4:st + 5], scale=SM[:, st + 6:st + 7])
            eng.drain()
            eng.sem_inc(S("b_lnorm"), 1)
            eng.wait_ge(S("b_ln"), gci + 1)
            if gci >= 2:
                eng.wait_ge(S(f"sp{p}"), 16 * (self.spb + gci // 2))
            eng.dma_start(out=mk.xmid[128 * gci:128 * (gci + 1), :],
                          in_=mk.BF3[:, :]).then_inc(S(f"sp{p}"), 16)
        eng.wait_ge(S("sp0"), 16 * (self.spb + (self.nck + 1) // 2))
        eng.wait_ge(S("sp1"), 16 * (self.spb + self.nck // 2))

    def gpsimd(self, eng):
        mk = self.mk
        S = mk.sem
        if "deq" in mk.phases:
            eng.wait_ge(S("d_out"), 16 * (FCHK + 2 * NCHK))
            for (src, dst) in ((mk.w1sh, mk.w1gr), (mk.w2sh, mk.w2gr),
                               (mk.w3sh, mk.w3g)):
                eng.collective_compute(
                    "AllGather", ALU.bypass,
                    replica_groups=[list(range(mk.n_cores))],
                    ins=[src], outs=[dst]).then_inc(S("agw"), 1)
            eng.wait_ge(S("agw"), 3)
            for (gr, lin) in ((mk.w1gr, mk.w1g), (mk.w2gr, mk.w2g)):
                for r in range(mk.n_cores):
                    eng.dma_start(out=lin[:, 704 * r:704 * (r + 1)],
                                  in_=gr[r, :, :]).then_inc(S("c_wready"), 16)


class PhC(Phase):

    def plan(self):
        mk = self.mk
        for n in ["c_ln", "c_ab", "c_aev", "c_bev", "c_h", "c_w3m", "c_oev",
                  "c_res", "c_w120", "c_w121",
                  "c_w30", "c_w31", "c_w32", "c_w33"]:
            mk.sem(n)
        n0, n1 = (self.mk.nchunks + 1) // 2, self.mk.nchunks // 2
        self.fb = (256 * n0, 256 * n1)
        self.nck = mk.nchunks
        self.ng = mk.ngc
        self.cpc = mk.cpc
        n0, n1 = mk.xb
        self.xb = (2 * n0, 2 * n1)
        self.stb = 3 * self.nck
        self.stb_stat = self.nck
        self.trb = 6 * self.nck
        self.spb = (mk.nsg * ((NCHK * (mk.sg_tok // 512)) // 2)
                    + (self.nck + 1) // 2,
                    mk.nsg * ((NCHK * (mk.sg_tok // 512)) // 2)
                    + self.nck // 2)

    def xcnt(self, gci):
        return self.xb[gci % 2] + gci // 2 + 1

    @property
    def cxb(self):
        return 4 if "B" in self.mk.phases else 0

    @property
    def bf3b(self):
        return (FCHK + 2 * NCHK) if "deq" in self.mk.phases else 0

    def ln3_ev(self, gci):
        h, ci = divmod(gci, 4)
        return (ci + 1) if h == 0 else (8 * h - 3 + ci)

    def ln4_ev(self, g, tc):
        if g < self.ng - 1:
            return 8 * g + 9 + tc
        return 8 * g + 5 + tc

    def w3_tile_dma(self, eng, g, il):
        mk = self.mk
        S = mk.sem
        i = g * 4 * FCHK + il
        if i >= 4:
            eng.wait_ge(S("c_w3m"), i - 3)
        dms, fc = divmod(il, FCHK)
        eng.dma_start(
            out=mk.BB3[:, 512 * (i % 4):512 * (i % 4 + 1)]
            .rearrange("p (r i) -> p r i", r=2),
            in_=mk.w3g[2 * dms:2 * dms + 2, 128 * fc:128 * (fc + 1), :]
            .rearrange("r f i -> f r i")
            ).then_inc(S(f"c_w3{i % 4}"), 16)

    def lnslot(self, gci):
        return self.mk.BB1[:, 8192 + 2048 * (gci % 4):
                           8192 + 2048 * (gci % 4) + 2048]

    def row(self, tc):
        mk = self.mk
        return (mk.BF0[:, 0:2048] if tc == 0 else
                mk.BF0[:, 2048:4096] if tc == 1 else
                mk.BF1[:, :] if tc == 2 else mk.BF2[:, :])

    def slab_dma(self, eng, g, fcp, gated):
        mk = self.mk
        S = mk.sem
        pw = fcp % 2
        if gated:
            eng.wait_ge(S("c_ab"), g * 2 * FCHK + 4 * (fcp - 1))
        for (wt, off) in ((mk.w1g, 0), (mk.w2g, 8192)):
            eng.dma_start(
                out=mk.BB2[:, off + 4096 * pw:off + 4096 * pw + 4096]
                .rearrange("p (c f) -> p c f", c=NCHK),
                in_=wt.rearrange("(c p) f -> p c f", p=128)
                [:, :, 256 * fcp:256 * (fcp + 1)]
                ).then_inc(S(f"c_w12{pw}"), 16)

    def sync(self, eng):
        mk = self.mk
        S = mk.sem
        if "deq" in mk.phases:
            eng.wait_ge(S("c_wready"), 16 * 16)
        for fcp in range(2):
            self.slab_dma(eng, 0, fcp, gated=False)
        for g in range(self.ng):
            for il in (0, 2):
                self.w3_tile_dma(eng, g, il)
            for fcp in range(2, FCHK // 2):
                self.slab_dma(eng, g, fcp, gated=True)
            if g + 1 < self.ng:
                for fcp in range(2):
                    eng.wait_ge(S("c_ab"),
                                g * 2 * FCHK + 4 * (FCHK // 2 - 1 + fcp))
                    self.slab_dma(eng, g + 1, fcp, gated=False)
            for il in range(4, 4 * FCHK, 2):
                self.w3_tile_dma(eng, g, il)
            for tc in range(self.cpc):
                gci = g * self.cpc + tc
                p = gci % 2
                eng.wait_ge(S("c_res"), gci + 1)
                if gci >= 2:
                    eng.wait_ge(S(f"sp{p}"), 16 * (self.spb[p] + gci // 2))
                eng.dma_start(out=mk.outp[128 * gci:128 * (gci + 1), :],
                              in_=self.row(tc)).then_inc(S(f"sp{p}"), 16)

    def t_trans(self, eng, gci):
        mk = self.mk
        S = mk.sem
        eng.wait_ge(S("c_ln"), gci + 1)
        lns = self.lnslot(gci)
        for s in range(4):
            b = (0, 2, 1, 3)[s]
            if gci * 4 + s >= 4:
                eng.wait_ge(S("trev"), self.trb + gci * 4 + s - 3)
            for k in range(4):
                dc = b * 4 + k
                mm = eng.transpose(
                    mk.ps_b[:, 512 * b + 128 * k:512 * b + 128 * k + 128],
                    lns[:, 128 * dc:128 * (dc + 1)],
                    mk.IDT[:, :])
            mm.then_inc(S("tr"), 1)

    def tensor(self, eng):
        mk = self.mk
        S = mk.sem
        for g in range(self.ng):
            if g == 0:
                for ci in range(self.cpc):
                    self.t_trans(eng, ci)
            eng.wait_ge(S("trev"), self.trb + (g + 1) * self.cpc * 4)
            for fc in range(FCHK):
                fcp = fc // 2
                pw = fcp % 2
                f2 = fc % 2
                ba, bb = (4, 5) if fc % 2 == 0 else (0, 1)
                if f2 == 0:
                    eng.wait_ge(S(f"c_w12{pw}"),
                                32 * (g * (FCHK // 2) // 2 + fcp // 2 + 1))
                if fc == 0:
                    if g >= 1:
                        eng.wait_ge(S("c_aev"), g * FCHK - 1)
                        eng.wait_ge(S("c_bev"), g * FCHK - 1)
                elif fc == 1:
                    if g >= 1:
                        eng.wait_ge(S("c_oev"), 16 * g)
                        eng.wait_ge(S("c_aev"), g * FCHK)
                        eng.wait_ge(S("c_bev"), g * FCHK)
                else:
                    eng.wait_ge(S("c_aev"), g * FCHK + fc - 1)
                    eng.wait_ge(S("c_bev"), g * FCHK + fc - 1)
                for dc in range(NCHK):
                    eng.matmul(mk.ps_f[ba][:, :],
                               mk.BB2[:, 4096 * pw + 256 * dc + 128 * f2:
                                      4096 * pw + 256 * dc + 128 * (f2 + 1)],
                               mk.BB1[:, 512 * dc:512 * (dc + 1)],
                               start=(dc == 0), stop=(dc == NCHK - 1))
                for dc in range(NCHK):
                    mm = eng.matmul(mk.ps_f[bb][:, :],
                                    mk.BB2[:, 8192 + 4096 * pw + 256 * dc + 128 * f2:
                                           8192 + 4096 * pw + 256 * dc + 128 * (f2 + 1)],
                                    mk.BB1[:, 512 * dc:512 * (dc + 1)],
                                    start=(dc == 0), stop=(dc == NCHK - 1))
                mm.then_inc(S("c_ab"), 2)
            eng.wait_ge(S("c_h"), (g + 1) * FCHK)
            for dms in range(4):
                if g * 4 + dms >= 1:
                    eng.wait_ge(S("c_oev"), 4 * (g * 4 + dms))
                for fc in range(FCHK):
                    i = (g * 4 + dms) * FCHK + fc
                    eng.wait_ge(S(f"c_w3{i % 4}"), 16 * (i // 4 + 1))
                    for tc in range(4):
                        mm = eng.matmul(
                            mk.ps_f[tc][:, :],
                            mk.BB0[:, 512 * fc + 128 * tc:
                                   512 * fc + 128 * (tc + 1)],
                            mk.BB3[:, 512 * (i % 4):512 * (i % 4) + 512],
                            start=(fc == 0), stop=(fc == FCHK - 1))
                    mm.then_inc(S("c_w3m"), 1)
                if g < self.ng - 1:
                    self.t_trans(eng, (g + 1) * self.cpc + dms)

    def xslot(self, gci):
        return self.mk.XB[:, 2048 * (gci % 2):2048 * (gci % 2) + 2048]

    def v_ln3(self, eng, gci):
        mk = self.mk
        S = mk.sem
        SM = mk.SM
        p = gci % 2
        st = SM_ST + 8 * p
        xs = self.xslot(gci)
        ev = self.ln3_ev(gci)
        eng.wait_ge(S("b_initl"), 16 * (gci + 1 + self.cxb))
        if gci >= 2:
            eng.wait_ge(S("c_ln"), gci - 1)
        eng.reduce_sum(SM[:, st:st + 1], xs, AX.X)
        eng.wait_ge(S("st_sq"), self.stb + ev)
        eng.drain()
        self.v_stats(eng, st, ev)

    def v_stats(self, eng, st, ev):
        mk = self.mk
        S = mk.sem
        SM = mk.SM
        eng.tensor_scalar_mul(SM[:, st + 2:st + 3], SM[:, st:st + 1],
                              1.0 / D)
        eng.tensor_scalar_mul(SM[:, st + 3:st + 4],
                              SM[:, st + 1:st + 2], 1.0 / D)
        eng.drain()
        eng.tensor_mul(SM[:, st + 7:st + 8], SM[:, st + 2:st + 3],
                       SM[:, st + 2:st + 3])
        eng.drain()
        eng.tensor_sub(SM[:, st + 3:st + 4], SM[:, st + 3:st + 4],
                       SM[:, st + 7:st + 8])
        eng.drain()
        eng.tensor_scalar_add(SM[:, st + 3:st + 4],
                              SM[:, st + 3:st + 4], EPS)
        eng.drain()
        eng.sem_inc(S("st_var"), 1)
        eng.wait_ge(S("st_sqr"), self.stb + ev)
        eng.reciprocal(SM[:, st + 6:st + 7], SM[:, st + 5:st + 6])
        eng.drain()
        eng.tensor_scalar(SM[:, st + 4:st + 5], SM[:, st + 2:st + 3],
                          SM[:, st + 6:st + 7], -1.0, ALU.mult, ALU.mult)
        eng.drain()
        eng.sem_inc(S("st_stat"), 1)

    def v_trev(self, eng, gci):
        mk = self.mk
        S = mk.sem
        ci = gci % 4
        for s in range(4):
            b = (0, 2, 1, 3)[s]
            eng.wait_ge(S("tr"), self.trb + gci * 4 + s + 1)
            for k in range(4):
                dc = b * 4 + k
                eng.tensor_copy(
                    mk.BB1[:, 512 * dc + 128 * ci:
                           512 * dc + 128 * (ci + 1)],
                    mk.ps_b[:, 512 * b + 128 * k:512 * b + 128 * k + 128])
            eng.drain()
            eng.sem_inc(S("trev"), 1)

    def vector(self, eng):
        mk = self.mk
        S = mk.sem
        SM = mk.SM
        for ci in range(self.cpc):
            self.v_ln3(eng, ci)
        for ci in range(self.cpc):
            self.v_trev(eng, ci)
        for g in range(self.ng):
            for fc in range(FCHK):
                eng.wait_ge(S("c_ab"), g * 2 * FCHK + 2 * (fc + 1))
                eng.tensor_copy(mk.CB[:, 512:1024],
                                mk.ps_f[5 if fc % 2 == 0 else 1][:, :])
                eng.drain()
                eng.sem_inc(S("c_bev"), 1)
                eng.wait_ge(S("c_aev"), g * FCHK + fc + 1)
                eng.tensor_mul(mk.BB0[:, 512 * fc:512 * (fc + 1)],
                               mk.CB[:, 0:512], mk.CB[:, 512:1024])
                eng.drain()
                eng.sem_inc(S("c_h"), 1)
            if g < self.ng - 1:
                for ci in range(self.cpc):
                    self.v_ln3(eng, (g + 1) * self.cpc + ci)
            for dms in range(4):
                eng.wait_ge(S("c_w3m"), (g * 4 + dms + 1) * FCHK)
                if dms == 0 and g >= 1:
                    eng.wait_ge(S("sp0"), 16 * (self.spb[0] + 2 * g))
                    eng.wait_ge(S("sp1"), 16 * (self.spb[1] + 2 * g))
                for tc in range(4):
                    eng.tensor_copy(self.row(tc)[:, 512 * dms:512 * (dms + 1)],
                                    mk.ps_f[tc][:, :])
                eng.drain()
                eng.sem_inc(S("c_oev"), 4)
                if g < self.ng - 1 and dms >= 1:
                    self.v_trev(eng, (g + 1) * self.cpc + dms - 1)
            if g < self.ng - 1:
                self.v_trev(eng, (g + 1) * self.cpc + 3)
            for tc in range(self.cpc):
                gci = g * self.cpc + tc
                p = tc % 2
                st = SM_ST + 8 * p
                ev = self.ln4_ev(g, tc)
                row = self.row(tc)
                eng.reduce_sum(SM[:, st:st + 1], row, AX.X)
                eng.wait_ge(S("st_sq"), self.stb + ev)
                eng.drain()
                self.v_stats(eng, st, ev)
                eng.wait_ge(S("b_lnorm"), self.nck + gci + 1)
                eng.wait_ge(S("d_out"), 16 * (self.bf3b + gci + 1))
                eng.tensor_add(row, mk.BF0[:, 4096:6144], mk.BF3[:, :])
                eng.drain()
                eng.sem_inc(S("c_res"), 1)

    def s_ln3(self, eng, gci):
        mk = self.mk
        S = mk.sem
        SM = mk.SM
        p = gci % 2
        st = SM_ST + 8 * p
        xs = self.xslot(gci)
        ev = self.ln3_ev(gci)
        eng.wait_ge(S("b_initl"), 16 * (gci + 1 + self.cxb))
        eng.activation(mk.BF0[:, 4096:6144], xs, AF.Square,
                       accum_out=SM[:, st + 1:st + 2])
        eng.drain()
        eng.sem_inc(S("st_sq"), 1)
        eng.wait_ge(S("st_var"), self.stb + ev)
        eng.activation(SM[:, st + 5:st + 6], SM[:, st + 3:st + 4],
                       AF.Sqrt)
        eng.drain()
        eng.sem_inc(S("st_sqr"), 1)
        eng.wait_ge(S("st_stat"), self.stb_stat + ev)
        if gci >= 4:
            eng.wait_ge(S("tr"), self.trb + 4 * (gci - 3))
        eng.activation(self.lnslot(gci), xs, AF.Identity,
                       bias=SM[:, st + 4:st + 5],
                       scale=SM[:, st + 6:st + 7])
        eng.drain()
        eng.sem_inc(S("c_ln"), 1)
        if gci + 2 < self.nck:
            self.xload(eng, gci + 2)

    def xload(self, eng, gci):
        mk = self.mk
        S = mk.sem
        if gci >= 1:
            eng.wait_ge(S("b_initl"), 16 * (gci + self.cxb))
        if gci >= 2:
            eng.wait_ge(S("c_ln"), gci - 1)
        eng.dma_start(out=self.xslot(gci),
                      in_=mk.xmid[128 * gci:128 * (gci + 1), :]
                      ).then_inc(S("b_initl"), 16)

    def scalar(self, eng):
        mk = self.mk
        S = mk.sem
        SM = mk.SM
        for gci in range(2):
            self.xload(eng, gci)
        for ci in range(self.cpc):
            self.s_ln3(eng, ci)
        for g in range(self.ng):
            for fc in range(FCHK):
                eng.wait_ge(S("c_ab"), g * 2 * FCHK + 2 * fc + 2)
                if fc >= 1:
                    eng.wait_ge(S("c_h"), g * FCHK + fc)
                eng.activation(mk.CB[:, 0:512],
                               mk.ps_f[4 if fc % 2 == 0 else 0][:, :], AF.Silu)
                eng.drain()
                eng.sem_inc(S("c_aev"), 1)
            if g < self.ng - 1:
                for ci in range(self.cpc):
                    self.s_ln3(eng, (g + 1) * self.cpc + ci)
            for tc in range(self.cpc):
                gci = g * self.cpc + tc
                p = tc % 2
                st = SM_ST + 8 * p
                ev = self.ln4_ev(g, tc)
                row = self.row(tc)
                eng.wait_ge(S("c_oev"), 16 * (g + 1))
                eng.wait_ge(S("c_res"), gci)
                eng.dma_start(out=mk.BF3[:, :],
                              in_=mk.xmid[128 * gci:128 * (gci + 1), :]
                              ).then_inc(S("d_out"), 16)
                eng.activation(mk.BF0[:, 4096:6144], row, AF.Square,
                               accum_out=SM[:, st + 1:st + 2])
                eng.drain()
                eng.sem_inc(S("st_sq"), 1)
                eng.wait_ge(S("st_var"), self.stb + ev)
                eng.activation(SM[:, st + 5:st + 6], SM[:, st + 3:st + 4],
                               AF.Sqrt)
                eng.drain()
                eng.sem_inc(S("st_sqr"), 1)
                eng.wait_ge(S("st_stat"), self.stb_stat + ev)
                eng.activation(mk.BF0[:, 4096:6144], row, AF.Identity,
                               bias=SM[:, st + 4:st + 5],
                               scale=SM[:, st + 6:st + 7])
                eng.drain()
                eng.sem_inc(S("b_lnorm"), 1)
        eng.wait_ge(S("sp0"), 16 * (self.spb[0] + (self.nck + 1) // 2))
        eng.wait_ge(S("sp1"), 16 * (self.spb[1] + self.nck // 2))

    def gpsimd(self, eng):
        mk = self.mk
        S = mk.sem
        for g in range(self.ng):
            for il in range(1, 4 * FCHK, 2):
                self.w3_tile_dma(eng, g, il)
        eng.wait_ge(S("sp0"), 16 * (self.spb[0] + (self.nck + 1) // 2))
        eng.wait_ge(S("sp1"), 16 * (self.spb[1] + self.nck // 2))


from concourse.bass_utils import run_bass_kernel_spmd


def kernel(L0_attn_x, L0_attn_log_decay, L0_attn_Wq, L0_attn_Wk, L0_attn_Wv,
           L0_attn_Wo, L0_attn_Wg1, L0_attn_Wg2, L0_attn_norm_w,
           L0_ffn_weight1, L0_ffn_weight2, L0_ffn_weight3):
    NCORES = 8
    TPC = 4096
    x = np.ascontiguousarray(
        np.asarray(L0_attn_x, dtype=np.float32).reshape(NCORES * TPC, D))
    ld = np.ascontiguousarray(
        np.asarray(L0_attn_log_decay, dtype=np.float32).reshape(16, 1))
    Wq = np.ascontiguousarray(np.asarray(L0_attn_Wq, dtype=np.float32))
    Wk = np.ascontiguousarray(np.asarray(L0_attn_Wk, dtype=np.float32))
    Wv = np.ascontiguousarray(np.asarray(L0_attn_Wv, dtype=np.float32))
    Wo = np.ascontiguousarray(np.asarray(L0_attn_Wo, dtype=np.float32))
    Wg1 = np.ascontiguousarray(np.asarray(L0_attn_Wg1, dtype=np.float32))
    Wg2 = np.ascontiguousarray(np.asarray(L0_attn_Wg2, dtype=np.float32))
    nw = np.ascontiguousarray(
        np.asarray(L0_attn_norm_w, dtype=np.float32).reshape(NCHK, 128))
    w1 = np.asarray(L0_ffn_weight1).view(ml_dtypes.bfloat16) \
        if np.asarray(L0_ffn_weight1).dtype != ml_dtypes.bfloat16 \
        else np.asarray(L0_ffn_weight1)
    w2 = np.asarray(L0_ffn_weight2)
    w3 = np.asarray(L0_ffn_weight3)
    w1 = np.ascontiguousarray(np.asarray(w1).reshape(DFF, D))
    w2 = np.ascontiguousarray(np.asarray(w2).reshape(DFF, D))
    w3 = np.ascontiguousarray(np.asarray(w3).reshape(D, DFF))
    w1t = np.ascontiguousarray(w1.T)
    w2t = np.ascontiguousarray(w2.T)
    w3t = np.ascontiguousarray(w3.T)

    mk = MK(tpc=TPC, n_cores=NCORES)
    nc = mk.build()

    in_maps = []
    for c in range(NCORES):
        m = {
            "x": np.ascontiguousarray(x[TPC * c:TPC * (c + 1)]),
            "log_decay": ld,
            "Wq": Wq, "Wk": Wk, "Wv": Wv, "Wo": Wo,
            "Wg1": Wg1, "Wg2": Wg2, "norm_w": nw,
            "w1s": np.ascontiguousarray(w1[704 * c:704 * (c + 1)]),
            "w2s": np.ascontiguousarray(w2[704 * c:704 * (c + 1)]),
            "w3s": np.ascontiguousarray(w3[256 * c:256 * (c + 1)]),
            "w1ts": np.ascontiguousarray(w1t[:, 704 * c:704 * (c + 1)]),
            "w2ts": np.ascontiguousarray(w2t[:, 704 * c:704 * (c + 1)]),
            "w3ts": np.ascontiguousarray(w3t[:, 256 * c:256 * (c + 1)]),
        }
        m.update(host_consts(c))
        in_maps.append(m)

    res = run_bass_kernel_spmd(nc, in_maps, core_ids=list(range(NCORES)))
    out = np.concatenate([np.asarray(res.results[c]["out"], dtype=np.float32)
                          .reshape(TPC, D) for c in range(NCORES)], axis=0)
    return out.reshape(4, 8192, D)


# version 13
# speedup vs baseline: 1.1781x; 1.0283x over previous
import numpy as np
import ml_dtypes
from contextlib import ExitStack

import concourse.bass as bass
import concourse.mybir as mybir

F32 = mybir.dt.float32
BF16 = mybir.dt.bfloat16
I32 = mybir.dt.int32
AF = mybir.ActivationFunctionType
ALU = mybir.AluOpType
AX = mybir.AxisListType

D = 2048
H = 16
HD = 128
DFF = 5632
NCHK = 16
FCHK = 44
EPS = 1e-5

NF4 = np.array([-1.0, -0.6962, -0.5251, -0.3949, -0.2844, -0.1848, -0.0911,
                0.0, 0.0796, 0.1609, 0.2461, 0.3379, 0.4407, 0.5626, 0.723,
                1.0], dtype=ml_dtypes.bfloat16).astype(np.float32)
NF4_MID = [float(x) for x in ((NF4[:-1] + NF4[1:]) / 2)]
NF4_DELTA = [float(x) for x in (NF4[1:] - NF4[:-1])]


def host_consts(core_id):
    c = {}
    c["ident"] = np.eye(128, dtype=np.float32).astype(ml_dtypes.bfloat16)
    c["identf"] = np.eye(128, dtype=np.float32)
    c["ones1"] = np.ones((1, 128), dtype=np.float32)
    c["onesc"] = np.ones((128, 1), dtype=np.float32)
    c["pos_127"] = (127 - np.arange(128, dtype=np.float32)).reshape(128, 1)
    i = np.arange(128)
    dT = (i[None, :] - i[:, None]).astype(np.float32)
    dT[dT < 0] = 1e9
    c["diffT"] = dT
    ho = np.zeros((128, H, H), dtype=np.float32)
    for h in range(H):
        ho[:, h, h] = 1.0
    c["headones"] = ho.reshape(128, H * H)
    eb = np.zeros((2, 128), dtype=np.float32)
    eb[0, 0:64] = 1.0
    eb[1, 64:128] = 1.0
    c["ebase2"] = eb
    c["posmat"] = np.tile(np.arange(1, 129, dtype=np.float32), (128, 4))
    m8 = np.zeros((1, 8), dtype=np.float32)
    if core_id % 2 == 1:
        m8[0, core_id - 1] = 1.0
    c["mask8"] = m8
    return c


SM_LDB = 0
SM_CD128 = 16
SM_CDSUP = 32
SM_KDP = 48
SM_KDCD = 64
SM_NW = 192
SM_HO = 208
SM_M8B = 464
SM_ST = 472
SM_ARG = 492
SM_ROW = 640
SM_MEAN = 920
SM_P127 = 960
SM_ONEC = 961
SM_RMS = 962
SM_KARG = 476


class MK:
    def __init__(self, tpc=4096, n_cores=8, phases=None):
        self.tpc = tpc
        self.n_cores = n_cores
        self.sg_tok = min(1024, tpc)
        self.cps = self.sg_tok // 128
        self.nsg = tpc // self.sg_tok
        self.gb_tok = min(512, tpc)
        self.ngb = tpc // self.gb_tok
        self.cpb = self.gb_tok // 128
        self.gc_tok = min(512, tpc)
        self.ngc = tpc // self.gc_tok
        self.cpc = self.gc_tok // 128
        self.nchunks = tpc // 128
        self.phases = phases or ["setup", "deq", "A", "AGstate", "B", "C"]
        self.nc = bass.Bass("TRN2", target_bir_lowering=False)
        self.stack = ExitStack()
        self.declare()

    def declare(self):
        nc = self.nc
        t = self.tpc
        self.inp = {}
        for name, shape, dt in [
            ("x", [t, D], F32),
            ("log_decay", [16, 1], F32),
            ("Wq", [D, D], F32), ("Wk", [D, D], F32), ("Wv", [D, D], F32),
            ("Wo", [D, D], F32), ("Wg1", [D, HD], F32), ("Wg2", [HD, D], F32),
            ("norm_w", [NCHK, 128], F32),
            ("w1s", [DFF // 8, D], BF16), ("w2s", [DFF // 8, D], BF16),
            ("w3s", [D // 8, DFF], BF16),
            ("w1ts", [D, DFF // 8], BF16), ("w2ts", [D, DFF // 8], BF16),
            ("w3ts", [DFF, D // 8], BF16),
        ]:
            self.inp[name] = nc.dram_tensor(name, shape, dt, kind="ExternalInput").ap()
        for name, arr in host_consts(0).items():
            dtv = BF16 if arr.dtype == ml_dtypes.bfloat16 else F32
            self.inp[name] = nc.dram_tensor(
                name, list(arr.shape), dtv, kind="ExternalInput").ap()
        self.outp = nc.dram_tensor("out", [t, D], F32, kind="ExternalOutput").ap()

        dr = lambda n, s, d: nc.dram_tensor(n, s, d, kind="Internal").ap()
        self.wq_bf = dr("wq_bf", [NCHK, 128, D], BF16)
        self.wk_bf = dr("wk_bf", [NCHK, 128, D], BF16)
        self.wv_bf = dr("wv_bf", [NCHK, 128, D], BF16)
        self.wo_bf = dr("wo_bf", [NCHK, 128, D], BF16)
        self.wg2d = dr("wg2d", [HD, D], BF16)
        self.q_sp = dr("q_sp", [NCHK, 128, t], BF16)
        self.k_sp = dr("k_sp", [t, D], BF16)
        self.v_sp = dr("v_sp", [t, D], BF16)
        self.xmid = dr("xmid", [t, D], F32)
        self.ag_a_in = dr("ag_a_in", [128, D], F32)
        self.ag_a_out = dr("ag_a_out", [self.n_cores, 128, D], F32)
        self.mean_ag_in = dr("mean_ag_in", [1, 4], F32)
        self.mean_ag_out = dr("mean_ag_out", [self.n_cores, 1, 4], F32)
        self.dum_in = dr("dum_in", [1, 4], F32)
        self.dum_out = dr("dum_out", [self.n_cores, 1, 4], F32)
        self.sflat = dr("sflat", [2, 22528], F32)
        self.w1sh = dr("w1sh", [D, DFF // 8], BF16)
        self.w2sh = dr("w2sh", [D, DFF // 8], BF16)
        self.w3sh = dr("w3sh", [DFF, D // 8], BF16)
        self.w1gr = dr("w1gr", [8, D, DFF // 8], BF16)
        self.w2gr = dr("w2gr", [8, D, DFF // 8], BF16)
        self.w1g = dr("w1g", [D, DFF], BF16)
        self.w2g = dr("w2g", [D, DFF], BF16)
        self.w3g = dr("w3g", [8, DFF, D // 8], BF16)

        sb = lambda n, s, d: self.stack.enter_context(nc.sbuf_tensor(n, s, d))
        self.BB0 = sb("BB0", [128, 32768], BF16)
        self.BB1 = sb("BB1", [128, 16384], BF16)
        self.BB2 = sb("BB2", [128, 16384], BF16)
        self.BB3 = sb("BB3", [128, 4096], BF16)
        self.G1R = sb("G1R", [128, 4096], BF16)
        self.CB = sb("CB", [128, 4096], BF16)
        self.IDT = sb("IDT", [128, 128], BF16)
        self.BF0 = sb("BF0", [128, 6144], F32)
        self.BF1 = sb("BF1", [128, 2048], F32)
        self.BF2 = sb("BF2", [128, 2048], F32)
        self.BF3 = sb("BF3", [128, 2048], F32)
        self.SM = sb("SM", [128, 1024], F32)
        self.ITMP = sb("ITMP", [128, 256], I32)
        self.XB = nc.virtual_sbuf_tensor("XB", [128, 4096], F32)
        nc._tensor(self.XB.name, self.XB.shape, self.XB.dtype, type="SB")
        mx = nc.lookup_mloc(self.XB)
        mx.allocated = True
        mx.addr = nc.lookup_mloc(self.BB0).addr + 49152
        self.ps_f = [self.stack.enter_context(
            nc.psum_tensor(f"psf{i}", [128, 512], F32)) for i in range(6)]
        self.ps_b = self.stack.enter_context(
            nc.psum_tensor("psb", [128, 2048], BF16))
        self.bar = self.stack.enter_context(nc.semaphore("bar"))
        self._sems = {}

    def sem(self, name):
        if name not in self._sems:
            self._sems[name] = self.stack.enter_context(self.nc.semaphore(name))
        return self._sems[name]

    @property
    def xb(self):
        return ((self.nchunks + 1) // 2, self.nchunks // 2)

    def build(self):
        nc = self.nc
        reg = {"setup": PhSetup, "deq": PhDeq, "A": PhA, "AGstate": PhAGState,
               "B": PhB, "C": PhC}
        self.ph_objs = [reg[p](self) for p in self.phases]
        groups = []
        i = 0
        while i < len(self.phases):
            if (self.phases[i] == "setup" and i + 1 < len(self.phases)
                    and self.phases[i + 1] == "deq"):
                groups.append([i, i + 1])
                i += 2
            else:
                groups.append([i])
                i += 1

        def drive(engname):
            def cb(eng):
                for gi, idxs in enumerate(groups):
                    eng.wait_ge(self.bar, 5 * gi)
                    for ix in idxs:
                        getattr(self.ph_objs[ix], engname)(eng)
                    eng.sem_inc(self.bar, 1)
            return cb

        with nc.Block() as block:
            block.sync(drive("sync"))
            block.tensor(drive("tensor"))
            block.vector(drive("vector"))
            block.scalar(drive("scalar"))
            block.gpsimd(drive("gpsimd"))
        return nc


class Phase:
    def __init__(self, mk: MK):
        self.mk = mk
        self.nc = mk.nc
        self.plan()

    def plan(self): pass
    def sync(self, eng): pass
    def tensor(self, eng): pass
    def vector(self, eng): pass
    def scalar(self, eng): pass
    def gpsimd(self, eng): pass


class PhSetup(Phase):

    def plan(self):
        mk = self.mk
        self.sl = mk.sem("su_l")
        self.sg_ = mk.sem("su_g")
        self.sc_ = mk.sem("su_c")

    def scalar(self, eng):
        mk, i = self.mk, self.mk.inp
        SM = mk.SM
        eng.dma_start(out=SM[0:1, SM_ROW:SM_ROW + 16],
                      in_=i["log_decay"].rearrange("h o -> o h")).then_inc(self.sl, 16)
        eng.dma_start(out=SM[0:1, SM_ROW + 16:SM_ROW + 144],
                      in_=i["ones1"][:, :]).then_inc(self.sl, 16)
        eng.dma_start(out=SM[0:2, SM_ROW + 144:SM_ROW + 272],
                      in_=i["ebase2"][:, :]).then_inc(self.sl, 16)
        eng.dma_start(out=SM[0:1, SM_ROW + 272:SM_ROW + 280],
                      in_=i["mask8"][:, :]).then_inc(self.sl, 16)
        eng.dma_start(out=SM[:, SM_P127:SM_P127 + 1],
                      in_=i["pos_127"][:, :]).then_inc(self.sl, 16)
        eng.dma_start(out=SM[:, SM_ONEC:SM_ONEC + 1],
                      in_=i["onesc"][:, :]).then_inc(self.sl, 16)
        eng.dma_start(out=SM[:, SM_NW:SM_NW + 16].rearrange("p (c o) -> p c o", c=NCHK),
                      in_=i["norm_w"].rearrange("c p -> p c").unsqueeze(2),
                      allow_slow_non_contiguous=True).then_inc(self.sl, 16)
        eng.dma_start(out=SM[:, SM_HO:SM_HO + 256],
                      in_=i["headones"][:, :]).then_inc(self.sl, 16)
        eng.dma_start(out=mk.BF1[:, 0:128], in_=i["diffT"][:, :]).then_inc(self.sl, 16)
        eng.dma_start(out=mk.IDT[:, :], in_=i["ident"][:, :]).then_inc(self.sl, 16)
        eng.wait_ge(self.sc_, 2)
        eng.activation(SM[:, SM_CD128:SM_CD128 + 16], SM[:, SM_LDB:SM_LDB + 16],
                       AF.Exp, scale=128.0)
        eng.activation(SM[:, SM_CDSUP:SM_CDSUP + 16], SM[:, SM_LDB:SM_LDB + 16],
                       AF.Exp, scale=float(128 * mk.cps))
        eng.activation(SM[:, SM_KDP:SM_KDP + 16], SM[:, SM_KARG:SM_KARG + 16], AF.Exp)
        for j in range(mk.cps):
            eng.activation(SM[:, SM_KDCD + 16 * j:SM_KDCD + 16 * (j + 1)],
                           SM[:, SM_ARG + 16 * j:SM_ARG + 16 * (j + 1)], AF.Exp)
        eng.wait_ge(self.sc_, 3)
        for h in range(H):
            eng.activation(mk.CB[:, 2048 + 128 * h:2048 + 128 * (h + 1)],
                           mk.BF2[:, 128 * h:128 * (h + 1)], AF.Exp)
        eng.drain()
        eng.sem_inc(self.sc_, 1)

    def tensor(self, eng):
        mk = self.mk
        SM = mk.SM
        eng.wait_ge(self.sl, 16 * 10)
        eng.matmul(mk.ps_f[0][:, 0:16], SM[0:1, SM_ROW + 16:SM_ROW + 144],
                   SM[0:1, SM_ROW:SM_ROW + 16], start=True, stop=False)
        mm = eng.matmul(mk.ps_f[0][:, 16:24], SM[0:1, SM_ROW + 16:SM_ROW + 144],
                        SM[0:1, SM_ROW + 272:SM_ROW + 280], start=False, stop=True)
        mm.then_inc(self.sc_, 1)

    def vector(self, eng):
        mk = self.mk
        SM = mk.SM
        eng.wait_ge(self.sc_, 1)
        eng.tensor_copy(SM[:, SM_LDB:SM_LDB + 16], mk.ps_f[0][:, 0:16])
        eng.tensor_copy(SM[:, SM_M8B:SM_M8B + 8], mk.ps_f[0][:, 16:24])
        eng.drain()
        eng.tensor_scalar_mul(SM[:, SM_KARG:SM_KARG + 16], SM[:, SM_LDB:SM_LDB + 16],
                              SM[:, SM_P127:SM_P127 + 1])
        eng.drain()
        for j in range(mk.cps):
            eng.tensor_scalar_mul(SM[:, SM_ARG + 16 * j:SM_ARG + 16 * (j + 1)],
                                  SM[:, SM_LDB:SM_LDB + 16],
                                  float(128 * (mk.cps - 1 - j)))
        eng.drain()
        for j in range(mk.cps):
            eng.tensor_add(SM[:, SM_ARG + 16 * j:SM_ARG + 16 * (j + 1)],
                           SM[:, SM_ARG + 16 * j:SM_ARG + 16 * (j + 1)],
                           SM[:, SM_KARG:SM_KARG + 16])
        eng.drain()
        eng.sem_inc(self.sc_, 1)
        for h in range(H):
            eng.tensor_scalar_mul(mk.BF2[:, 128 * h:128 * (h + 1)],
                                  mk.BF1[:, 0:128],
                                  SM[:, SM_LDB + h:SM_LDB + h + 1])
        eng.drain()
        eng.sem_inc(self.sc_, 1)

    def gpsimd(self, eng):
        mk, i = self.mk, self.mk.inp
        eng.collective_compute(
            "AllGather", ALU.bypass, replica_groups=[list(range(mk.n_cores))],
            ins=[mk.dum_in], outs=[mk.dum_out]).then_inc(self.sg_, 1)
        if "deq" in mk.phases:
            eng.wait_ge(mk.sem("d_bnc"), 16)
            eng.collective_compute(
                "AllGather", ALU.bypass,
                replica_groups=[list(range(mk.n_cores))],
                ins=[mk.mean_ag_in], outs=[mk.mean_ag_out]
                ).then_inc(mk.sem("d_agm"), 1)
        eng.dma_start(out=mk.CB[:, 0:2048].rearrange("p (c g) -> p c g", c=NCHK),
                      in_=i["Wg1"].rearrange("(c p) g -> p c g", p=128)
                      ).then_inc(self.sg_, 16)
        for name, dst in [("Wq", mk.wq_bf), ("Wk", mk.wk_bf), ("Wv", mk.wv_bf),
                          ("Wo", mk.wo_bf)]:
            eng.dma_start(out=dst[:, :, :],
                          in_=i[name].rearrange("(c p) e -> c p e", p=128)
                          ).then_inc(self.sg_, 16)
        eng.dma_start(out=mk.wg2d[:, :], in_=i["Wg2"][:, :]).then_inc(self.sg_, 16)
        eng.wait_ge(self.sg_, 16 * 6)


class PhA(Phase):

    CUT = 9
    CUT0 = 9

    def plan(self):
        mk = self.mk
        for n in ["x0", "x1", "a_ln", "tr", "trev", "a_mm", "a_ev2",
                  "sp0", "sp1", "a_spk0", "a_spk1", "a_spv0", "a_spv1",
                  "a_kvk0", "a_kvk1", "a_kvv0", "a_kvv1",
                  "a_kt", "a_st", "a_g1t", "a_g1s", "a_mrg",
                  "st_sq", "st_var", "st_sqr", "st_stat", "ag_a_d", "ag_a",
                  "a_w0", "a_w1", "a_w2", "a_w3"]:
            mk.sem(n)
        self.h2 = mk.sg_tok // 512
        self.nsh = mk.sg_tok // 512
        self.gq = NCHK * self.nsh
        self.gkv = mk.cps * 4
        self.gsg = self.gq + 2 * self.gkv
        self.sp_sg = self.gq + 2 * mk.cps

    def spill_done_wait(self, eng, sg):
        S = self.mk.sem
        for par in range(2):
            eng.wait_ge(S(f"sp{par}"), 16 * ((sg + 1) * self.gq // 2))
            eng.wait_ge(S(f"a_spk{par}"), 16 * ((sg + 1) * self.mk.cps // 2))
            eng.wait_ge(S(f"a_spv{par}"), 16 * ((sg + 1) * self.mk.cps // 2))

    def sync(self, eng):
        mk = self.mk
        S = mk.sem
        for sg in range(mk.nsg):
            for ci in range(mk.cps):
                gci = sg * mk.cps + ci
                if gci >= 2:
                    eng.wait_ge(S("a_ln"), gci - 1)
                eng.dma_start(out=mk.BF0[:, 2048 * (gci % 2):2048 * (gci % 2) + 2048],
                              in_=mk.inp["x"][128 * gci:128 * (gci + 1), :]
                              ).then_inc(S(f"x{gci % 2}"), 16)
            for wi, wsrc in enumerate([mk.wq_bf, mk.wk_bf, mk.wv_bf][:max(0, self.CUT - 2)]):
                eng.wait_ge(S("a_mm"), sg * self.gsg
                            + (0 if wi == 0 else self.gq + (wi - 1) * self.gkv))
                for dg in range(4):
                    eng.dma_start(out=mk.BB0[:, 8192 * dg:8192 * (dg + 1)]
                                  .rearrange("p (c e) -> p c e", c=4),
                                  in_=wsrc[4 * dg:4 * (dg + 1), :, :]
                                  .rearrange("c p e -> p c e")
                                  ).then_inc(S(f"a_w{dg}"), 16)
            if self.CUT < 6:
                continue
            self.spill_done_wait(eng, sg)
            if sg > 0:
                eng.wait_ge(S("a_st"), sg * mk.cps)
            for ci in range(mk.cps):
                if ci >= 2:
                    eng.wait_ge(S("a_st"), sg * mk.cps + ci - 1)
                gci = sg * mk.cps + ci
                n = sg * (mk.cps // 2) + ci // 2 + 1
                eng.dma_start(
                    out=mk.BB2[:, 4096 * (ci % 2):4096 * (ci % 2) + 2048],
                    in_=mk.k_sp[128 * gci:128 * (gci + 1), :]
                    ).then_inc(S(f"a_kvk{ci % 2}"), 16)
                eng.dma_start(
                    out=mk.BB2[:, 4096 * (ci % 2) + 2048:4096 * (ci % 2) + 4096],
                    in_=mk.v_sp[128 * gci:128 * (gci + 1), :]
                    ).then_inc(S(f"a_kvv{ci % 2}"), 16)

    def scalar(self, eng):
        mk = self.mk
        SM = mk.SM
        S = mk.sem
        for sg in range(mk.nsg):
            for ci in range(mk.cps):
                gci = sg * mk.cps + ci
                st = SM_ST + 8 * (gci % 2)
                xs = mk.BF0[:, 2048 * (gci % 2):2048 * (gci % 2) + 2048]
                eng.wait_ge(S(f"x{gci % 2}"), 16 * (gci // 2 + 1))
                if self.CUT0 >= 2:
                    eng.activation(mk.BF0[:, 4096:6144], xs, AF.Square,
                                   accum_out=SM[:, st + 1:st + 2])
                    eng.drain()
                eng.sem_inc(S("st_sq"), 1)
                eng.wait_ge(S("st_var"), gci + 1)
                if self.CUT0 >= 2:
                    eng.activation(SM[:, st + 5:st + 6], SM[:, st + 3:st + 4],
                                   AF.Sqrt)
                    eng.drain()
                eng.sem_inc(S("st_sqr"), 1)
                eng.wait_ge(S("st_stat"), gci + 1)
                if ci == 0 and sg > 0:
                    eng.wait_ge(S("a_st"), sg * mk.cps)
                if gci >= 1:
                    eng.wait_ge(S("tr"), 4 * gci)
                if self.CUT0 >= 3:
                    eng.activation(mk.BB3[:, 0:2048], xs, AF.Identity,
                                   bias=SM[:, st + 4:st + 5],
                                   scale=SM[:, st + 6:st + 7])
                    eng.drain()
                eng.sem_inc(S("a_ln"), 1)
            for ns in range(self.h2 if self.CUT >= 2 else 0):
                eng.wait_ge(S("a_g1t"), sg * self.h2 + ns + 1)
                eng.activation(mk.G1R[:, sg * mk.sg_tok + 512 * ns:
                                      sg * mk.sg_tok + 512 * ns + 512],
                               mk.ps_f[4 + ns % 2][:, :], AF.Identity)
                eng.drain()
                eng.sem_inc(S("a_g1s"), 1)
            for g in range(self.gq if self.CUT >= 3 else 0):
                ec, ns = divmod(g, self.nsh)
                eng.wait_ge(S("a_mm"), sg * self.gsg + g + 1)
                if g >= 2:
                    eng.wait_ge(S(f"sp{g % 2}"),
                                16 * (self.gq // 2 * sg + (g - 2) // 2 + 1))
                slot = mk.BB3[:, 512 * (g % 2):512 * (g % 2) + 512]
                eng.activation(slot, mk.ps_f[4 + g % 2][:, :], AF.Silu)
                eng.drain()
                eng.sem_inc(S("a_ev2"), 1)
                eng.dma_start(out=mk.q_sp[ec, :, sg * mk.sg_tok + 512 * ns:
                                          sg * mk.sg_tok + 512 * ns + 512],
                              in_=slot).then_inc(S(f"sp{g % 2}"), 16)
            for pi, (sp, tag) in enumerate(
                    [(mk.k_sp, "k"), (mk.v_sp, "v")][:max(0, self.CUT - 3)]):
                base = self.gq + pi * self.gkv
                for g in range(self.gkv):
                    tc, es = divmod(g, 4)
                    eng.wait_ge(S("a_mm"), sg * self.gsg + base + g + 1)
                    if g == 0:
                        if pi == 0:
                            for par in range(2):
                                eng.wait_ge(S(f"sp{par}"),
                                            16 * ((sg + 1) * self.gq // 2))
                        else:
                            for par in range(2):
                                eng.wait_ge(S(f"a_spk{par}"),
                                            16 * ((sg + 1) * mk.cps // 2))
                    if tc >= 2 and es == 0:
                        eng.wait_ge(S(f"a_sp{tag}{tc % 2}"),
                                    16 * (mk.cps // 2 * sg + (tc - 2) // 2 + 1))
                    slot = mk.BB3[:, 2048 * (tc % 2):2048 * (tc % 2) + 2048]
                    eng.activation(slot[:, 512 * es:512 * es + 512],
                                   mk.ps_f[4 + g % 2][:, :], AF.Silu)
                    eng.drain()
                    eng.sem_inc(S("a_ev2"), 1)
                    if es == 3:
                        gci = sg * mk.cps + tc
                        eng.dma_start(out=sp[128 * gci:128 * (gci + 1), :],
                                      in_=slot).then_inc(S(f"a_sp{tag}{tc % 2}"), 16)

    def vector(self, eng):
        mk = self.mk
        SM = mk.SM
        S = mk.sem
        for sg in range(mk.nsg):
            for ci in range(mk.cps):
                gci = sg * mk.cps + ci
                st = SM_ST + 8 * (gci % 2)
                xs = mk.BF0[:, 2048 * (gci % 2):2048 * (gci % 2) + 2048]
                eng.wait_ge(S(f"x{gci % 2}"), 16 * (gci // 2 + 1))
                if gci >= 2:
                    eng.wait_ge(S("a_ln"), gci - 1)
                if self.CUT0 >= 2:
                    eng.reduce_sum(SM[:, st:st + 1], xs, AX.X)
                eng.wait_ge(S("st_sq"), gci + 1)
                eng.drain()
                if self.CUT0 >= 2:
                    eng.tensor_scalar_mul(SM[:, st + 2:st + 3], SM[:, st:st + 1],
                                          1.0 / D)
                    eng.tensor_scalar_mul(SM[:, st + 3:st + 4],
                                          SM[:, st + 1:st + 2], 1.0 / D)
                    eng.drain()
                    eng.tensor_mul(SM[:, st + 7:st + 8], SM[:, st + 2:st + 3],
                                   SM[:, st + 2:st + 3])
                    eng.drain()
                    eng.tensor_sub(SM[:, st + 3:st + 4], SM[:, st + 3:st + 4],
                                   SM[:, st + 7:st + 8])
                    eng.drain()
                    eng.tensor_scalar_add(SM[:, st + 3:st + 4],
                                          SM[:, st + 3:st + 4], EPS)
                    eng.drain()
                eng.sem_inc(S("st_var"), 1)
                eng.wait_ge(S("st_sqr"), gci + 1)
                if self.CUT0 >= 2:
                    eng.reciprocal(SM[:, st + 6:st + 7], SM[:, st + 5:st + 6])
                    eng.drain()
                    eng.tensor_scalar(SM[:, st + 4:st + 5], SM[:, st + 2:st + 3],
                                      SM[:, st + 6:st + 7], -1.0,
                                      ALU.mult, ALU.mult)
                    eng.drain()
                eng.sem_inc(S("st_stat"), 1)
                eng.wait_ge(S("tr"), gci * 4 + 4)
                if self.CUT0 >= 5:
                    for dc in range(NCHK):
                        eng.tensor_copy(
                            mk.BB1[:, 1024 * dc + 128 * ci:
                                   1024 * dc + 128 * ci + 128],
                            mk.ps_b[:, 128 * dc:128 * dc + 128])
                    eng.drain()
                eng.sem_inc(S("trev"), 4)
            for ci in range(mk.cps if self.CUT >= 6 else 0):
                n = sg * (mk.cps // 2) + ci // 2 + 1
                eng.wait_ge(S(f"a_kvk{ci % 2}"), 16 * n)
                if ci >= 1:
                    eng.wait_ge(S("a_st"), sg * mk.cps + ci)
                kv = mk.BB2[:, 4096 * (ci % 2):4096 * (ci % 2) + 2048]
                kd = SM[:, SM_KDCD + 16 * ci:SM_KDCD + 16 * (ci + 1)]
                eng.tensor_tensor(
                    mk.BB3[:, 0:2048].rearrange("p (h e) -> p h e", h=H),
                    kv.rearrange("p (h e) -> p h e", h=H),
                    kd.unsqueeze(2).broadcast_to([128, H, HD]),
                    ALU.mult)
                eng.drain()
                eng.sem_inc(S("a_kt"), 1)
            if self.CUT < 6:
                continue
            eng.wait_ge(S("a_st"), (sg + 1) * mk.cps)
            if sg == 0:
                eng.memset(mk.BF1[:, :], 0.0)
                eng.drain()
            eng.tensor_tensor(
                mk.BF1[:, :].rearrange("p (h e) -> p h e", h=H),
                mk.BF1[:, :].rearrange("p (h e) -> p h e", h=H),
                SM[:, SM_CDSUP:SM_CDSUP + 16].unsqueeze(2)
                .broadcast_to([128, H, HD]), ALU.mult)
            eng.drain()
            for b in range(4):
                eng.tensor_add(mk.BF1[:, 512 * b:512 * b + 512],
                               mk.BF1[:, 512 * b:512 * b + 512], mk.ps_f[b][:, :])
            eng.drain()
            eng.sem_inc(S("a_mrg"), 1)

    def tensor(self, eng):
        mk = self.mk
        S = mk.sem
        for sg in range(mk.nsg):
            for ci in range(mk.cps):
                gci = sg * mk.cps + ci
                eng.wait_ge(S("a_ln"), gci + 1)
                for s in range(4):
                    b = (0, 2, 1, 3)[s]
                    if gci * 4 + s >= 4:
                        eng.wait_ge(S("trev"), gci * 4 + s - 3)
                    for k in range(4):
                        dc = b * 4 + k
                        mm = eng.transpose(
                            mk.ps_b[:, 512 * b + 128 * k:512 * b + 128 * k + 128],
                            mk.BB3[:, 128 * dc:128 * dc + 128], mk.IDT[:, :])
                    mm.then_inc(S("tr"), 1)
            eng.wait_ge(S("trev"), ((sg + 1) * mk.cps) * 4)
            if sg > 0 and self.CUT >= 3:
                eng.wait_ge(S("a_ev2"), sg * self.gsg)
            for ns in range(self.h2 if self.CUT >= 2 else 0):
                for dc in range(NCHK):
                    mm = eng.matmul(mk.ps_f[4 + ns % 2][:, :],
                               mk.CB[:, 128 * dc:128 * dc + 128],
                               mk.BB1[:, 1024 * dc + 512 * ns:
                                      1024 * dc + 512 * ns + 512],
                               start=(dc == 0), stop=(dc == NCHK - 1))
                mm.then_inc(S("a_g1t"), 1)
            eng.wait_ge(S("a_g1s"), (sg + 1) * self.h2 if self.CUT >= 2 else 0)
            for g in range(self.gq if self.CUT >= 3 else 0):
                ec, ns = divmod(g, self.nsh)
                if sg * self.gsg + g - 1 > 0:
                    eng.wait_ge(S("a_ev2"), sg * self.gsg + g - 1)
                for dc in range(NCHK):
                    if g == 0:
                        eng.wait_ge(S(f"a_w{dc // 4}"), 16 * (3 * sg + 1))
                    mm = eng.matmul(mk.ps_f[4 + g % 2][:, :],
                               mk.BB0[:, 2048 * dc + 128 * ec:
                                      2048 * dc + 128 * ec + 128],
                               mk.BB1[:, 1024 * dc + 512 * ns:
                                      1024 * dc + 512 * ns + 512],
                               start=(dc == 0), stop=(dc == NCHK - 1))
                mm.then_inc(S("a_mm"), 1)
            for g in range(self.gkv if self.CUT >= 4 else 0):
                tc, es = divmod(g, 4)
                eng.wait_ge(S("a_ev2"), sg * self.gsg + self.gq + g - 1)
                for dc in range(NCHK):
                    if g == 0:
                        eng.wait_ge(S(f"a_w{dc // 4}"), 16 * (3 * sg + 2))
                    mm = eng.matmul(mk.ps_f[4 + g % 2][:, :],
                               mk.BB1[:, 1024 * dc + 128 * tc:
                                      1024 * dc + 128 * tc + 128],
                               mk.BB0[:, 2048 * dc + 512 * es:
                                      2048 * dc + 512 * es + 512],
                               start=(dc == 0), stop=(dc == NCHK - 1))
                mm.then_inc(S("a_mm"), 1)
            for g in range(self.gkv if self.CUT >= 5 else 0):
                tc, es = divmod(g, 4)
                eng.wait_ge(S("a_ev2"),
                            sg * self.gsg + self.gq + self.gkv + g - 1)
                for dc in range(NCHK):
                    if g == 0:
                        eng.wait_ge(S(f"a_w{dc // 4}"), 16 * (3 * sg + 3))
                    mm = eng.matmul(mk.ps_f[4 + g % 2][:, :],
                               mk.BB1[:, 1024 * dc + 128 * tc:
                                      1024 * dc + 128 * tc + 128],
                               mk.BB0[:, 2048 * dc + 512 * es:
                                      2048 * dc + 512 * es + 512],
                               start=(dc == 0), stop=(dc == NCHK - 1))
                mm.then_inc(S("a_mm"), 1)
            if sg > 0 and self.CUT >= 6:
                eng.wait_ge(S("a_mrg"), sg)
            for ci in range(mk.cps if self.CUT >= 6 else 0):
                eng.wait_ge(S("a_kt"), sg * mk.cps + ci + 1)
                eng.wait_ge(S(f"a_kvv{ci % 2}"),
                            16 * (sg * (mk.cps // 2) + ci // 2 + 1))
                kt = mk.BB3[:, 0:2048]
                vv = mk.BB2[:, 4096 * (ci % 2) + 2048:4096 * (ci % 2) + 4096]
                for h in range(H):
                    mm = eng.matmul(
                        mk.ps_f[h // 4][:, 128 * (h % 4):128 * (h % 4) + 128],
                        kt[:, 128 * h:128 * h + 128],
                        vv[:, 128 * h:128 * h + 128],
                        start=(ci == 0 and h % 4 == 0),
                        stop=(ci == mk.cps - 1 and h % 4 == 3))
                mm.then_inc(S("a_st"), 1)

    def gpsimd(self, eng):
        if self.CUT < 7:
            return
        mk = self.mk
        S = mk.sem
        eng.wait_ge(S("a_mrg"), mk.nsg)
        eng.dma_start(out=mk.ag_a_in[:, :], in_=mk.BF1[:, :]).then_inc(S("ag_a_d"), 16)
        eng.wait_ge(S("ag_a_d"), 16)
        eng.collective_compute(
            "AllGather", ALU.bypass,
            replica_groups=[list(range(mk.n_cores))],
            ins=[mk.ag_a_in], outs=[mk.ag_a_out],
        ).then_inc(S("ag_a"), 1)


class PhAGState(Phase):

    def plan(self):
        for n in ["ag_a", "ags_l", "ags_v"]:
            self.mk.sem(n)

    def gpsimd(self, eng):
        mk = self.mk
        S = mk.sem
        eng.wait_ge(S("ag_a"), 1)
        for r in range(mk.n_cores):
            eng.wait_ge(S("ags_v"), r)
            eng.dma_start(out=mk.BF0[:, 0:2048], in_=mk.ag_a_out[r, :, :]
                          ).then_inc(S("ags_l"), 16)
        eng.wait_ge(S("ags_v"), mk.n_cores)

    def vector(self, eng):
        mk = self.mk
        S = mk.sem
        eng.memset(mk.BF1[:, :], 0.0)
        eng.drain()
        eng.sem_inc(S("ags_v"), 0) if False else None
        for r in range(mk.n_cores):
            eng.wait_ge(S("ags_l"), 16 * (r + 1))
            eng.scalar_tensor_tensor(mk.BF1[:, :], mk.BF0[:, 0:2048],
                                     mk.SM[:, SM_M8B + r:SM_M8B + r + 1],
                                     mk.BF1[:, :], ALU.mult, ALU.add)
            eng.drain()
            eng.sem_inc(S("ags_v"), 1)


class PhDeq(Phase):

    def plan(self):
        mk = self.mk
        for n in ["d_s1", "d_part", "d_agm", "d_mld",
                  "d_mn", "d_bnc", "d_arl", "d_art", "d_arv",
                  "d_wt", "d_stg", "d_exp", "d_eev", "d_fin", "d_out",
                  "agw", "c_wready", "su_l"]:
            mk.sem(n)
        self.NB = 180224

    def sync(self, eng):
        mk = self.mk
        S = mk.sem
        i = mk.inp
        for wi, name in enumerate(["w1s", "w2s", "w3s"]):
            if wi > 0:
                eng.wait_ge(S("d_s1"), wi)
            sh = i[name]
            flat = sh.rearrange("a b -> (a b)").rearrange("(s q) -> s q", q=16384)
            eng.dma_start(out=mk.BB1[0:88, :], in_=flat).then_inc(S("d_wt"), 16)
        eng.wait_ge(S("d_bnc"), 16 * 3)
        for a in range(2):
            for t in range(2):
                gi = 2 * a + t
                if gi >= 1:
                    eng.wait_ge(S("d_art"), gi)
                eng.dma_start(
                    out=mk.BF0[0:128, 4000 + 88 * t:4000 + 88 * (t + 1)],
                    in_=mk.sflat[a, :].rearrange("(x y) -> x y", y=88)
                    [128 * t:128 * (t + 1), :]).then_inc(S("d_arl"), 16)
        for fc in range(FCHK):
            if fc > 1:
                eng.wait_ge(S("d_fin"), fc - 1)
            eng.dma_start(out=mk.BB3[:, 2816 * (fc % 2):2816 * (fc % 2) + 256],
                          in_=i["w3ts"][128 * fc:128 * (fc + 1), :]
                          ).then_inc(S("d_wt"), 16)
        for wi, name in enumerate(["w1ts", "w2ts"]):
            eng.wait_ge(S("d_bnc"), 16 * (5 + 2 * wi))
            for a in range(2):
                for t in range(6):
                    gi = 4 + wi * 12 + a * 6 + t
                    rows = 128 if t < 5 else 64
                    eng.wait_ge(S("d_art"), gi)
                    eng.dma_start(
                        out=mk.BF0[0:rows, 4000 + 32 * (t % 2):
                                   4000 + 32 * (t % 2) + 32],
                        in_=mk.sflat[a, :].rearrange("(x y) -> x y", y=32)
                        [128 * t:128 * t + rows, :]).then_inc(S("d_arl"), 16)
            for dc in range(NCHK):
                ti = FCHK + wi * NCHK + dc
                eng.wait_ge(S("d_fin"), ti - 1)
                eng.dma_start(out=mk.BB3[:, 2816 * (ti % 2):
                                         2816 * (ti % 2) + 704],
                              in_=i[name][128 * dc:128 * (dc + 1), :]
                              ).then_inc(S("d_wt"), 16)

    def tensor(self, eng):
        mk = self.mk
        S = mk.sem
        SM = mk.SM
        for wi in range(3):
            eng.wait_ge(S("d_s1"), wi + 1)
            mm = eng.matmul(mk.ps_f[0][0:1, wi:wi + 1],
                            SM[0:88, 940 + wi:941 + wi],
                            SM[0:88, SM_ONEC:SM_ONEC + 1],
                            start=(wi == 0), stop=(wi == 2))
        mm.then_inc(S("d_part"), 1)
        eng.wait_ge(S("d_mld"), 16)
        for wi in range(3):
            mm = eng.matmul(mk.ps_f[1][0:1, wi:wi + 1],
                            SM[0:8, 924 + wi:925 + wi],
                            SM[0:8, SM_ONEC:SM_ONEC + 1],
                            start=(wi == 0), stop=(wi == 2))
        mm.then_inc(S("d_mn"), 1)
        eng.wait_ge(S("d_mn"), 2)
        mm = eng.matmul(mk.ps_f[1][:, 8:11], SM[0:1, SM_ROW + 16:SM_ROW + 144],
                        SM[0:1, 928:931], start=True, stop=True)
        mm.then_inc(S("d_mn"), 1)
        eng.wait_ge(S("su_l"), 16 * 11)
        for a in range(2):
            for t in range(2):
                gi = 2 * a + t
                eng.wait_ge(S("d_arl"), 16 * (gi + 1))
                mm = eng.transpose(mk.ps_f[2][0:88, 0:128],
                                   mk.BF0[0:128, 4000 + 88 * t:4000 + 88 * (t + 1)],
                                   SM[0:128, 472:600])
                mm.then_inc(S("d_art"), 1)
                eng.wait_ge(S("d_arv"), gi + 1)
        for fc in range(FCHK):
            eng.wait_ge(S("d_stg"), 32 * (fc + 1))
            eng.wait_ge(S("d_eev"), 2 * fc)
            mm = eng.matmul(mk.ps_f[4 + fc % 2][:, :],
                            SM[0:2, SM_ROW + 144:SM_ROW + 272],
                            mk.BF0[0:2, 1408:1920],
                            start=True, stop=True)
            mm.then_inc(S("d_exp"), 1)
        for wi in range(2):
            for a in range(2):
                for t in range(6):
                    gi = 4 + wi * 12 + a * 6 + t
                    rows = 128 if t < 5 else 64
                    eng.wait_ge(S("d_arl"), 16 * (gi + 1))
                    mm = eng.transpose(
                        mk.ps_f[2][0:32, 0:rows],
                        mk.BF0[0:rows, 4000 + 32 * (t % 2):
                               4000 + 32 * (t % 2) + 32],
                        SM[0:rows, 472:472 + rows])
                    mm.then_inc(S("d_art"), 1)
                    eng.wait_ge(S("d_arv"), gi + 1)
            for dc in range(NCHK):
                ti = FCHK + wi * NCHK + dc
                eng.wait_ge(S("d_stg"), 32 * (ti + 1))
                for rnd in range(2):
                    eng.wait_ge(S("d_eev"), 2 * ti + rnd)
                    off = 1408 + 704 * rnd
                    eng.matmul(mk.ps_f[4][:, 0:512],
                               SM[0:2, SM_ROW + 144:SM_ROW + 272],
                               mk.BF0[0:2, off:off + 512],
                               start=True, stop=True)
                    mm = eng.matmul(mk.ps_f[5][:, 0:192],
                                    SM[0:2, SM_ROW + 144:SM_ROW + 272],
                                    mk.BF0[0:2, off + 512:off + 704],
                                    start=True, stop=True)
                    mm.then_inc(S("d_exp"), 1)

    def vector(self, eng):
        mk = self.mk
        S = mk.sem
        SM = mk.SM
        BF1, BF2 = mk.BF1, mk.BF2
        for wi in range(3):
            eng.wait_ge(S("d_wt"), 16 * (wi + 1))
            eng.tensor_reduce(BF1[0:88, 256 * wi:256 * (wi + 1)],
                              mk.BB1[0:88, :].rearrange("s (b e) -> s b e", e=64),
                              AX.X, ALU.max, apply_absolute_value=True)
            eng.drain()
            eng.reduce_sum(SM[0:88, 940 + wi:941 + wi],
                           BF1[0:88, 256 * wi:256 * (wi + 1)], AX.X)
            eng.drain()
            eng.sem_inc(S("d_s1"), 1)
        eng.wait_ge(S("d_part"), 1)
        eng.memset(SM[0:1, 920:924], 0.0)
        eng.drain()
        eng.tensor_copy(SM[0:1, 920:923], mk.ps_f[0][0:1, 0:3])
        eng.drain()
        eng.sem_inc(S("d_part"), 1)
        eng.wait_ge(S("d_mn"), 1)
        eng.tensor_scalar_mul(SM[0:1, 928:931], mk.ps_f[1][0:1, 0:3],
                              1.0 / self.NB)
        eng.drain()
        eng.sem_inc(S("d_mn"), 1)
        eng.wait_ge(S("d_mn"), 3)
        eng.tensor_copy(SM[:, 936:939], mk.ps_f[1][:, 8:11])
        eng.drain()
        eng.sem_inc(S("d_mn"), 1)
        eng.wait_ge(S("su_c"), 4)
        for wi in range(3):
            s1 = BF1[0:88, 256 * wi:256 * (wi + 1)]
            eng.tensor_scalar(BF2[0:88, 0:256], s1, SM[0:88, 936 + wi:937 + wi],
                              None, ALU.subtract)
            eng.drain()
            eng.tensor_reduce(SM[0:88, 944:945], BF2[0:88, 0:256], AX.X,
                              ALU.max, apply_absolute_value=True)
            eng.drain()
            eng.reciprocal(SM[0:88, 945:946], SM[0:88, 944:945])
            eng.drain()
            eng.tensor_scalar_mul(SM[0:88, 946:947], SM[0:88, 945:946], 128.0)
            eng.drain()
            eng.reciprocal(SM[0:88, 947:948], SM[0:88, 946:947])
            eng.drain()
            eng.tensor_scalar_mul(BF2[0:88, 256:512], BF2[0:88, 0:256],
                                  SM[0:88, 946:947])
            eng.drain()
            eng.tensor_copy(mk.ITMP[0:88, :], BF2[0:88, 256:512])
            eng.drain()
            eng.tensor_copy(BF2[0:88, 256:512], mk.ITMP[0:88, :])
            eng.drain()
            eng.tensor_scalar_min(BF2[0:88, 256:512], BF2[0:88, 256:512], 127.0)
            eng.drain()
            eng.tensor_scalar_max(BF2[0:88, 256:512], BF2[0:88, 256:512], -128.0)
            eng.drain()
            eng.tensor_scalar_mul(mk.BB3[0:88, 3584:3840], BF2[0:88, 256:512],
                                  SM[0:88, 947:948])
            eng.drain()
            eng.tensor_scalar(BF2[0:88, 512 + 256 * wi:768 + 256 * wi],
                              mk.BB3[0:88, 3584:3840],
                              SM[0:88, 936 + wi:937 + wi], None, ALU.add)
            eng.drain()
            eng.reciprocal(BF2[0:88, 1280 + 256 * wi:1536 + 256 * wi], s1)
            eng.drain()
            eng.sem_inc(S("d_s1"), 1)
        for a in range(2):
            for t in range(2):
                gi = 2 * a + t
                eng.wait_ge(S("d_art"), gi + 1)
                eng.tensor_copy(
                    BF1[0:88, 768 + 256 * a + 128 * t:768 + 256 * a + 128 * (t + 1)],
                    mk.ps_f[2][0:88, 0:128])
                eng.drain()
                eng.sem_inc(S("d_arv"), 1)
        for fc in range(FCHK):
            eng.wait_ge(S("d_exp"), fc + 1)
            eng.tensor_copy(mk.BF3[:, 0:512], mk.ps_f[4 + fc % 2][:, :])
            eng.drain()
            eng.sem_inc(S("d_eev"), 2)
            self.emit_tile(eng, fc, 256)
        for wi in range(2):
            for a in range(2):
                for t in range(6):
                    gi = 4 + wi * 12 + a * 6 + t
                    cols = 128 if t < 5 else 64
                    eng.wait_ge(S("d_art"), gi + 1)
                    eng.tensor_copy(
                        mk.BF0[0:32, 704 * a + 128 * t:704 * a + 128 * t + cols],
                        mk.ps_f[2][0:32, 0:cols])
                    eng.drain()
                    eng.sem_inc(S("d_arv"), 1)
            for dc in range(NCHK):
                ti = FCHK + wi * NCHK + dc
                for rnd in range(2):
                    eng.wait_ge(S("d_exp"), FCHK + wi * 2 * NCHK + 2 * dc
                                + rnd + 1)
                    eng.tensor_copy(mk.BF3[:, 704 * rnd:704 * rnd + 512],
                                    mk.ps_f[4][:, 0:512])
                    eng.tensor_copy(mk.BF3[:, 704 * rnd + 512:704 * rnd + 704],
                                    mk.ps_f[5][:, 0:192])
                    eng.drain()
                    eng.sem_inc(S("d_eev"), 1)
                self.emit_tile(eng, ti, 704)
        eng.wait_ge(S("d_out"), 16 * (FCHK + 2 * NCHK))

    def emit_tile(self, eng, ti, W):
        mk = self.mk
        S = mk.sem
        if True:
            eng.wait_ge(S("d_wt"), 16 * (ti + 4))
            if ti >= 1:
                eng.wait_ge(S("d_out"), 16 * ti)
            wt = mk.BB3[:, 2816 * (ti % 2):2816 * (ti % 2) + W]
            sc = mk.BB3[:, W:2 * W]
            acc = mk.BB3[:, 2 * W:3 * W]
            inds = (mk.BB3[:, 3 * W:4 * W], mk.G1R[:, 0:W])
            eng.tensor_tensor(sc, wt, mk.BF3[:, 0:W], ALU.mult)
            eng.drain()
            eng.tensor_scalar(acc, sc, NF4_MID[0], NF4_DELTA[0],
                              ALU.is_gt, ALU.mult)
            eng.tensor_scalar(inds[0], sc, NF4_MID[1], None, ALU.is_gt)
            eng.drain()
            for j in range(1, 15):
                if j < 14:
                    eng.tensor_scalar(inds[j % 2], sc, NF4_MID[j + 1],
                                      None, ALU.is_gt)
                eng.scalar_tensor_tensor(acc, inds[(j - 1) % 2],
                                         NF4_DELTA[j], acc,
                                         ALU.mult, ALU.add)
                eng.drain()
            eng.scalar_tensor_tensor(inds[0], acc, -1.0, mk.BF3[:, W:2 * W],
                                     ALU.add, ALU.mult)
            eng.drain()
            eng.sem_inc(S("d_fin"), 1)

    def scalar(self, eng):
        mk = self.mk
        S = mk.sem
        SM = mk.SM
        i = mk.inp
        eng.dma_start(out=SM[:, 472:600], in_=i["identf"][:, :]
                      ).then_inc(S("su_l"), 16)
        eng.wait_ge(S("d_part"), 2)
        eng.dma_start(out=mk.mean_ag_in[:, :], in_=SM[0:1, 920:924]
                      ).then_inc(S("d_bnc"), 16)
        eng.wait_ge(S("d_agm"), 1)
        eng.dma_start(out=mk.SM[0:8, 924:928],
                      in_=mk.mean_ag_out.rearrange("r o w -> (r o) w")
                      ).then_inc(S("d_mld"), 16)
        eng.wait_ge(S("d_s1"), 6)
        eng.dma_start(out=mk.sflat[0, :].rearrange("(a b) -> a b", b=256),
                      in_=mk.BF2[0:88, 1792:2048]).then_inc(S("d_bnc"), 16)
        eng.dma_start(out=mk.sflat[1, :].rearrange("(a b) -> a b", b=256),
                      in_=mk.BF2[0:88, 1024:1280]).then_inc(S("d_bnc"), 16)
        for fc in range(FCHK):
            eng.wait_ge(S("d_fin"), fc + 1)
            eng.dma_start(out=mk.w3sh[128 * fc:128 * (fc + 1), :],
                          in_=mk.BB3[:, 768:1024]).then_inc(S("d_out"), 16)
        for wi in range(2):
            eng.wait_ge(S("d_arv"), 4 + wi * 12)
            eng.dma_start(out=mk.sflat[0, :].rearrange("(a b) -> a b", b=256),
                          in_=mk.BF2[0:88, 1280 + 256 * wi:1536 + 256 * wi]
                          ).then_inc(S("d_bnc"), 16)
            eng.dma_start(out=mk.sflat[1, :].rearrange("(a b) -> a b", b=256),
                          in_=mk.BF2[0:88, 512 + 256 * wi:768 + 256 * wi]
                          ).then_inc(S("d_bnc"), 16)
            sh = mk.w1sh if wi == 0 else mk.w2sh
            for dc in range(NCHK):
                ti = FCHK + wi * NCHK + dc
                eng.wait_ge(S("d_fin"), ti + 1)
                eng.dma_start(out=sh[128 * dc:128 * (dc + 1), :],
                              in_=mk.BB3[:, 2112:2816]).then_inc(S("d_out"), 16)

    def gpsimd(self, eng):
        mk = self.mk
        S = mk.sem
        for fc in range(FCHK):
            eng.wait_ge(S("d_arv"), 4)
            if fc >= 1:
                eng.wait_ge(S("d_exp"), fc)
            eng.dma_start(out=mk.BF0[0:2, 1408:1664],
                          in_=mk.BF1[2 * fc:2 * fc + 2, 768:1024]
                          ).then_inc(S("d_stg"), 16)
            eng.dma_start(out=mk.BF0[0:2, 1664:1920],
                          in_=mk.BF1[2 * fc:2 * fc + 2, 1024:1280]
                          ).then_inc(S("d_stg"), 16)
        for wi in range(2):
            for dc in range(NCHK):
                ti = FCHK + wi * NCHK + dc
                eng.wait_ge(S("d_arv"), 4 + (wi + 1) * 12)
                eng.wait_ge(S("d_exp"), FCHK + wi * 2 * NCHK + 2 * dc)
                eng.dma_start(out=mk.BF0[0:2, 1408:2112],
                              in_=mk.BF0[2 * dc:2 * dc + 2, 0:704]
                              ).then_inc(S("d_stg"), 16)
                eng.dma_start(out=mk.BF0[0:2, 2112:2816],
                              in_=mk.BF0[2 * dc:2 * dc + 2, 704:1408]
                              ).then_inc(S("d_stg"), 16)



class PhB(Phase):

    def plan(self):
        mk = self.mk
        for n in ["b_wo", "b_gate", "b_gev", "b_sc", "b_msk", "b_qt", "b_o",
                  "b_kt", "b_stA", "b_stB", "b_sevA", "b_sevB", "b_og",
                  "b_on", "b_attn", "b_aev", "b_ln", "b_Sbf", "b_init",
                  "b_q0", "b_q1", "b_k0", "b_k1", "b_v0",
                  "b_v1", "b_f0", "b_f1",
                  "b_initl", "b_nw2", "b_lstat", "b_lnorm"]:
            mk.sem(n)
        self.nck = mk.nchunks
        mkk = self.mk
        self.xb0, self.xb1 = mkk.xb
        self.trb = 4 * self.nck
        self.stb = self.nck
        self.spb = mkk.nsg * ((NCHK * (mkk.sg_tok // 512)) // 2)

    def sync(self, eng):
        mk = self.mk
        S = mk.sem
        for ec in range(NCHK):
            eng.dma_start(out=mk.BB0[:, 2048 * ec:2048 * (ec + 1)],
                          in_=mk.wo_bf[ec, :, :]).then_inc(S("b_wo"), 16)
        eng.wait_ge(S("b_init"), 2)
        for gci in range(self.nck):
            p = gci % 2
            if gci >= 2:
                eng.wait_ge(S("b_ln"), gci - 1)
            eng.dma_start(out=mk.BF0[:, 2048 * p:2048 * p + 2048],
                          in_=mk.inp["x"][128 * gci:128 * (gci + 1), :]
                          ).then_inc(S(f"x{p}"), 16)
            if gci >= 2:
                eng.wait_ge(S("b_qt"), gci - 1)
            eng.dma_start(out=mk.BB1[:, 12288 + 2048 * p:12288 + 2048 * p + 2048]
                          .rearrange("p (h e) -> p h e", h=H),
                          in_=mk.q_sp[:, :, 128 * gci:128 * (gci + 1)]
                          .rearrange("h p e -> p h e")).then_inc(S(f"b_q{p}"), 16)
            if gci >= 2:
                eng.wait_ge(S("b_kt"), gci - 1)
            eng.dma_start(out=mk.BB1[:, 2048 * p:2048 * p + 2048],
                          in_=mk.k_sp[128 * gci:128 * (gci + 1), :]
                          ).then_inc(S(f"b_k{p}"), 16)
            if gci >= 2:
                eng.wait_ge(S("b_stB"), gci - 1)
            eng.dma_start(out=mk.BB1[:, 4096 + 2048 * p:4096 + 2048 * p + 2048],
                          in_=mk.v_sp[128 * gci:128 * (gci + 1), :]
                          ).then_inc(S(f"b_v{p}"), 16)
            if gci >= 2:
                eng.wait_ge(S("b_sc"), gci - 1)
            for h in range(H):
                eng.dma_start(
                    out=mk.BB1[:, 8192 + 2048 * p + 128 * h:
                               8192 + 2048 * p + 128 * (h + 1)],
                    in_=mk.k_sp[128 * gci:128 * (gci + 1),
                                128 * h:128 * (h + 1)],
                    transpose=True).then_inc(S(f"b_f{p}"), 16)

    def tensor(self, eng):
        mk = self.mk
        S = mk.sem
        eng.wait_ge(S("b_initl"), 16 * 4)
        for sl in range(4):
            mm = eng.matmul(mk.ps_f[sl][:, :],
                            mk.SM[0:1, SM_ROW + 16:SM_ROW + 144],
                            mk.BF0[0:1, 512 * sl:512 * (sl + 1)],
                            start=True, stop=True)
        mm.then_inc(S("b_nw2"), 1)
        eng.wait_ge(S("b_init"), 2)
        eng.wait_ge(S("b_wo"), 16 * NCHK)
        for gci in range(self.nck):
            p = gci % 2
            if gci >= 1:
                eng.wait_ge(S("b_aev"), 8 * gci)
            for es in range(4):
                mm = eng.matmul(mk.ps_f[4 + es % 2][:, :],
                                mk.G1R[:, 128 * gci:128 * (gci + 1)],
                                mk.BB2[:, 4096 + 512 * es:4096 + 512 * (es + 1)],
                                start=True, stop=True)
                mm.then_inc(S("b_gate"), 1)
                if es >= 1:
                    eng.wait_ge(S("b_gev"), 4 * gci + es)
            eng.wait_ge(S(f"b_f{p}"), 16 * H * (gci // 2 + 1))
            eng.wait_ge(S(f"b_q{p}"), 16 * (gci // 2 + 1))
            if gci >= 1:
                eng.wait_ge(S("b_og"), 4 * gci)
            for h in range(H):
                mm = eng.matmul(mk.ps_f[h // 4][:, 128 * (h % 4):128 * (h % 4) + 128],
                                mk.BB1[:, 8192 + 2048 * p + 128 * h:
                                       8192 + 2048 * p + 128 * (h + 1)],
                                mk.BB1[:, 12288 + 2048 * p + 128 * h:
                                       12288 + 2048 * p + 128 * (h + 1)],
                                start=(h % 4 == 0), stop=(h % 4 == 3))
            mm.then_inc(S("b_sc"), 1)
            eng.wait_ge(S("b_msk"), gci + 1)
            eng.wait_ge(S("b_qt"), gci + 1)
            eng.wait_ge(S("b_Sbf"), gci + 1)
            eng.wait_ge(S(f"b_v{p}"), 16 * (gci // 2 + 1))
            for h in range(H):
                eng.matmul(mk.ps_f[h // 4][:, 128 * (h % 4):128 * (h % 4) + 128],
                           mk.BB2[:, 2048 + 128 * h:2048 + 128 * (h + 1)],
                           mk.BB1[:, 4096 + 2048 * p + 128 * h:
                                  4096 + 2048 * p + 128 * (h + 1)],
                           start=(h % 4 == 0), stop=False)
                mm = eng.matmul(
                    mk.ps_f[h // 4][:, 128 * (h % 4):128 * (h % 4) + 128],
                    mk.BB2[:, 14336 + 128 * h:14336 + 128 * (h + 1)],
                    mk.BB2[:, 128 * h:128 * (h + 1)],
                    start=False, stop=(h % 4 == 3))
            mm.then_inc(S("b_o"), 1)
            eng.wait_ge(S("b_kt"), gci + 1)
            eng.wait_ge(S("b_gev"), 4 * (gci + 1))
            for half in range(2):
                if half == 1:
                    eng.wait_ge(S("b_sevA"), gci + 1)
                for hh in range(8):
                    h = 8 * half + hh
                    bank = 4 + hh // 4
                    eng.matmul(mk.ps_f[bank][:, 128 * (hh % 4):128 * (hh % 4) + 128],
                               mk.BF2[:, 128 * h:128 * (h + 1)],
                               mk.BF1[:, 128 * h:128 * (h + 1)],
                               start=(hh % 4 == 0), stop=False)
                    mm = eng.matmul(
                        mk.ps_f[bank][:, 128 * (hh % 4):128 * (hh % 4) + 128],
                        mk.BB3[:, 128 * h:128 * (h + 1)],
                        mk.BB1[:, 4096 + 2048 * p + 128 * h:
                               4096 + 2048 * p + 128 * (h + 1)],
                        start=False, stop=(hh % 4 == 3))
                mm.then_inc(S("b_stA" if half == 0 else "b_stB"), 1)
            eng.wait_ge(S("b_on"), gci + 1)
            for half in range(2):
                if half == 0 and gci >= 1:
                    eng.wait_ge(S("trev"), self.trb + 2 * gci)
                if half == 1:
                    eng.wait_ge(S("trev"), self.trb + 2 * gci + 1)
                for hh in range(8):
                    h = 8 * half + hh
                    mm = eng.transpose(
                        mk.ps_b[:, 1024 * half + 128 * hh:
                                1024 * half + 128 * (hh + 1)],
                        mk.BB2[:, 6144 + 128 * h:6144 + 128 * (h + 1)],
                        mk.IDT[:, :])
                mm.then_inc(S("tr"), 1)
            eng.wait_ge(S("b_sevB"), gci + 1)
            eng.wait_ge(S("trev"), self.trb + 2 * (gci + 1))
            for dms in range(4):
                if dms >= 2:
                    eng.wait_ge(S("b_aev"), 8 * gci + 2 * (dms - 1))
                for half in range(2):
                    for ec in range(8 * half, 8 * half + 8):
                        mm = eng.matmul(
                            mk.ps_f[4 + dms % 2][:, :],
                            mk.BB2[:, 12288 + 128 * ec:12288 + 128 * (ec + 1)],
                            mk.BB0[:, 2048 * ec + 512 * dms:
                                   2048 * ec + 512 * (dms + 1)],
                            start=(ec == 0), stop=(ec == NCHK - 1))
                    mm.then_inc(S("b_attn"), 1)

    def vector(self, eng):
        mk = self.mk
        S = mk.sem
        SM = mk.SM
        eng.wait_ge(S("b_initl"), 16 * 4)
        for h in range(H):
            eng.tensor_scalar_mul(mk.BF2[:, 128 * h:128 * (h + 1)],
                                  mk.BF0[:, 2048:2176],
                                  SM[:, SM_CD128 + h:SM_CD128 + h + 1])
        for h in range(H):
            eng.tensor_scalar_mul(mk.BF3[:, 128 * h:128 * (h + 1)],
                                  mk.BF0[:, 4096:4224],
                                  SM[:, SM_LDB + h:SM_LDB + h + 1])
        eng.drain()
        eng.sem_inc(S("b_init"), 1)
        for gci in range(self.nck):
            p = gci % 2
            if gci == 0:
                eng.wait_ge(S("b_init"), 1)
            else:
                eng.wait_ge(S("b_sevB"), gci)
                eng.wait_ge(S("b_o"), gci)
            eng.tensor_copy(mk.BB2[:, 0:2048], mk.BF1[:, :])
            eng.drain()
            eng.sem_inc(S("b_Sbf"), 1)
            eng.wait_ge(S("b_sc"), gci + 1)
            for b in range(4):
                eng.tensor_tensor(
                    mk.BB2[:, 2048 + 512 * b:2048 + 512 * (b + 1)],
                    mk.ps_f[b][:, :],
                    mk.CB[:, 2048 + 512 * b:2048 + 512 * (b + 1)],
                    ALU.mult)
            eng.drain()
            eng.sem_inc(S("b_msk"), 1)
            eng.wait_ge(S(f"b_q{p}"), 16 * (gci // 2 + 1))
            eng.tensor_mul(mk.BB2[:, 14336:16384],
                           mk.BB1[:, 12288 + 2048 * p:12288 + 2048 * p + 2048],
                           mk.CB[:, 0:2048])
            eng.drain()
            eng.sem_inc(S("b_qt"), 1)
            eng.wait_ge(S(f"b_k{p}"), 16 * (gci // 2 + 1))
            if gci >= 1:
                eng.wait_ge(S("b_stB"), gci)
            eng.tensor_tensor(
                mk.BB3[:, 0:2048].rearrange("p (h e) -> p h e", h=H),
                mk.BB1[:, 2048 * p:2048 * p + 2048]
                .rearrange("p (h e) -> p h e", h=H),
                SM[:, SM_KDP:SM_KDP + 16].unsqueeze(2).broadcast_to([128, H, HD]),
                ALU.mult)
            eng.drain()
            eng.sem_inc(S("b_kt"), 1)
            eng.wait_ge(S("b_o"), gci + 1)
            eng.wait_ge(S("b_gev"), 4 * (gci + 1))
            if gci >= 1:
                pp = (gci - 1) % 2
                eng.wait_ge(S(f"sp{pp}"),
                            16 * (self.spb + (gci - 1) // 2 + 1))
            for b in range(4):
                eng.tensor_tensor(mk.BF3[:, 512 * b:512 * (b + 1)],
                                  mk.ps_f[b][:, :],
                                  mk.BB2[:, 8192 + 512 * b:8192 + 512 * (b + 1)],
                                  ALU.mult)
            eng.drain()
            eng.sem_inc(S("b_og"), 4)
            rs = SM_RMS + 16 * p
            eng.wait_ge(S("st_sq"), self.stb + 2 * gci + 1)
            eng.tensor_reduce(SM[:, rs:rs + 16],
                              mk.BF0[:, 4096:6144].rearrange(
                                  "p (h e) -> p h e", h=H),
                              AX.X, ALU.add)
            eng.drain()
            eng.tensor_scalar(SM[:, rs:rs + 16], SM[:, rs:rs + 16],
                              1.0 / HD, EPS, ALU.mult, ALU.add)
            eng.drain()
            eng.sem_inc(S("st_var"), 1)
            eng.wait_ge(S("st_sqr"), self.stb + 2 * gci + 1)
            eng.reciprocal(SM[:, rs:rs + 16], SM[:, rs:rs + 16])
            eng.drain()
            eng.tensor_tensor(
                mk.BF3[:, :].rearrange("p (h e) -> p h e", h=H),
                mk.BF3[:, :].rearrange("p (h e) -> p h e", h=H),
                SM[:, rs:rs + 16].unsqueeze(2).broadcast_to([128, H, HD]),
                ALU.mult)
            eng.drain()
            eng.tensor_mul(mk.BB2[:, 6144:8192], mk.BF3[:, :],
                           mk.BB2[:, 10240:12288])
            eng.drain()
            eng.sem_inc(S("b_on"), 1)
            eng.wait_ge(S("b_aev"), 8 * (gci + 1))
            st = SM_ST + 8 * p
            eng.reduce_sum(SM[:, st:st + 1], mk.BF3[:, :], AX.X)
            eng.wait_ge(S("st_sq"), self.stb + 2 * gci + 2)
            eng.drain()
            eng.tensor_scalar_mul(SM[:, st + 2:st + 3], SM[:, st:st + 1], 1.0 / D)
            eng.tensor_scalar_mul(SM[:, st + 3:st + 4], SM[:, st + 1:st + 2],
                                  1.0 / D)
            eng.drain()
            eng.tensor_mul(SM[:, st + 7:st + 8], SM[:, st + 2:st + 3],
                           SM[:, st + 2:st + 3])
            eng.drain()
            eng.tensor_sub(SM[:, st + 3:st + 4], SM[:, st + 3:st + 4],
                           SM[:, st + 7:st + 8])
            eng.drain()
            eng.tensor_scalar_add(SM[:, st + 3:st + 4], SM[:, st + 3:st + 4], EPS)
            eng.drain()
            eng.sem_inc(S("st_var"), 1)
            eng.wait_ge(S("st_sqr"), self.stb + 2 * gci + 2)
            eng.reciprocal(SM[:, st + 6:st + 7], SM[:, st + 5:st + 6])
            eng.drain()
            eng.tensor_scalar(SM[:, st + 4:st + 5], SM[:, st + 2:st + 3],
                              SM[:, st + 6:st + 7], -1.0, ALU.mult, ALU.mult)
            eng.drain()
            eng.sem_inc(S("b_lstat"), 1)
            eng.wait_ge(S("b_lnorm"), gci + 1)
            xbase = self.xb0 if p == 0 else self.xb1
            eng.wait_ge(S(f"x{p}"), 16 * (xbase + gci // 2 + 1))
            eng.tensor_add(mk.BF3[:, :], mk.BF0[:, 4096:6144],
                           mk.BF0[:, 2048 * p:2048 * p + 2048])
            eng.drain()
            eng.sem_inc(S("b_ln"), 1)

    def scalar(self, eng):
        mk = self.mk
        S = mk.sem
        SM = mk.SM
        i = mk.inp
        eng.dma_start(out=mk.BF0[:, 4096:4224], in_=i["posmat"][:, 0:128]
                      ).then_inc(S("b_initl"), 16)
        eng.dma_start(out=mk.BF0[:, 2048:2176], in_=i["identf"][:, :]
                      ).then_inc(S("b_initl"), 16)
        eng.dma_start(out=mk.BF0[0:1, 0:2048],
                      in_=i["norm_w"].rearrange("(o c) p -> o (c p)", o=1)
                      ).then_inc(S("b_initl"), 16)
        eng.dma_start(out=mk.BB2[:, 4096:6144], in_=mk.wg2d[:, :]
                      ).then_inc(S("b_initl"), 16)
        eng.wait_ge(S("b_init"), 1)
        for h in range(H):
            eng.activation(mk.CB[:, 128 * h:128 * (h + 1)],
                           mk.BF3[:, 128 * h:128 * (h + 1)], AF.Exp)
        eng.drain()
        eng.wait_ge(S("b_nw2"), 1)
        for sl in range(4):
            eng.activation(mk.BB2[:, 10240 + 512 * sl:10240 + 512 * (sl + 1)],
                           mk.ps_f[sl][:, :], AF.Identity)
        eng.drain()
        eng.sem_inc(S("b_init"), 1)
        for gci in range(self.nck):
            p = gci % 2
            for es in range(4):
                eng.wait_ge(S("b_gate"), 4 * gci + es + 1)
                eng.activation(mk.BB2[:, 8192 + 512 * es:8192 + 512 * (es + 1)],
                               mk.ps_f[4 + es % 2][:, :], AF.Sigmoid)
                eng.drain()
                eng.sem_inc(S("b_gev"), 1)
            eng.wait_ge(S("b_stA"), gci + 1)
            eng.activation(mk.BF1[:, 0:512], mk.ps_f[4][:, :], AF.Identity)
            eng.activation(mk.BF1[:, 512:1024], mk.ps_f[5][:, :], AF.Identity)
            eng.drain()
            eng.sem_inc(S("b_sevA"), 1)
            eng.wait_ge(S("b_stB"), gci + 1)
            eng.activation(mk.BF1[:, 1024:1536], mk.ps_f[4][:, :], AF.Identity)
            eng.activation(mk.BF1[:, 1536:2048], mk.ps_f[5][:, :], AF.Identity)
            eng.drain()
            eng.sem_inc(S("b_sevB"), 1)
            eng.wait_ge(S("b_og"), 4 * (gci + 1))
            eng.activation(mk.BF0[:, 4096:6144], mk.BF3[:, :], AF.Square)
            eng.drain()
            eng.sem_inc(S("st_sq"), 1)
            rs = SM_RMS + 16 * p
            eng.wait_ge(S("st_var"), self.stb + 2 * gci + 1)
            eng.activation(SM[:, rs:rs + 16], SM[:, rs:rs + 16], AF.Sqrt)
            eng.drain()
            eng.sem_inc(S("st_sqr"), 1)
            for half in range(2):
                eng.wait_ge(S("tr"), self.trb + 2 * gci + half + 1)
                for hh in range(8):
                    h = 8 * half + hh
                    eng.tensor_copy(
                        mk.BB2[:, 12288 + 128 * h:12288 + 128 * (h + 1)],
                        mk.ps_b[:, 1024 * half + 128 * hh:
                                1024 * half + 128 * (hh + 1)])                         if False else                     eng.activation(
                        mk.BB2[:, 12288 + 128 * h:12288 + 128 * (h + 1)],
                        mk.ps_b[:, 1024 * half + 128 * hh:
                                1024 * half + 128 * (hh + 1)], AF.Identity)
                eng.drain()
                eng.sem_inc(S("trev"), 1)
            for dms in range(4):
                eng.wait_ge(S("b_attn"), 8 * gci + 2 * dms + 2)
                eng.activation(mk.BF3[:, 512 * dms:512 * (dms + 1)],
                               mk.ps_f[4 + dms % 2][:, :], AF.Identity)
                eng.drain()
                eng.sem_inc(S("b_aev"), 2)
            st = SM_ST + 8 * p
            eng.wait_ge(S("b_aev"), 8 * (gci + 1))
            eng.activation(mk.BF0[:, 4096:6144], mk.BF3[:, :], AF.Square,
                           accum_out=SM[:, st + 1:st + 2])
            eng.drain()
            eng.sem_inc(S("st_sq"), 1)
            eng.wait_ge(S("st_var"), self.stb + 2 * gci + 2)
            eng.activation(SM[:, st + 5:st + 6], SM[:, st + 3:st + 4], AF.Sqrt)
            eng.drain()
            eng.sem_inc(S("st_sqr"), 1)
            eng.wait_ge(S("b_lstat"), gci + 1)
            eng.activation(mk.BF0[:, 4096:6144], mk.BF3[:, :], AF.Identity,
                           bias=SM[:, st + 4:st + 5], scale=SM[:, st + 6:st + 7])
            eng.drain()
            eng.sem_inc(S("b_lnorm"), 1)
            eng.wait_ge(S("b_ln"), gci + 1)
            if gci >= 2:
                eng.wait_ge(S(f"sp{p}"), 16 * (self.spb + gci // 2))
            eng.dma_start(out=mk.xmid[128 * gci:128 * (gci + 1), :],
                          in_=mk.BF3[:, :]).then_inc(S(f"sp{p}"), 16)
        eng.wait_ge(S("sp0"), 16 * (self.spb + (self.nck + 1) // 2))
        eng.wait_ge(S("sp1"), 16 * (self.spb + self.nck // 2))

    def gpsimd(self, eng):
        mk = self.mk
        S = mk.sem
        if "deq" in mk.phases:
            eng.wait_ge(S("d_out"), 16 * (FCHK + 2 * NCHK))
            for (src, dst) in ((mk.w1sh, mk.w1gr), (mk.w2sh, mk.w2gr),
                               (mk.w3sh, mk.w3g)):
                eng.collective_compute(
                    "AllGather", ALU.bypass,
                    replica_groups=[list(range(mk.n_cores))],
                    ins=[src], outs=[dst]).then_inc(S("agw"), 1)
            eng.wait_ge(S("agw"), 3)
            for (gr, lin) in ((mk.w1gr, mk.w1g), (mk.w2gr, mk.w2g)):
                for r in range(mk.n_cores):
                    eng.dma_start(out=lin[:, 704 * r:704 * (r + 1)],
                                  in_=gr[r, :, :]).then_inc(S("c_wready"), 16)


class PhC(Phase):

    def plan(self):
        mk = self.mk
        for n in ["c_ln", "c_ab", "c_aev", "c_bev", "c_h", "c_w3m", "c_oev",
                  "c_res", "c_w120", "c_w121",
                  "c_w30", "c_w31", "c_w32", "c_w33"]:
            mk.sem(n)
        n0, n1 = (self.mk.nchunks + 1) // 2, self.mk.nchunks // 2
        self.fb = (256 * n0, 256 * n1)
        self.nck = mk.nchunks
        self.ng = mk.ngc
        self.cpc = mk.cpc
        n0, n1 = mk.xb
        self.xb = (2 * n0, 2 * n1)
        self.stb = 3 * self.nck
        self.stb_stat = self.nck
        self.trb = 6 * self.nck
        self.spb = (mk.nsg * ((NCHK * (mk.sg_tok // 512)) // 2)
                    + (self.nck + 1) // 2,
                    mk.nsg * ((NCHK * (mk.sg_tok // 512)) // 2)
                    + self.nck // 2)

    def xcnt(self, gci):
        return self.xb[gci % 2] + gci // 2 + 1

    @property
    def cxb(self):
        return 4 if "B" in self.mk.phases else 0

    @property
    def bf3b(self):
        return (FCHK + 2 * NCHK) if "deq" in self.mk.phases else 0

    def ln3_ev(self, gci):
        h, ci = divmod(gci, 4)
        return (ci + 1) if h == 0 else (8 * h - 3 + ci)

    def ln4_ev(self, g, tc):
        if g < self.ng - 1:
            return 8 * g + 9 + tc
        return 8 * g + 5 + tc

    def w3_tile_dma(self, eng, g, il):
        mk = self.mk
        S = mk.sem
        i = g * 4 * FCHK + il
        if i >= 4:
            eng.wait_ge(S("c_w3m"), i - 3)
        dms, fc = divmod(il, FCHK)
        eng.dma_start(
            out=mk.BB3[:, 512 * (i % 4):512 * (i % 4 + 1)]
            .rearrange("p (r i) -> p r i", r=2),
            in_=mk.w3g[2 * dms:2 * dms + 2, 128 * fc:128 * (fc + 1), :]
            .rearrange("r f i -> f r i")
            ).then_inc(S(f"c_w3{i % 4}"), 16)

    def lnslot(self, gci):
        return self.mk.BB1[:, 8192 + 2048 * (gci % 4):
                           8192 + 2048 * (gci % 4) + 2048]

    def row(self, tc):
        mk = self.mk
        return (mk.BF0[:, 0:2048] if tc == 0 else
                mk.BF0[:, 2048:4096] if tc == 1 else
                mk.BF1[:, :] if tc == 2 else mk.BF2[:, :])

    def slab_dma(self, eng, g, fcp, gated):
        mk = self.mk
        S = mk.sem
        pw = fcp % 2
        if gated:
            eng.wait_ge(S("c_ab"), g * 2 * FCHK + 4 * (fcp - 1))
        for (wt, off) in ((mk.w1g, 0), (mk.w2g, 8192)):
            eng.dma_start(
                out=mk.BB2[:, off + 4096 * pw:off + 4096 * pw + 4096]
                .rearrange("p (c f) -> p c f", c=NCHK),
                in_=wt.rearrange("(c p) f -> p c f", p=128)
                [:, :, 256 * fcp:256 * (fcp + 1)]
                ).then_inc(S(f"c_w12{pw}"), 16)

    def sync(self, eng):
        mk = self.mk
        S = mk.sem
        if "deq" in mk.phases:
            eng.wait_ge(S("c_wready"), 16 * 16)
        for fcp in range(2):
            self.slab_dma(eng, 0, fcp, gated=False)
        for g in range(self.ng):
            for il in (0, 2):
                self.w3_tile_dma(eng, g, il)
            for fcp in range(2, FCHK // 2):
                self.slab_dma(eng, g, fcp, gated=True)
            if g + 1 < self.ng:
                for fcp in range(2):
                    eng.wait_ge(S("c_ab"),
                                g * 2 * FCHK + 4 * (FCHK // 2 - 1 + fcp))
                    self.slab_dma(eng, g + 1, fcp, gated=False)
            for il in range(4, 4 * FCHK, 2):
                self.w3_tile_dma(eng, g, il)
            for tc in range(self.cpc):
                gci = g * self.cpc + tc
                p = gci % 2
                eng.wait_ge(S("c_res"), gci + 1)
                if gci >= 2:
                    eng.wait_ge(S(f"sp{p}"), 16 * (self.spb[p] + gci // 2))
                eng.dma_start(out=mk.outp[128 * gci:128 * (gci + 1), :],
                              in_=self.row(tc)).then_inc(S(f"sp{p}"), 16)

    def t_trans(self, eng, gci):
        mk = self.mk
        S = mk.sem
        eng.wait_ge(S("c_ln"), gci + 1)
        lns = self.lnslot(gci)
        for s in range(4):
            b = (0, 2, 1, 3)[s]
            if gci * 4 + s >= 4:
                eng.wait_ge(S("trev"), self.trb + gci * 4 + s - 3)
            for k in range(4):
                dc = b * 4 + k
                mm = eng.transpose(
                    mk.ps_b[:, 512 * b + 128 * k:512 * b + 128 * k + 128],
                    lns[:, 128 * dc:128 * (dc + 1)],
                    mk.IDT[:, :])
            mm.then_inc(S("tr"), 1)

    def tensor(self, eng):
        mk = self.mk
        S = mk.sem
        for g in range(self.ng):
            if g == 0:
                for ci in range(self.cpc):
                    self.t_trans(eng, ci)
            eng.wait_ge(S("trev"), self.trb + (g + 1) * self.cpc * 4)
            for fc in range(FCHK):
                fcp = fc // 2
                pw = fcp % 2
                f2 = fc % 2
                ba, bb = (4, 5) if fc % 2 == 0 else (0, 1)
                if f2 == 0:
                    eng.wait_ge(S(f"c_w12{pw}"),
                                32 * (g * (FCHK // 2) // 2 + fcp // 2 + 1))
                if fc == 0:
                    if g >= 1:
                        eng.wait_ge(S("c_aev"), g * FCHK - 1)
                        eng.wait_ge(S("c_bev"), g * FCHK - 1)
                elif fc == 1:
                    if g >= 1:
                        eng.wait_ge(S("c_oev"), 16 * g)
                        eng.wait_ge(S("c_aev"), g * FCHK)
                        eng.wait_ge(S("c_bev"), g * FCHK)
                else:
                    eng.wait_ge(S("c_aev"), g * FCHK + fc - 1)
                    eng.wait_ge(S("c_bev"), g * FCHK + fc - 1)
                for dc in range(NCHK):
                    eng.matmul(mk.ps_f[ba][:, :],
                               mk.BB2[:, 4096 * pw + 256 * dc + 128 * f2:
                                      4096 * pw + 256 * dc + 128 * (f2 + 1)],
                               mk.BB1[:, 512 * dc:512 * (dc + 1)],
                               start=(dc == 0), stop=(dc == NCHK - 1))
                for dc in range(NCHK):
                    mm = eng.matmul(mk.ps_f[bb][:, :],
                                    mk.BB2[:, 8192 + 4096 * pw + 256 * dc + 128 * f2:
                                           8192 + 4096 * pw + 256 * dc + 128 * (f2 + 1)],
                                    mk.BB1[:, 512 * dc:512 * (dc + 1)],
                                    start=(dc == 0), stop=(dc == NCHK - 1))
                mm.then_inc(S("c_ab"), 2)
            eng.wait_ge(S("c_h"), (g + 1) * FCHK)
            for dms in range(4):
                if g * 4 + dms >= 1:
                    eng.wait_ge(S("c_oev"), 4 * (g * 4 + dms))
                for fc in range(FCHK):
                    i = (g * 4 + dms) * FCHK + fc
                    eng.wait_ge(S(f"c_w3{i % 4}"), 16 * (i // 4 + 1))
                    for tc in range(4):
                        mm = eng.matmul(
                            mk.ps_f[tc][:, :],
                            mk.BB0[:, 512 * fc + 128 * tc:
                                   512 * fc + 128 * (tc + 1)],
                            mk.BB3[:, 512 * (i % 4):512 * (i % 4) + 512],
                            start=(fc == 0), stop=(fc == FCHK - 1))
                    mm.then_inc(S("c_w3m"), 1)
                if g < self.ng - 1:
                    self.t_trans(eng, (g + 1) * self.cpc + dms)

    def xslot(self, gci):
        return self.mk.XB[:, 2048 * (gci % 2):2048 * (gci % 2) + 2048]

    def v_ln3(self, eng, gci):
        mk = self.mk
        S = mk.sem
        SM = mk.SM
        p = gci % 2
        st = SM_ST + 8 * p
        xs = self.xslot(gci)
        ev = self.ln3_ev(gci)
        eng.wait_ge(S("b_initl"), 16 * (gci + 1 + self.cxb))
        if gci >= 2:
            eng.wait_ge(S("c_ln"), gci - 1)
        eng.reduce_sum(SM[:, st:st + 1], xs, AX.X)
        eng.wait_ge(S("st_sq"), self.stb + ev)
        eng.drain()
        self.v_stats(eng, st, ev)

    def v_stats(self, eng, st, ev):
        mk = self.mk
        S = mk.sem
        SM = mk.SM
        eng.tensor_scalar_mul(SM[:, st + 2:st + 3], SM[:, st:st + 1],
                              1.0 / D)
        eng.tensor_scalar_mul(SM[:, st + 3:st + 4],
                              SM[:, st + 1:st + 2], 1.0 / D)
        eng.drain()
        eng.tensor_mul(SM[:, st + 7:st + 8], SM[:, st + 2:st + 3],
                       SM[:, st + 2:st + 3])
        eng.drain()
        eng.tensor_sub(SM[:, st + 3:st + 4], SM[:, st + 3:st + 4],
                       SM[:, st + 7:st + 8])
        eng.drain()
        eng.tensor_scalar_add(SM[:, st + 3:st + 4],
                              SM[:, st + 3:st + 4], EPS)
        eng.drain()
        eng.sem_inc(S("st_var"), 1)
        eng.wait_ge(S("st_sqr"), self.stb + ev)
        eng.reciprocal(SM[:, st + 6:st + 7], SM[:, st + 5:st + 6])
        eng.drain()
        eng.tensor_scalar(SM[:, st + 4:st + 5], SM[:, st + 2:st + 3],
                          SM[:, st + 6:st + 7], -1.0, ALU.mult, ALU.mult)
        eng.drain()
        eng.sem_inc(S("st_stat"), 1)

    def v_trev(self, eng, gci):
        mk = self.mk
        S = mk.sem
        ci = gci % 4
        for s in range(4):
            b = (0, 2, 1, 3)[s]
            eng.wait_ge(S("tr"), self.trb + gci * 4 + s + 1)
            for k in range(4):
                dc = b * 4 + k
                eng.tensor_copy(
                    mk.BB1[:, 512 * dc + 128 * ci:
                           512 * dc + 128 * (ci + 1)],
                    mk.ps_b[:, 512 * b + 128 * k:512 * b + 128 * k + 128])
            eng.drain()
            eng.sem_inc(S("trev"), 1)

    def vector(self, eng):
        mk = self.mk
        S = mk.sem
        SM = mk.SM
        for ci in range(self.cpc):
            self.v_ln3(eng, ci)
        for ci in range(self.cpc):
            self.v_trev(eng, ci)
        for g in range(self.ng):
            for fc in range(FCHK):
                eng.wait_ge(S("c_ab"), g * 2 * FCHK + 2 * (fc + 1))
                eng.tensor_copy(mk.CB[:, 512:1024],
                                mk.ps_f[5 if fc % 2 == 0 else 1][:, :])
                eng.drain()
                eng.sem_inc(S("c_bev"), 1)
                eng.wait_ge(S("c_aev"), g * FCHK + fc + 1)
                eng.tensor_mul(mk.BB0[:, 512 * fc:512 * (fc + 1)],
                               mk.CB[:, 0:512], mk.CB[:, 512:1024])
                eng.drain()
                eng.sem_inc(S("c_h"), 1)
            if g < self.ng - 1:
                for ci in range(self.cpc):
                    self.v_ln3(eng, (g + 1) * self.cpc + ci)
            for dms in range(4):
                eng.wait_ge(S("c_w3m"), (g * 4 + dms + 1) * FCHK)
                if dms == 0 and g >= 1:
                    eng.wait_ge(S("sp0"), 16 * (self.spb[0] + 2 * g))
                    eng.wait_ge(S("sp1"), 16 * (self.spb[1] + 2 * g))
                for tc in range(4):
                    eng.tensor_copy(self.row(tc)[:, 512 * dms:512 * (dms + 1)],
                                    mk.ps_f[tc][:, :])
                eng.drain()
                eng.sem_inc(S("c_oev"), 4)
                if g < self.ng - 1 and dms >= 1:
                    self.v_trev(eng, (g + 1) * self.cpc + dms - 1)
            if g < self.ng - 1:
                self.v_trev(eng, (g + 1) * self.cpc + 3)
            for tc in range(self.cpc):
                gci = g * self.cpc + tc
                p = tc % 2
                st = SM_ST + 8 * p
                ev = self.ln4_ev(g, tc)
                row = self.row(tc)
                eng.reduce_sum(SM[:, st:st + 1], row, AX.X)
                eng.wait_ge(S("st_sq"), self.stb + ev)
                eng.drain()
                self.v_stats(eng, st, ev)
                eng.wait_ge(S("b_lnorm"), self.nck + gci + 1)
                eng.wait_ge(S("d_out"), 16 * (self.bf3b + gci + 1))
                eng.tensor_add(row, mk.BF0[:, 4096:6144], mk.BF3[:, :])
                eng.drain()
                eng.sem_inc(S("c_res"), 1)

    def s_ln3(self, eng, gci):
        mk = self.mk
        S = mk.sem
        SM = mk.SM
        p = gci % 2
        st = SM_ST + 8 * p
        xs = self.xslot(gci)
        ev = self.ln3_ev(gci)
        eng.wait_ge(S("b_initl"), 16 * (gci + 1 + self.cxb))
        eng.activation(mk.BF0[:, 4096:6144], xs, AF.Square,
                       accum_out=SM[:, st + 1:st + 2])
        eng.drain()
        eng.sem_inc(S("st_sq"), 1)
        eng.wait_ge(S("st_var"), self.stb + ev)
        eng.activation(SM[:, st + 5:st + 6], SM[:, st + 3:st + 4],
                       AF.Sqrt)
        eng.drain()
        eng.sem_inc(S("st_sqr"), 1)
        eng.wait_ge(S("st_stat"), self.stb_stat + ev)
        if gci >= 4:
            eng.wait_ge(S("tr"), self.trb + 4 * (gci - 3))
        eng.activation(self.lnslot(gci), xs, AF.Identity,
                       bias=SM[:, st + 4:st + 5],
                       scale=SM[:, st + 6:st + 7])
        eng.drain()
        eng.sem_inc(S("c_ln"), 1)
        if gci + 2 < self.nck:
            self.xload(eng, gci + 2)

    def xload(self, eng, gci):
        mk = self.mk
        S = mk.sem
        if gci >= 1:
            eng.wait_ge(S("b_initl"), 16 * (gci + self.cxb))
        if gci >= 2:
            eng.wait_ge(S("c_ln"), gci - 1)
        eng.dma_start(out=self.xslot(gci),
                      in_=mk.xmid[128 * gci:128 * (gci + 1), :]
                      ).then_inc(S("b_initl"), 16)

    def scalar(self, eng):
        mk = self.mk
        S = mk.sem
        SM = mk.SM
        for gci in range(2):
            self.xload(eng, gci)
        for ci in range(self.cpc):
            self.s_ln3(eng, ci)
        for g in range(self.ng):
            for fc in range(FCHK):
                eng.wait_ge(S("c_ab"), g * 2 * FCHK + 2 * fc + 2)
                if fc >= 1:
                    eng.wait_ge(S("c_h"), g * FCHK + fc)
                eng.activation(mk.CB[:, 0:512],
                               mk.ps_f[4 if fc % 2 == 0 else 0][:, :], AF.Silu)
                eng.drain()
                eng.sem_inc(S("c_aev"), 1)
            if g < self.ng - 1:
                for ci in range(self.cpc):
                    self.s_ln3(eng, (g + 1) * self.cpc + ci)
            for tc in range(self.cpc):
                gci = g * self.cpc + tc
                p = tc % 2
                st = SM_ST + 8 * p
                ev = self.ln4_ev(g, tc)
                row = self.row(tc)
                eng.wait_ge(S("c_oev"), 16 * (g + 1))
                eng.wait_ge(S("c_res"), gci)
                eng.dma_start(out=mk.BF3[:, :],
                              in_=mk.xmid[128 * gci:128 * (gci + 1), :]
                              ).then_inc(S("d_out"), 16)
                eng.activation(mk.BF0[:, 4096:6144], row, AF.Square,
                               accum_out=SM[:, st + 1:st + 2])
                eng.drain()
                eng.sem_inc(S("st_sq"), 1)
                eng.wait_ge(S("st_var"), self.stb + ev)
                eng.activation(SM[:, st + 5:st + 6], SM[:, st + 3:st + 4],
                               AF.Sqrt)
                eng.drain()
                eng.sem_inc(S("st_sqr"), 1)
                eng.wait_ge(S("st_stat"), self.stb_stat + ev)
                eng.activation(mk.BF0[:, 4096:6144], row, AF.Identity,
                               bias=SM[:, st + 4:st + 5],
                               scale=SM[:, st + 6:st + 7])
                eng.drain()
                eng.sem_inc(S("b_lnorm"), 1)
        eng.wait_ge(S("sp0"), 16 * (self.spb[0] + (self.nck + 1) // 2))
        eng.wait_ge(S("sp1"), 16 * (self.spb[1] + self.nck // 2))

    def gpsimd(self, eng):
        mk = self.mk
        S = mk.sem
        for g in range(self.ng):
            for il in range(1, 4 * FCHK, 2):
                self.w3_tile_dma(eng, g, il)
        eng.wait_ge(S("sp0"), 16 * (self.spb[0] + (self.nck + 1) // 2))
        eng.wait_ge(S("sp1"), 16 * (self.spb[1] + self.nck // 2))


from concourse.bass_utils import run_bass_kernel_spmd


def kernel(L0_attn_x, L0_attn_log_decay, L0_attn_Wq, L0_attn_Wk, L0_attn_Wv,
           L0_attn_Wo, L0_attn_Wg1, L0_attn_Wg2, L0_attn_norm_w,
           L0_ffn_weight1, L0_ffn_weight2, L0_ffn_weight3):
    NCORES = 8
    TPC = 4096
    x = np.ascontiguousarray(
        np.asarray(L0_attn_x, dtype=np.float32).reshape(NCORES * TPC, D))
    ld = np.ascontiguousarray(
        np.asarray(L0_attn_log_decay, dtype=np.float32).reshape(16, 1))
    Wq = np.ascontiguousarray(np.asarray(L0_attn_Wq, dtype=np.float32))
    Wk = np.ascontiguousarray(np.asarray(L0_attn_Wk, dtype=np.float32))
    Wv = np.ascontiguousarray(np.asarray(L0_attn_Wv, dtype=np.float32))
    Wo = np.ascontiguousarray(np.asarray(L0_attn_Wo, dtype=np.float32))
    Wg1 = np.ascontiguousarray(np.asarray(L0_attn_Wg1, dtype=np.float32))
    Wg2 = np.ascontiguousarray(np.asarray(L0_attn_Wg2, dtype=np.float32))
    nw = np.ascontiguousarray(
        np.asarray(L0_attn_norm_w, dtype=np.float32).reshape(NCHK, 128))
    w1 = np.asarray(L0_ffn_weight1).view(ml_dtypes.bfloat16) \
        if np.asarray(L0_ffn_weight1).dtype != ml_dtypes.bfloat16 \
        else np.asarray(L0_ffn_weight1)
    w2 = np.asarray(L0_ffn_weight2)
    w3 = np.asarray(L0_ffn_weight3)
    w1 = np.ascontiguousarray(np.asarray(w1).reshape(DFF, D))
    w2 = np.ascontiguousarray(np.asarray(w2).reshape(DFF, D))
    w3 = np.ascontiguousarray(np.asarray(w3).reshape(D, DFF))
    w1t = np.ascontiguousarray(w1.T)
    w2t = np.ascontiguousarray(w2.T)
    w3t = np.ascontiguousarray(w3.T)

    mk = MK(tpc=TPC, n_cores=NCORES)
    nc = mk.build()

    in_maps = []
    for c in range(NCORES):
        m = {
            "x": np.ascontiguousarray(x[TPC * c:TPC * (c + 1)]),
            "log_decay": ld,
            "Wq": Wq, "Wk": Wk, "Wv": Wv, "Wo": Wo,
            "Wg1": Wg1, "Wg2": Wg2, "norm_w": nw,
            "w1s": np.ascontiguousarray(w1[704 * c:704 * (c + 1)]),
            "w2s": np.ascontiguousarray(w2[704 * c:704 * (c + 1)]),
            "w3s": np.ascontiguousarray(w3[256 * c:256 * (c + 1)]),
            "w1ts": np.ascontiguousarray(w1t[:, 704 * c:704 * (c + 1)]),
            "w2ts": np.ascontiguousarray(w2t[:, 704 * c:704 * (c + 1)]),
            "w3ts": np.ascontiguousarray(w3t[:, 256 * c:256 * (c + 1)]),
        }
        m.update(host_consts(c))
        in_maps.append(m)

    res = run_bass_kernel_spmd(nc, in_maps, core_ids=list(range(NCORES)))
    out = np.concatenate([np.asarray(res.results[c]["out"], dtype=np.float32)
                          .reshape(TPC, D) for c in range(NCORES)], axis=0)
    return out.reshape(4, 8192, D)
